# version 03
# baseline (speedup 1.0000x reference)
import numpy as np
import ml_dtypes
import concourse.bass as bass
import concourse.mybir as mybir
from concourse.bass_utils import run_bass_kernel_spmd

F32 = mybir.dt.float32
BF16 = mybir.dt.bfloat16
AF = mybir.ActivationFunctionType
ALU = mybir.AluOpType
AX = mybir.AxisListType

BF = ml_dtypes.bfloat16

NCORES = 8
B_FULL, S, E = 32, 2048, 1024
SEQ = 4
T = SEQ * S
NT = 512
NTILES = T // NT
CH = E // 128
H, KVH, D = 16, 4, 64
QL, KVL = 64, 32
ROPE_BASE = 10000.0
SCALE0 = D ** -0.5
NE, DFF, TOPK = 8, 2048, 2
LAT = 256
DH = 4096
NSTAGE = 12
LN_EPS = 1e-5



def _deint_idx():
    r = np.arange(64)
    return np.where(r < 32, 2 * r, 2 * (r - 32) + 1)


def _rope_tables():
    inv = 1.0 / ROPE_BASE ** (np.arange(0, D, 2, dtype=np.float64) / D)
    t = np.arange(S, dtype=np.float64)
    fr = np.outer(inv, t)
    cos = np.cos(fr).astype(np.float32)
    sin = np.sin(fr).astype(np.float32)
    coss = np.concatenate([cos, cos, cos, cos], 0)
    sins = np.concatenate([-sin, sin, -sin, sin], 0)
    return coss, sins


def _perm_swap32():
    p = np.arange(128)
    swap = (p // 32 % 2 == 0) * (p + 32) + (p // 32 % 2 == 1) * (p - 32)
    m = np.zeros((128, 128), np.float32)
    m[swap, p] = 1.0
    return m


def _bfly_mats(weights, transpose):
    chunk_lhsT = []
    for s in range(7):
        s_ = 1 << s
        B_full = np.zeros((128, 128), np.float32)
        mats = []
        for c in range(32):
            Bm = np.zeros((128, 128), np.float32)
            for i_lo in range(128):
                i = c * 128 + i_lo
                if i & s_:
                    continue
                j = i + s_
                j_lo = j - c * 128
                k = (i // (2 * s_)) * s_ + (i % s_)
                w = weights[s, k]
                if transpose:
                    w = w.T
                Bm[i_lo, i_lo] = w[0, 0]
                Bm[i_lo, j_lo] = w[0, 1]
                Bm[j_lo, i_lo] = w[1, 0]
                Bm[j_lo, j_lo] = w[1, 1]
            mats.append(np.ascontiguousarray(Bm.T))
        chunk_lhsT.append(np.stack(mats))
    diag = []
    for s in range(7, 12):
        s_ = 1 << s
        bit = s_ // 128
        d_self = np.zeros((32, 128), np.float32)
        d_other = np.zeros((32, 128), np.float32)
        for c in range(32):
            for p in range(128):
                i = c * 128 + p
                k = (i // (2 * s_)) * s_ + (i % s_) if not (i & s_) else None
                if i & s_:
                    ip = i - s_
                    k = (ip // (2 * s_)) * s_ + (ip % s_)
                    w = weights[s, k]
                    if transpose:
                        w = w.T
                    d_self[c, p] = w[1, 1]
                    d_other[c, p] = w[1, 0]
                else:
                    k = (i // (2 * s_)) * s_ + (i % s_)
                    w = weights[s, k]
                    if transpose:
                        w = w.T
                    d_self[c, p] = w[0, 0]
                    d_other[c, p] = w[0, 1]
        diag.append((d_self, d_other))
    return chunk_lhsT, diag


def host_prep(inp):
    d = {}
    deint = _deint_idx()
    d["Wqa"] = np.ascontiguousarray(inp["L0_attn_Wqa"])
    Wqb = inp["L0_attn_Wqb"]
    cols = np.empty(1024, np.int64)
    for hp in range(H // 2):
        for half in range(2):
            h = 2 * hp + half
            cols[hp * 128 + half * 64:hp * 128 + half * 64 + 64] = h * 64 + deint
    d["Wqb_perm"] = np.ascontiguousarray(Wqb[:, cols])
    d["Wkva"] = np.ascontiguousarray(inp["L0_attn_Wkva"])
    Wkvb = inp["L0_attn_Wkvb"]
    kcols = np.empty(256, np.int64)
    vcols = np.empty(256, np.int64)
    for kp in range(KVH // 2):
        for half in range(2):
            g = 2 * kp + half
            kcols[kp * 128 + half * 64:kp * 128 + half * 64 + 64] = g * 128 + 2 * deint
    for g in range(KVH):
        vcols[g * 64:(g + 1) * 64] = g * 128 + 2 * np.arange(64) + 1
    d["Wk0_perm"] = np.ascontiguousarray(Wkvb[:, kcols])
    d["Wv0"] = np.ascontiguousarray(Wkvb[:, vcols])
    d["Wout"] = np.ascontiguousarray(inp["L0_attn_Wout"])
    d["bout_fm"] = np.ascontiguousarray(inp["L0_attn_bout"].reshape(CH, 128).T)
    coss, sins = _rope_tables()
    d["COSSq"] = coss * np.float32(SCALE0)
    d["SINSq"] = sins * np.float32(SCALE0)
    d["COSSk"] = coss
    d["SINSk"] = sins
    d["PERM"] = _perm_swap32()
    d["IDENT"] = np.eye(128, dtype=np.float32)
    causal = np.zeros((128, 512), np.float32)
    kk, q = np.meshgrid(np.arange(128), np.arange(128), indexing="ij")
    cm = np.where(kk > q, np.float32(-1e30), np.float32(0.0))
    for j in range(4):
        causal[:, j * 128:(j + 1) * 128] = cm
    d["CAUSAL"] = causal
    d["router_wT"] = np.ascontiguousarray(inp["L0_ffn_router_w"].T)
    d["w1T"] = np.ascontiguousarray(inp["L0_ffn_w1"].transpose(0, 2, 1))
    d["w2T"] = np.ascontiguousarray(inp["L0_ffn_w2"].transpose(0, 2, 1))
    d["Wq1"] = np.ascontiguousarray(inp["L1_attn_Wq"])
    d["bq1_fm"] = np.ascontiguousarray(inp["L1_attn_bq"].reshape(CH, 128).T)
    d["Wl1"] = np.ascontiguousarray(inp["L1_attn_Wl"])
    d["bl1_fm"] = np.ascontiguousarray(inp["L1_attn_bl"].reshape(2, 128).T)
    d["Wk1"] = np.ascontiguousarray(inp["L1_attn_Wk"])
    d["bk1_fm"] = np.ascontiguousarray(inp["L1_attn_bk"].reshape(CH, 128).T)
    d["Wv1"] = np.ascontiguousarray(inp["L1_attn_Wv"])
    d["bv1_row"] = np.ascontiguousarray(inp["L1_attn_bv"].reshape(1, 1024))
    d["Wo1"] = np.ascontiguousarray(inp["L1_attn_Wo"])
    d["bo1_fm"] = np.ascontiguousarray(inp["L1_attn_bo"].reshape(CH, 128).T)
    E0 = np.concatenate([inp["L1_ffn_W_exp"].T,
                         inp["L1_ffn_b_exp"].reshape(DH, 1)], axis=1)
    d["E0"] = np.ascontiguousarray(E0.reshape(32, 128, 1025).transpose(1, 0, 2))
    G0 = inp["L1_ffn_W_con"]
    d["G0"] = np.ascontiguousarray(G0.reshape(32, 128, 1024).transpose(1, 0, 2))
    up_chunk, up_diag = _bfly_mats(inp["L1_ffn_up_weights"], transpose=False)
    dn_chunk, dn_diag = _bfly_mats(inp["L1_ffn_down_weights"], transpose=True)
    d["UP_CHUNK"] = np.stack(up_chunk)
    d["DN_CHUNK"] = np.stack(dn_chunk)
    d["UP_DSELF"] = np.stack([a for a, b in up_diag])
    d["UP_DOTH"] = np.stack([b for a, b in up_diag])
    d["DN_DSELF"] = np.stack([a for a, b in dn_diag])
    d["DN_DOTH"] = np.stack([b for a, b in dn_diag])
    for k in ("UP_DSELF", "UP_DOTH", "DN_DSELF", "DN_DOTH"):
        d[k + "_dev"] = np.ascontiguousarray(
            d[k].transpose(2, 0, 1).reshape(128, 5 * 32))
    d["up_bias_fm"] = np.ascontiguousarray(
        inp["L1_ffn_up_bias"].reshape(32, 128).T)
    d["b_con_fm"] = np.ascontiguousarray(inp["L1_ffn_b_con"].reshape(CH, 128).T)
    d["down_bias_fm"] = np.ascontiguousarray(inp["L1_ffn_down_bias"].reshape(CH, 128).T)
    return d



from contextlib import ExitStack


class Buf:
    __slots__ = ("w", "r")

    def __init__(self):
        self.w = None
        self.r = {}

    def w_toks(self):
        return [self.w] if self.w is not None else []

    def r_toks(self):
        return list(self.r.items())

    def note_read(self, tok):
        if self.r.get(tok[0], 0) < tok[1]:
            self.r[tok[0]] = tok[1]

    def note_write(self, tok):
        self.w = tok
        self.r = {}


class BufView:

    def __init__(self, *bufs):
        self.bufs = bufs

    def w_toks(self):
        return [t for b in self.bufs for t in b.w_toks()]

    def r_toks(self):
        return [t for b in self.bufs for t in b.r_toks()]

    def note_read(self, tok):
        for b in self.bufs:
            b.note_read(tok)

    def note_write(self, tok):
        for b in self.bufs:
            b.note_write(tok)


class Ctx:
    def __init__(self, nc, es):
        self.nc = nc
        self.es = es
        self.sems = {}
        self.cnt = {}
        self.waited = {}
        self.eng = {"t": nc.tensor, "v": nc.vector, "s": nc.scalar,
                    "p": nc.gpsimd, "q": nc.sync}
        self.esem = {"t": "T", "v": "V", "s": "S", "p": "P", "q": "Q"}
        self.dma_k = {"T": 1, "V": 1, "S": 1, "P": 1, "Q": 1, "DQ": 16, "DP": 16, "DS": 16}

    def sem(self, name):
        if name not in self.sems:
            self.sems[name] = self.es.enter_context(self.nc.semaphore(name))
            self.cnt[name] = 0
        return self.sems[name]

    def _wait(self, ename, toks):
        need = {}
        for t in toks:
            if t is None:
                continue
            s, v = t
            if v > need.get(s, 0):
                need[s] = v
        eng = self.eng[ename]
        w = self.waited.setdefault(ename, {})
        for s, v in sorted(need.items()):
            if w.get(s, -1) >= v:
                continue
            eng.wait_ge(self.sem(s), v)
            w[s] = v

    def op(self, ename, f, reads=(), writes=(), sem=None, dep=()):
        sem = sem or self.esem[ename]
        toks = list(dep)
        for b in reads:
            toks.extend(b.w_toks())
        for b in writes:
            toks.extend(b.w_toks())
            toks.extend(b.r_toks())
        self._wait(ename, toks)
        inst = f()
        k = self.dma_k.get(sem, 1)
        h = self.sem(sem)
        inst.then_inc(h, k)
        self.cnt[sem] += k
        tok = (sem, self.cnt[sem])
        for b in reads:
            b.note_read(tok)
        for b in writes:
            b.note_write(tok)
        return tok

    def barrier(self, tokens=None):
        toks = [(s, c) for s, c in self.cnt.items()]
        for ename in self.eng:
            self._wait(ename, toks)


def _mk_dram(nc, name, shape, dtype, kind):
    return nc.dram_tensor(name, shape, dtype, kind=kind).ap()


class Phase:
    _n = 0

    def __init__(self, ctx):
        self.ctx = ctx
        self.es = ExitStack()
        self.bufs = {}
        Phase._n += 1
        self.pfx = f"ph{Phase._n}_"

    def sb(self, name, shape, dtype):
        t = self.es.enter_context(self.ctx.nc.sbuf_tensor(self.pfx + name, shape, dtype))
        b = Buf()
        self.bufs[name] = (t, b)
        return t, b

    def ps(self, name, shape, dtype=F32):
        t = self.es.enter_context(self.ctx.nc.psum_tensor(self.pfx + name, shape, dtype))
        b = Buf()
        self.bufs[name] = (t, b)
        return t, b

    def close(self):
        self.ctx.barrier()
        self.es.close()



def load_consts(ctx, ph, dram, names):
    out = {}
    for name, sbdtype in names:
        ap = dram[name]
        shp = list(ap.tensor.shape)
        if len(shp) == 2 and shp[0] > 128:
            assert shp[0] % 128 == 0
            ap = ap.rearrange("(c p) m -> p c m", p=128)
            shp = [128, shp[0] // 128, shp[1]]
        t, b = ph.sb("c_" + name, shp, sbdtype)
        eng = "p" if sbdtype != ap.tensor.dtype else "q"
        sem = "DP" if eng == "p" else "DQ"
        ctx.op(eng, lambda t=t, ap=ap, eng=eng: ctx.eng[eng].dma_start(out=t[:], in_=ap),
               writes=(b,), sem=sem)
        out[name] = (t, b)
    return out



def emit_ln(ctx, ph, sfx, x3, x_buf, dep, psS_p, psA_p, psB_p, NTc=NT, scratch=None):
    nc = ctx.nc
    key = "ln_stat" + sfx
    if key not in ph.bufs:
        ph.sb("ln_stat" + sfx, [1, 6 * NTc], F32)
        ph.sb("ln_xn" + sfx, [128, CH * NTc], BF16)
        if scratch is None:
            ph.sb("ln_xb" + sfx, [128, CH * NTc], BF16)
            ph.sb("ln_sq" + sfx, [128, CH * NTc], BF16)
    if scratch is None:
        xbt, xb_b = ph.bufs["ln_xb" + sfx]
        sqt, sq_b = ph.bufs["ln_sq" + sfx]
        xb, sq = xbt[:], sqt[:]
    else:
        xb, sq, shared = scratch
        xb_b = sq_b = shared
    stat, stat_b = ph.bufs["ln_stat" + sfx]
    xn, xn_b = ph.bufs["ln_xn" + sfx]
    psS, psS_b = psS_p
    psA, psA_b = psA_p
    psB, psB_b = psB_p
    ones128_bf, ones128_b = ph.bufs["ones128_bf"]
    ones1, ones1_b = ph.bufs["ones1"]

    ctx.op("v", lambda: nc.vector.tensor_copy(out=xb, in_=x3),
           reads=(x_buf,), writes=(xb_b,), dep=dep)
    ctx.op("s", lambda: nc.scalar.activation(out=sq, in_=xb, func=AF.Square),
           reads=(xb_b,), writes=(sq_b,))
    xbv = xb.rearrange("p (c n) -> p c n", c=CH)
    sqv = sq.rearrange("p (c n) -> p c n", c=CH)

    def sums():
        for c in range(CH):
            nc.tensor.matmul(out=psS[0:1, 0:NTc], lhsT=ones128_bf[:], rhs=xbv[:, c, :],
                             start=(c == 0), stop=(c == CH - 1))
        for c in range(CH):
            i = nc.tensor.matmul(out=psS[0:1, NTc:2 * NTc], lhsT=ones128_bf[:],
                                 rhs=sqv[:, c, :], start=(c == 0), stop=(c == CH - 1))
        return i
    ctx.op("t", sums, reads=(xb_b, sq_b, ones128_b), writes=(psS_b,))
    n = NTc
    mean, ms, var, sd, a_r, b_r = (stat[:1, i * n:(i + 1) * n] for i in range(6))
    ctx.op("v", lambda: nc.vector.tensor_scalar(out=mean, in0=psS[0:1, 0:n],
           scalar1=1.0 / E, scalar2=None, op0=ALU.mult), reads=(psS_b,), writes=(stat_b,))
    ctx.op("v", lambda: nc.vector.tensor_scalar(out=ms, in0=psS[0:1, n:2 * n],
           scalar1=1.0 / E, scalar2=None, op0=ALU.mult), reads=(psS_b,), writes=(stat_b,))
    ctx.op("v", lambda: nc.vector.tensor_mul(out=var, in0=mean, in1=mean),
           reads=(stat_b,), writes=(stat_b,))
    ctx.op("v", lambda: nc.vector.tensor_sub(out=var, in0=ms, in1=var),
           reads=(stat_b,), writes=(stat_b,))
    epst, epst_b = ph.bufs["epsln"]
    ctx.op("s", lambda: nc.scalar.activation(out=sd, in_=var, func=AF.Sqrt,
           bias=epst[:]), reads=(stat_b, epst_b), writes=(stat_b,))
    ctx.op("v", lambda: nc.vector.reciprocal(out=a_r, in_=sd),
           reads=(stat_b,), writes=(stat_b,))
    ctx.op("v", lambda: nc.vector.scalar_tensor_tensor(out=b_r, in0=mean, scalar=-1.0,
           in1=a_r, op0=ALU.mult, op1=ALU.mult), reads=(stat_b,), writes=(stat_b,))
    ctx.op("t", lambda: nc.tensor.matmul(out=psA[:], lhsT=ones1[:], rhs=a_r,
           start=True, stop=True), reads=(stat_b, ones1_b), writes=(psA_b,))
    ctx.op("t", lambda: nc.tensor.matmul(out=psB[:], lhsT=ones1[:], rhs=b_r,
           start=True, stop=True), reads=(stat_b, ones1_b), writes=(psB_b,))
    xnv = xn[:].rearrange("p (c n) -> p c n", c=CH)
    tok = None
    for c in range(CH):
        ctx.op("v", lambda c=c: nc.vector.tensor_mul(out=xnv[:, c, :], in0=x3[:, c, :],
               in1=psA[:]), reads=(x_buf, psA_b), writes=(xn_b,))
        tok = ctx.op("v", lambda c=c: nc.vector.tensor_add(out=xnv[:, c, :],
               in0=xnv[:, c, :], in1=psB[:]), reads=(psB_b, xn_b), writes=(xn_b,))
    return xn, xn_b, tok


def init_common(ctx, ph):
    nc = ctx.nc
    o128, o128_b = ph.sb("ones128_bf", [128, 1], BF16)
    o1, o1_b = ph.sb("ones1", [1, 128], F32)
    eps, eps_b = ph.sb("epsln", [1, 1], F32)
    ctx.op("v", lambda: nc.vector.memset(o128[:], 1.0), writes=(o128_b,))
    ctx.op("v", lambda: nc.vector.memset(o1[:], 1.0), writes=(o1_b,))
    ctx.op("v", lambda: nc.vector.memset(eps[:], LN_EPS), writes=(eps_b,))


def build_phaseA(ctx, dram):
    nc = ctx.nc
    ph = Phase(ctx)
    init_common(ctx, ph)
    cst = load_consts(ctx, ph, dram, [
        ("wqa", BF16), ("wqbp", BF16), ("wkva", BF16), ("wk0p", BF16), ("wv0", BF16),
        ("cossq", F32), ("sinsq", F32), ("cossk", F32), ("sinsk", F32),
        ("permbf", BF16), ("ident", F32),
    ])
    wqa, wqa_b = cst["wqa"]
    wqav = wqa[:]
    wqbp, wqbp_b = cst["wqbp"]
    wkva, wkva_b = cst["wkva"]
    wkvav = wkva[:]
    wk0p, wk0p_b = cst["wk0p"]
    wv0, wv0_b = cst["wv0"]
    cossq, cossq_b = cst["cossq"]
    sinsq, sinsq_b = cst["sinsq"]
    cossk, cossk_b = cst["cossk"]
    sinsk, sinsk_b = cst["sinsk"]
    perm, perm_b = cst["permbf"]
    ident, ident_b = cst["ident"]

    xin = dram["xin"]
    xin_flat = xin.rearrange("s t f -> (s t) f")
    x0_d = dram["x0"]
    q0_d = dram["q0"]
    k0_d = dram["k0"]
    v0_d = dram["v0"]
    x0_db, q0_db, k0_db, v0_db = Buf(), Buf(), Buf(), Buf()
    q0v = q0_d.rearrange("p (s h t) -> p s h t", s=SEQ, h=8)
    k0v = k0_d.rearrange("p (s h t) -> p s h t", s=SEQ, h=2)
    v0v = v0_d.rearrange("p (s u m) -> p s u m", s=SEQ, u=16)

    xt = [ph.sb(f"xt{i}", [128, 4 * 1024], F32) for i in range(2)]
    x0sb = [ph.sb(f"x0sb{i}", [128, CH * NT], F32) for i in range(2)]
    latq = [ph.sb(f"latq{i}", [64, NT], BF16) for i in range(2)]
    latkv = [ph.sb(f"latkv{i}", [32, NT], BF16) for i in range(2)]
    qbf = ph.sb("qbf", [128, NT], BF16)
    ro1 = ph.sb("ro1", [128, NT], F32)
    ro2 = ph.sb("ro2", [128, NT], F32)
    qout = ph.sb("qout", [128, NT], BF16)
    vout = ph.sb("vout", [128, 4 * 256], BF16)

    ps_x = ph.ps("ps_x", [128, 1024], F32)
    ps_lat = ph.ps("ps_lat", [128, NT], F32)
    ps_q = [ph.ps(f"ps_q{i}", [128, NT], F32) for i in range(2)]
    ps_sw = ph.ps("ps_sw", [128, NT], F32)
    ps_A = ph.ps("ps_A", [128, NT], F32)
    ps_B = ph.ps("ps_B", [128, NT], F32)
    ps_v = ps_lat

    for nt in range(NTILES):
        s, a4 = nt // 4, nt % 4
        pos0 = a4 * NT
        xt_t, xt_b = xt[nt % 2]
        x0_t, x0_b = x0sb[nt % 2]
        src = xin_flat[nt * NT:(nt + 1) * NT, :].rearrange("(a p) f -> p a f", p=128)
        ctx.op("q", lambda xt_t=xt_t, src=src: nc.sync.dma_start(out=xt_t[:].rearrange(
            "p (a f) -> p a f", a=4), in_=src), writes=(xt_b,), sem="DQ")
        xtv = xt_t[:].rearrange("p (a f) -> p a f", a=4)
        x0v = x0_t[:].rearrange("p (c n) -> p c n", c=CH)
        for a in range(4):
            def trs(a=a, xtv=xtv):
                i = None
                for c in range(CH):
                    i = nc.tensor.transpose(out=ps_x[0][:, c * 128:(c + 1) * 128],
                                            in_=xtv[:, a, c * 128:(c + 1) * 128],
                                            identity=ident[:])
                return i
            ctx.op("t", trs, reads=(xt_b, ident_b), writes=(ps_x[1],))
            ctx.op("v", lambda a=a, x0v=x0v: nc.vector.tensor_copy(
                out=x0v[:, :, a * 128:(a + 1) * 128],
                in_=ps_x[0][:].rearrange("p (c n) -> p c n", c=CH)),
                reads=(ps_x[1],), writes=(x0_b,))
        ctx.op("q", lambda x0v=x0v, nt=nt: nc.sync.dma_start(
            out=x0_d[:, :, nt * NT:(nt + 1) * NT], in_=x0v), reads=(x0_b,), sem="DQ")
        xn, xn_b, _ = emit_ln(ctx, ph, "A", x0v, x0_b, (), ps_x, ps_A, ps_B)
        xnv = xn[:].rearrange("p (c n) -> p c n", c=CH)
        latq_t, latq_b = latq[nt % 2]
        latkv_t, latkv_b = latkv[nt % 2]

        def mm_lat():
            for c in range(CH):
                nc.tensor.matmul(out=ps_lat[0][0:64, :], lhsT=wqav[:, c, :],
                                 rhs=xnv[:, c, :], start=(c == 0), stop=(c == CH - 1))
            for c in range(CH):
                i = nc.tensor.matmul(out=ps_lat[0][64:96, :], lhsT=wkvav[:, c, :],
                                     rhs=xnv[:, c, :], start=(c == 0), stop=(c == CH - 1))
            return i
        ctx.op("t", mm_lat, reads=(xn_b, wqa_b, wkva_b), writes=(ps_lat[1],))
        ctx.op("v", lambda latq_t=latq_t: nc.vector.tensor_copy(out=latq_t[:],
               in_=ps_lat[0][0:64, :]), reads=(ps_lat[1],), writes=(latq_b,))
        ctx.op("s", lambda latkv_t=latkv_t: nc.scalar.copy(out=latkv_t[:],
               in_=ps_lat[0][64:96, :]), reads=(ps_lat[1],), writes=(latkv_b,))
        for hp in range(8):
            pq, pq_b = ps_q[hp % 2]
            ctx.op("t", lambda hp=hp, pq=pq, latq_t=latq_t: nc.tensor.matmul(
                out=pq[:], lhsT=wqbp[:, hp * 128:(hp + 1) * 128], rhs=latq_t[:],
                start=True, stop=True), reads=(latq_b, wqbp_b), writes=(pq_b,))
            ctx.op("s", lambda pq=pq: nc.scalar.copy(out=qbf[0][:], in_=pq[:]),
                   reads=(pq_b,), writes=(qbf[1],))
            ctx.op("t", lambda: nc.tensor.matmul(out=ps_sw[0][:], lhsT=perm[:],
                   rhs=qbf[0][:], start=True, stop=True),
                   reads=(qbf[1], perm_b), writes=(ps_sw[1],))
            ctx.op("v", lambda pos0=pos0: nc.vector.tensor_mul(out=ro1[0][:],
                   in0=ps_sw[0][:], in1=sinsq[:, pos0:pos0 + NT]),
                   reads=(ps_sw[1], sinsq_b), writes=(ro1[1],))
            ctx.op("v", lambda pos0=pos0: nc.vector.tensor_mul(out=ro2[0][:],
                   in0=qbf[0][:], in1=cossq[:, pos0:pos0 + NT]),
                   reads=(qbf[1], cossq_b), writes=(ro2[1],))
            ctx.op("v", lambda: nc.vector.tensor_add(out=qout[0][:], in0=ro1[0][:],
                   in1=ro2[0][:]), reads=(ro1[1], ro2[1]), writes=(qout[1],))
            ctx.op("q", lambda s=s, hp=hp, pos0=pos0: nc.sync.dma_start(
                out=q0v[:, s, hp, pos0:pos0 + NT], in_=qout[0][:]),
                reads=(qout[1],), sem="DQ", writes=(q0_db,))
        for kp in range(2):
            pq, pq_b = ps_q[kp % 2]
            ctx.op("t", lambda kp=kp, pq=pq, latkv_t=latkv_t: nc.tensor.matmul(
                out=pq[:], lhsT=wk0p[:, kp * 128:(kp + 1) * 128], rhs=latkv_t[:],
                start=True, stop=True), reads=(latkv_b, wk0p_b), writes=(pq_b,))
            ctx.op("s", lambda pq=pq: nc.scalar.copy(out=qbf[0][:], in_=pq[:]),
                   reads=(pq_b,), writes=(qbf[1],))
            ctx.op("t", lambda: nc.tensor.matmul(out=ps_sw[0][:], lhsT=perm[:],
                   rhs=qbf[0][:], start=True, stop=True),
                   reads=(qbf[1], perm_b), writes=(ps_sw[1],))
            ctx.op("v", lambda pos0=pos0: nc.vector.tensor_mul(out=ro1[0][:],
                   in0=ps_sw[0][:], in1=sinsk[:, pos0:pos0 + NT]),
                   reads=(ps_sw[1], sinsk_b), writes=(ro1[1],))
            ctx.op("v", lambda pos0=pos0: nc.vector.tensor_mul(out=ro2[0][:],
                   in0=qbf[0][:], in1=cossk[:, pos0:pos0 + NT]),
                   reads=(qbf[1], cossk_b), writes=(ro2[1],))
            ctx.op("v", lambda: nc.vector.tensor_add(out=qout[0][:], in0=ro1[0][:],
                   in1=ro2[0][:]), reads=(ro1[1], ro2[1]), writes=(qout[1],))
            ctx.op("q", lambda s=s, kp=kp, pos0=pos0: nc.sync.dma_start(
                out=k0v[:, s, kp, pos0:pos0 + NT], in_=qout[0][:]),
                reads=(qout[1],), sem="DQ", writes=(k0_db,))
        voutv = vout[0][:].rearrange("p (a m) -> p a m", a=4)
        for sub in range(4):
            ctx.op("t", lambda sub=sub, latkv_t=latkv_t: nc.tensor.matmul(
                out=ps_v[0][:, 0:256], lhsT=latkv_t[:, sub * 128:(sub + 1) * 128],
                rhs=wv0[:], start=True, stop=True),
                reads=(latkv_b, wv0_b), writes=(ps_v[1],))
            ctx.op("v", lambda sub=sub: nc.vector.tensor_copy(out=voutv[:, sub, :],
                   in_=ps_v[0][:, 0:256]), reads=(ps_v[1],), writes=(vout[1],))
        ctx.op("q", lambda s=s, a4=a4: nc.sync.dma_start(
            out=v0v[:, s, a4 * 4:(a4 + 1) * 4, :], in_=voutv),
            reads=(vout[1],), sem="DQ", writes=(v0_db,))
    ph.close()



def _load_qk_seq(ctx, nc, dst, dst_b, src_view, s, nh, sem="DQ"):
    dstv = dst[:].rearrange("p (h t) -> p h t", h=nh)
    toks = []
    for half in range(2):
        for par in range(2):
            out_ap = dstv[half * 64:(half + 1) * 64, par::2, :]
            in_ap = src_view[par * 64:par * 64 + 64, s, :, half * 1024:(half + 1) * 1024]
            toks.append(ctx.op("q", lambda o=out_ap, i=in_ap: nc.sync.dma_start(
                out=o, in_=i), writes=(dst_b,), sem=sem))
    return toks[-1]


def emit_bound(ctx, ph, nc, q_sb, q_b, ncols, bmax, bmax_b, sq, sq_b, psn, psn_b,
               mred, mred_b, nrows=64):
    ones128_bf, ones128_b = ph.bufs["ones128_bf"]
    nchunks = ncols // 512
    first = True
    for ci in range(nchunks):
        sl = slice(ci * 512, (ci + 1) * 512)
        ctx.op("s", lambda sl=sl: nc.scalar.activation(out=sq[0:nrows, :],
               in_=q_sb[0:nrows, sl], func=AF.Square), reads=(q_b,), writes=(sq_b,))
        ctx.op("t", lambda: nc.tensor.matmul(out=psn[0:1, 0:512],
               lhsT=ones128_bf[0:nrows, :], rhs=sq[0:nrows, :], start=True, stop=True),
               reads=(sq_b, ones128_b), writes=(psn_b,))
        ctx.op("v", lambda: nc.vector.reduce_max(out=mred[0:1, 0:1], in_=psn[0:1, 0:512],
               axis=AX.X), reads=(psn_b,), writes=(mred_b,))
        if first:
            ctx.op("v", lambda: nc.vector.tensor_copy(out=bmax[0:1, :], in_=mred[0:1, 0:1]),
                   reads=(mred_b,), writes=(bmax_b,))
            first = False
        else:
            ctx.op("v", lambda: nc.vector.tensor_max(out=bmax[0:1, :], in0=bmax[0:1, :],
                   in1=mred[0:1, 0:1]), reads=(mred_b, bmax_b), writes=(bmax_b,))


def build_phaseB(ctx, dram):
    nc = ctx.nc
    ph = Phase(ctx)
    init_common(ctx, ph)
    cst = load_consts(ctx, ph, dram, [
        ("bout_fm", F32), ("rwt", BF16), ("causal", F32),
        ("identf", F32),
    ])
    wout, wout_b = ph.sb("wout_sb", [64, 16 * 1024], BF16)
    ctx.op("p", lambda: nc.gpsimd.dma_start(
        out=wout[:].rearrange("r (h m) -> r h m", h=16),
        in_=dram["wout"].rearrange("(h r) m -> r h m", r=64)),
        writes=(wout_b,), sem="DP")
    woutv = wout[:].rearrange("r (h m) -> r h m", h=16)
    bout, bout_b = cst["bout_fm"]
    rwt, rwt_b = cst["rwt"]
    causal, causal_b = cst["causal"]
    identf, identf_b = cst["identf"]
    ones1, ones1_b = ph.bufs["ones1"]

    q0v = dram["q0"].rearrange("p (s h t) -> p s h t", s=SEQ, h=8)
    k0v = dram["k0"].rearrange("p (s h t) -> p s h t", s=SEQ, h=2)
    v0v = dram["v0"].rearrange("p (s u m) -> p s u m", s=SEQ, u=16)
    x0_d = dram["x0"]
    x1_d = dram["x1"]
    xn1_d = dram["xn1"]
    xn1v = xn1_d.rearrange("p (c t) -> p c t", c=CH)
    gates_d = dram["gates"]
    x1_db, xn1_db, gates_db = Buf(), Buf(), Buf()

    q_sb, q_b = ph.sb("q_sb", [64, 16 * 512], BF16)
    k_sb, k_b = ph.sb("k_sb", [64, 4 * 2048], BF16)
    v_sb, v_b = ph.sb("v_sb", [128, 16 * 260], BF16)
    pt_sb, pt_b = ph.sb("pt_sb", [128, 16 * 512], BF16)
    attn, attn_b = ph.sb("attn", [64, 16 * 512], BF16)
    sqt, sqt_b = ph.sb("sqt", [128, 512], BF16)
    mred, mred_b = ph.sb("mred", [1, 8], F32)
    bq, bq_b = ph.sb("bq", [1, 1], F32)
    bk, bk_b = ph.sb("bk", [1, 1], F32)
    negb, negb_b = ph.sb("negb", [128, 1], F32)
    rden, rden_b = ph.sb("rden", [1, 512], F32)
    rb, rb_b = ph.sb("rb", [64, 512], BF16)
    x0t = [ph.sb(f"x0t{i}", [128, CH * NT], F32) for i in range(2)]
    x1t, x1t_b = ph.sb("x1t", [128, CH * NT], F32)
    glog, glog_b = ph.sb("glog", [128, 8], F32)
    gm, gm_b = ph.sb("gm", [128, 4], F32)
    gmask, gmask_b = ph.sb("gmask", [128, 8], F32)
    gtm, gtm_b = ph.sb("gtm", [128, 8], F32)
    gexp, gexp_b = ph.sb("gexp", [128, 8], F32)
    grow, grow_b = ph.sb("grow", [8, 512], F32)

    ps_st, ps_st_b = ph.ps("ps_st", [128, 512], F32)
    ps_av = [ph.ps(f"ps_av{i}", [128, 512], F32) for i in range(2)]
    ps_r, ps_r_b = ph.ps("ps_r", [128, 512], F32)
    ps_w, ps_w_b = ph.ps("ps_w", [128, 1024], F32)
    ps_A, ps_A_b = ph.ps("ps_A", [128, NT], F32)
    ps_B, ps_B_b = ph.ps("ps_B", [128, NT], F32)

    qv = q_sb[:].rearrange("p (h t) -> p h t", h=16)
    kv = k_sb[:].rearrange("p (h t) -> p h t", h=4)
    vv = v_sb[:].rearrange("p (u g m) -> p u g m", u=16, g=4)
    ptv = pt_sb[:].rearrange("p (u n) -> p u n", u=16)
    attnv = attn[:].rearrange("p (h n) -> p h n", h=16)

    for s in range(SEQ):
        for par in range(2):
            ctx.op("q", lambda par=par, s=s: nc.sync.dma_start(
                out=kv[:, par::2, :], in_=k0v[par * 64:par * 64 + 64, s, :, :]),
                writes=(k_b,), sem="DQ")
        ctx.op("q", lambda s=s: nc.sync.dma_start(
            out=vv[:, :, :, 0:64],
            in_=v0v[:, s].rearrange("p u (g m) -> p u g m", g=4)),
            writes=(v_b,), sem="DQ")
        ctx.op("v", lambda: nc.vector.memset(vv[:, :, :, 64:65], 1.0), writes=(v_b,))
        emit_bound(ctx, ph, nc, k_sb, k_b, 4 * 2048, bk, bk_b, sqt, sqt_b,
                   ps_r, ps_r_b, mred, mred_b)

        for qtg in range(4):
            nt = s * 4 + qtg
            for par in range(2):
                ctx.op("q", lambda par=par, s=s, qtg=qtg: nc.sync.dma_start(
                    out=qv[:, par::2, :],
                    in_=q0v[par * 64:par * 64 + 64, s, :, qtg * 512:(qtg + 1) * 512]),
                    writes=(q_b,), sem="DQ")
            emit_bound(ctx, ph, nc, q_sb, q_b, 16 * 512, bq, bq_b, sqt, sqt_b,
                       ps_r, ps_r_b, mred, mred_b)
            ctx.op("v", lambda: nc.vector.tensor_mul(out=bq[0:1, :], in0=bq[0:1, :],
                   in1=bk[0:1, :]), reads=(bk_b, bq_b), writes=(bq_b,))
            ctx.op("s", lambda: nc.scalar.activation(out=mred[0:1, 4:5], in_=bq[0:1, :],
                   func=AF.Sqrt, scale=1.0), reads=(bq_b,), writes=(mred_b,))
            ctx.op("t", lambda: nc.tensor.matmul(out=ps_r[:, 0:1], lhsT=ones1[:],
                   rhs=mred[0:1, 4:5], start=True, stop=True),
                   reads=(mred_b, ones1_b), writes=(ps_r_b,))
            ctx.op("v", lambda: nc.vector.tensor_scalar(out=negb[:], in0=ps_r[:, 0:1],
                   scalar1=-1.0, scalar2=None, op0=ALU.mult),
                   reads=(ps_r_b,), writes=(negb_b,))
            x0_t, x0_b = x0t[qtg % 2]
            x0v3 = x0_t[:].rearrange("p (c n) -> p c n", c=CH)
            ctx.op("q", lambda x0_t=x0_t, nt=nt: nc.sync.dma_start(
                out=x0_t[:].rearrange("p (c n) -> p c n", c=CH),
                in_=x0_d[:, :, nt * NT:(nt + 1) * NT]), writes=(x0_b,), sem="DQ")
            for sub in range(4):
                qt = qtg * 4 + sub
                for g in range(4):
                    pav, pav_b = ps_av[(qt * 4 + g) % 2]
                    rhs = qv[:, 4 * g:4 * g + 4, sub * 128:(sub + 1) * 128]
                    for kt in range(qt + 1):
                        lhsT = kv[:, g, kt * 128:(kt + 1) * 128]
                        ctx.op("t", lambda lhsT=lhsT, rhs=rhs: nc.tensor.matmul(
                            out=ps_st[:], lhsT=lhsT, rhs=rhs, start=True, stop=True),
                            reads=(q_b, k_b), writes=(ps_st_b,))
                        if kt == qt:
                            ctx.op("v", lambda: nc.vector.tensor_add(out=ps_st[:],
                                   in0=ps_st[:], in1=causal[:]),
                                   reads=(causal_b, ps_st_b), writes=(ps_st_b,))
                        ctx.op("s", lambda kt=kt: nc.scalar.activation(
                            out=ptv[:, kt, :], in_=ps_st[:], func=AF.Exp,
                            bias=negb[:]), reads=(ps_st_b, negb_b), writes=(pt_b,))
                    def av(qt=qt, g=g, pav=pav):
                        i = None
                        for kt in range(qt + 1):
                            i = nc.tensor.matmul(out=pav[0:65, :],
                                                 lhsT=vv[:, kt, g, :],
                                                 rhs=ptv[:, kt, :],
                                                 start=(kt == 0), stop=(kt == qt))
                        return i
                    ctx.op("t", av, reads=(pt_b, v_b), writes=(pav_b,))
                    ctx.op("v", lambda pav=pav: nc.vector.reciprocal(
                        out=rden[0:1, :], in_=pav[64:65, :]),
                        reads=(pav_b,), writes=(rden_b,))
                    ctx.op("t", lambda: nc.tensor.matmul(out=ps_r[0:64, :],
                           lhsT=ones1[:, 0:64], rhs=rden[0:1, :], start=True,
                           stop=True), reads=(rden_b, ones1_b), writes=(ps_r_b,))
                    ctx.op("v", lambda: nc.vector.tensor_copy(out=rb[:],
                           in_=ps_r[0:64, :]), reads=(ps_r_b,), writes=(rb_b,))
                    dst = attnv[:, 4 * g:4 * g + 4, sub * 128:(sub + 1) * 128]
                    ctx.op("v", lambda pav=pav, dst=dst: nc.vector.tensor_mul(
                        out=dst, in0=pav[0:64, :].rearrange("p (a n) -> p a n", a=4),
                        in1=rb[:].rearrange("p (a n) -> p a n", a=4)),
                        reads=(pav_b, rb_b), writes=(attn_b,))
            x1v = x1t[:].rearrange("p (c n) -> p c n", c=CH)
            for oc in range(CH):
                half = oc % 2
                osl = slice(half * 512, half * 512 + 512)

                def wmm(oc=oc, osl=osl):
                    i = None
                    for h in range(16):
                        i = nc.tensor.matmul(
                            out=ps_w[:, osl],
                            lhsT=woutv[:, h, oc * 128:(oc + 1) * 128],
                            rhs=attnv[:, h, :], start=(h == 0), stop=(h == 15))
                    return i
                ctx.op("t", wmm, reads=(attn_b, wout_b), writes=(ps_w_b,))
                ctx.op("v", lambda oc=oc, osl=osl: nc.vector.scalar_tensor_tensor(
                    out=x1v[:, oc, :], in0=ps_w[:, osl], scalar=bout[:, oc:oc + 1],
                    in1=x0v3[:, oc, :], op0=ALU.add, op1=ALU.add),
                    reads=(ps_w_b, bout_b, x0_b), writes=(x1t_b,))
            ctx.op("q", lambda nt=nt: nc.sync.dma_start(
                out=x1_d[:, :, nt * NT:(nt + 1) * NT], in_=x1v),
                reads=(x1t_b,), writes=(x1_db,), sem="DQ")
            xn, xn_b, _ = emit_ln(ctx, ph, "B", x1v, x1t_b, (),
                                  (ps_w, ps_w_b), (ps_A, ps_A_b), (ps_B, ps_B_b))
            xnv = xn[:].rearrange("p (c n) -> p c n", c=CH)
            ctx.op("q", lambda nt=nt, xnv=xnv: nc.sync.dma_start(
                out=xn1v[:, :, nt * NT:(nt + 1) * NT], in_=xnv),
                reads=(xn_b,), writes=(xn1_db,), sem="DQ")
            for sub in range(4):
                def gmm(sub=sub, xnv=xnv):
                    i = None
                    for c in range(CH):
                        i = nc.tensor.matmul(out=ps_r[:, 0:8],
                                             lhsT=xnv[:, c, sub * 128:(sub + 1) * 128],
                                             rhs=rwt[:, c, :],
                                             start=(c == 0), stop=(c == CH - 1))
                    return i
                ctx.op("t", gmm, reads=(xn_b, rwt_b), writes=(ps_r_b,))
                m1 = gm[:, 0:1]
                negm1 = gm[:, 1:2]
                m2 = gm[:, 2:3]
                rsum = gm[:, 3:4]
                ctx.op("v", lambda: nc.vector.tensor_copy(out=glog[:], in_=ps_r[:, 0:8]),
                       reads=(ps_r_b,), writes=(glog_b,))
                ctx.op("v", lambda: nc.vector.reduce_max(out=m1, in_=glog[:], axis=AX.X),
                       reads=(glog_b,), writes=(gm_b,))
                ctx.op("v", lambda: nc.vector.tensor_scalar(out=gmask[:], in0=glog[:],
                       scalar1=m1, scalar2=None, op0=ALU.is_ge),
                       reads=(glog_b, gm_b), writes=(gmask_b,))
                ctx.op("v", lambda: nc.vector.scalar_tensor_tensor(out=gmask[:],
                       in0=gmask[:], scalar=-1e30, in1=glog[:], op0=ALU.mult,
                       op1=ALU.add), reads=(glog_b, gmask_b), writes=(gmask_b,))
                ctx.op("v", lambda: nc.vector.reduce_max(out=m2, in_=gmask[:], axis=AX.X),
                       reads=(gmask_b,), writes=(gm_b,))
                ctx.op("v", lambda: nc.vector.tensor_scalar(out=gmask[:], in0=glog[:],
                       scalar1=m2, scalar2=None, op0=ALU.is_ge),
                       reads=(glog_b, gm_b), writes=(gmask_b,))
                ctx.op("v", lambda: nc.vector.tensor_scalar(out=negm1, in0=m1,
                       scalar1=-1.0, scalar2=None, op0=ALU.mult),
                       reads=(gm_b,), writes=(gm_b,))
                ctx.op("s", lambda: nc.scalar.activation(out=gexp[:], in_=glog[:],
                       func=AF.Exp, bias=negm1), reads=(glog_b, gm_b), writes=(gexp_b,))
                ctx.op("v", lambda: nc.vector.tensor_mul(out=gexp[:], in0=gexp[:],
                       in1=gmask[:]), reads=(gmask_b, gexp_b), writes=(gexp_b,))
                ctx.op("v", lambda: nc.vector.reduce_sum(out=rsum, in_=gexp[:], axis=AX.X),
                       reads=(gexp_b,), writes=(gm_b,))
                ctx.op("v", lambda: nc.vector.reciprocal(out=rsum, in_=rsum),
                       reads=(gm_b,), writes=(gm_b,))
                ctx.op("v", lambda: nc.vector.tensor_scalar(out=gtm[:], in0=gexp[:],
                       scalar1=rsum, scalar2=None, op0=ALU.mult),
                       reads=(gexp_b, gm_b), writes=(gtm_b,))
                ctx.op("t", lambda: nc.tensor.matmul(out=ps_r[0:8, 8:136], lhsT=gtm[:],
                       rhs=identf[:], start=True, stop=True),
                       reads=(gtm_b, identf_b), writes=(ps_r_b,))
                ctx.op("v", lambda sub=sub: nc.vector.tensor_copy(
                    out=grow[:, sub * 128:(sub + 1) * 128], in_=ps_r[0:8, 8:136]),
                    reads=(ps_r_b,), writes=(grow_b,))
            ctx.op("q", lambda nt=nt: nc.sync.dma_start(
                out=gates_d[:, nt * NT:(nt + 1) * NT], in_=grow[:]),
                reads=(grow_b,), writes=(gates_db,), sem="DQ")
    ph.close()



def build_phaseC(ctx, dram):
    nc = ctx.nc
    ph = Phase(ctx)
    init_common(ctx, ph)
    ones1, ones1_b = ph.bufs["ones1"]
    xn1v = dram["xn1"].rearrange("p (c t) -> p c t", c=CH)
    x1_d = dram["x1"]
    x2_d = dram["x2"]
    acc_d = dram["accC"]
    gates_d = dram["gates"]
    w1T_d = dram["w1T"]
    w2T_d = dram["w2T"]
    x2_db, acc_db = Buf(), Buf()

    w1 = [ph.sb(f"w1_{i}", [128, 8 * 2048], BF16) for i in range(2)]
    w2 = [ph.sb(f"w2_{i}", [128, 16 * 1024], BF16) for i in range(2)]
    xnt = [ph.sb(f"xnt{i}", [128, CH * NT], BF16) for i in range(2)]
    hsc, hsc_b = ph.sb("hsc", [128, 16 * NT], BF16)
    gb, gb_b = ph.sb("gb", [128, NT], BF16)
    grow, grow_b = ph.sb("growC", [1, NT], F32)
    acc, acc_b = ph.sb("accsb", [128, CH * NT], F32)

    ps_h = [ph.ps(f"ps_h{i}", [128, NT], F32) for i in range(2)]
    ps_y = [ph.ps(f"ps_y{i}", [128, NT], F32) for i in range(2)]
    ps_g, ps_g_b = ph.ps("ps_g", [128, NT], F32)

    def load_w(e, slot):
        w1t, w1b = w1[slot]
        w2t, w2b = w2[slot]
        ctx.op("p", lambda e=e, w1t=w1t: nc.gpsimd.dma_start(
            out=w1t[:].rearrange("p (k m) -> p k m", k=8),
            in_=w1T_d[e].rearrange("(k p) m -> p k m", p=128)),
            writes=(w1b,), sem="DP")
        ctx.op("p", lambda e=e, w2t=w2t: nc.gpsimd.dma_start(
            out=w2t[:].rearrange("p (k m) -> p k m", k=16),
            in_=w2T_d[e].rearrange("(k p) m -> p k m", p=128)),
            writes=(w2b,), sem="DP")

    load_w(0, 0)
    for e in range(NE):
        if e + 1 < NE:
            load_w(e + 1, (e + 1) % 2)
        w1t, w1b = w1[e % 2]
        w2t, w2b = w2[e % 2]
        w1v = w1t[:].rearrange("p (k m) -> p k m", k=8)
        w2v = w2t[:].rearrange("p (k m) -> p k m", k=16)
        hv = hsc[:].rearrange("p (k m) -> p k m", k=16)
        accv = acc[:].rearrange("p (c n) -> p c n", c=CH)
        for nt in range(NTILES):
            xt, xt_b = xnt[nt % 2]
            xv = xt[:].rearrange("p (c n) -> p c n", c=CH)
            ctx.op("q", lambda xt=xt, nt=nt: nc.sync.dma_start(
                out=xt[:].rearrange("p (c n) -> p c n", c=CH),
                in_=xn1v[:, :, nt * NT:(nt + 1) * NT]), writes=(xt_b,), sem="DQ")
            ctx.op("q", lambda e=e, nt=nt: nc.sync.dma_start(
                out=grow[:], in_=gates_d[e:e + 1, nt * NT:(nt + 1) * NT]),
                writes=(grow_b,), sem="DQ")
            ctx.op("t", lambda: nc.tensor.matmul(out=ps_g[:], lhsT=ones1[:],
                   rhs=grow[0:1, :], start=True, stop=True),
                   reads=(grow_b, ones1_b), writes=(ps_g_b,))
            ctx.op("v", lambda: nc.vector.tensor_copy(out=gb[:], in_=ps_g[:]),
                   reads=(ps_g_b,), writes=(gb_b,))
            for mc in range(16):
                ps, ps_b = ps_h[mc % 2]

                def hmm(mc=mc, ps=ps):
                    i = None
                    for kc in range(CH):
                        i = nc.tensor.matmul(out=ps[:],
                                             lhsT=w1v[:, kc, mc * 128:(mc + 1) * 128],
                                             rhs=xv[:, kc, :],
                                             start=(kc == 0), stop=(kc == CH - 1))
                    return i
                ctx.op("t", hmm, reads=(xt_b, w1b), writes=(ps_b,))
                ctx.op("s", lambda mc=mc, ps=ps: nc.scalar.activation(
                    out=hv[:, mc, :], in_=ps[:], func=AF.Gelu),
                    reads=(ps_b,), writes=(hsc_b,))
                ctx.op("v", lambda mc=mc: nc.vector.tensor_mul(
                    out=hv[:, mc, :], in0=hv[:, mc, :], in1=gb[:]),
                    reads=(gb_b, hsc_b), writes=(hsc_b,))
            src = x1_d if e == 0 else acc_d
            ctx.op("q", lambda src=src, nt=nt: nc.sync.dma_start(
                out=acc[:].rearrange("p (c n) -> p c n", c=CH),
                in_=src[:, :, nt * NT:(nt + 1) * NT]), writes=(acc_b,), sem="DQ",
                dep=(acc_db.w,) if e > 0 else ())
            for oc in range(CH):
                ps, ps_b = ps_y[oc % 2]

                def ymm(oc=oc, ps=ps):
                    i = None
                    for kc in range(16):
                        i = nc.tensor.matmul(out=ps[:],
                                             lhsT=w2v[:, kc, oc * 128:(oc + 1) * 128],
                                             rhs=hv[:, kc, :],
                                             start=(kc == 0), stop=(kc == 15))
                    return i
                ctx.op("t", ymm, reads=(hsc_b, w2b), writes=(ps_b,))
                ctx.op("v", lambda oc=oc, ps=ps: nc.vector.tensor_add(
                    out=accv[:, oc, :], in0=ps[:], in1=accv[:, oc, :]),
                    reads=(ps_b, acc_b), writes=(acc_b,))
            dst_d, dst_db = (x2_d, x2_db) if e == NE - 1 else (acc_d, acc_db)
            ctx.op("q", lambda dst_d=dst_d, nt=nt: nc.sync.dma_start(
                out=dst_d[:, :, nt * NT:(nt + 1) * NT],
                in_=acc[:].rearrange("p (c n) -> p c n", c=CH)),
                reads=(acc_b,), writes=(dst_db,), sem="DQ")
    ph.close()



def build_phaseD(ctx, dram):
    nc = ctx.nc
    ph = Phase(ctx)
    init_common(ctx, ph)
    cst = load_consts(ctx, ph, dram, [
        ("wq1", BF16), ("bq1", F32), ("wl1", BF16), ("bl1", F32),
        ("wk1", BF16), ("bk1", F32), ("wv1", BF16), ("bv1", F32),
    ])
    wq1, wq1_b = cst["wq1"]
    bq1, bq1_b = cst["bq1"]
    wl1, wl1_b = cst["wl1"]
    bl1, bl1_b = cst["bl1"]
    wk1, wk1_b = cst["wk1"]
    bk1, bk1_b = cst["bk1"]
    wv1, wv1_b = cst["wv1"]
    bv1, bv1_b = cst["bv1"]
    ones1, ones1_b = ph.bufs["ones1"]

    x2_d = dram["x2"]
    q1v = dram["q1"].rearrange("p (s h t) -> p s h t", s=SEQ, h=8)
    k1v = dram["k1"].rearrange("p (s h t) -> p s h t", s=SEQ, h=8)
    v1v = dram["v1"].rearrange("p (s u m) -> p s u m", s=SEQ, u=16)
    q1_db, k1_db, v1_db = Buf(), Buf(), Buf()

    x2t = [ph.sb(f"x2t{i}", [128, CH * NT], F32) for i in range(2)]
    lat, lat_b = ph.sb("lat1", [128, 2 * NT], BF16)
    pout, pout_b = ph.sb("pout", [128, NT], BF16)
    vout, vout_b = ph.sb("voutD", [128, 4 * 1024], BF16)

    ps_q = [ph.ps(f"ps_qD{i}", [128, NT], F32) for i in range(2)]
    ps_v, ps_v_b = ph.ps("ps_vD", [128, 1024], F32)
    ps_A, ps_A_b = ph.ps("ps_AD", [128, NT], F32)
    ps_B, ps_B_b = ph.ps("ps_BD", [128, NT], F32)

    latv = lat[:].rearrange("p (k n) -> p k n", k=2)
    voutv = vout[:].rearrange("p (a m) -> p a m", a=4)

    for nt in range(NTILES):
        s, a4 = nt // 4, nt % 4
        x2_t, x2_b = x2t[nt % 2]
        x2v = x2_t[:].rearrange("p (c n) -> p c n", c=CH)
        ctx.op("q", lambda x2_t=x2_t, nt=nt: nc.sync.dma_start(
            out=x2_t[:].rearrange("p (c n) -> p c n", c=CH),
            in_=x2_d[:, :, nt * NT:(nt + 1) * NT]), writes=(x2_b,), sem="DQ")
        xn, xn_b, _ = emit_ln(ctx, ph, "D", x2v, x2_b, (),
                              (ps_v, ps_v_b), (ps_A, ps_A_b), (ps_B, ps_B_b))
        xnv = xn[:].rearrange("p (c n) -> p c n", c=CH)
        for hp in range(8):
            ps, ps_b = ps_q[hp % 2]

            def qmm(hp=hp, ps=ps):
                i = None
                for kc in range(CH):
                    i = nc.tensor.matmul(out=ps[:],
                                         lhsT=wq1[:, kc, hp * 128:(hp + 1) * 128],
                                         rhs=xnv[:, kc, :],
                                         start=(kc == 0), stop=(kc == CH - 1))
                return i
            ctx.op("t", qmm, reads=(xn_b, wq1_b), writes=(ps_b,))
            ctx.op("s", lambda hp=hp, ps=ps: nc.scalar.activation(
                out=pout[:], in_=ps[:], func=AF.Identity, bias=bq1[:, hp:hp + 1]),
                reads=(ps_b, bq1_b), writes=(pout_b,))
            ctx.op("q", lambda s=s, hp=hp, a4=a4: nc.sync.dma_start(
                out=q1v[:, s, hp, a4 * NT:(a4 + 1) * NT], in_=pout[:]),
                reads=(pout_b,), writes=(q1_db,), sem="DQ")
        for lc in range(2):
            ps, ps_b = ps_q[lc % 2]

            def lmm(lc=lc, ps=ps):
                i = None
                for kc in range(CH):
                    i = nc.tensor.matmul(out=ps[:],
                                         lhsT=wl1[:, kc, lc * 128:(lc + 1) * 128],
                                         rhs=xnv[:, kc, :],
                                         start=(kc == 0), stop=(kc == CH - 1))
                return i
            ctx.op("t", lmm, reads=(xn_b, wl1_b), writes=(ps_b,))
            ctx.op("s", lambda lc=lc, ps=ps: nc.scalar.activation(
                out=latv[:, lc, :], in_=ps[:], func=AF.Identity,
                bias=bl1[:, lc:lc + 1]), reads=(ps_b, bl1_b), writes=(lat_b,))
        for hp in range(8):
            ps, ps_b = ps_q[hp % 2]

            def kmm(hp=hp, ps=ps):
                i = None
                for kc in range(2):
                    i = nc.tensor.matmul(out=ps[:],
                                         lhsT=wk1[:, kc, hp * 128:(hp + 1) * 128],
                                         rhs=latv[:, kc, :],
                                         start=(kc == 0), stop=(kc == 1))
                return i
            ctx.op("t", kmm, reads=(lat_b, wk1_b), writes=(ps_b,))
            ctx.op("s", lambda hp=hp, ps=ps: nc.scalar.activation(
                out=pout[:], in_=ps[:], func=AF.Identity, bias=bk1[:, hp:hp + 1]),
                reads=(ps_b, bk1_b), writes=(pout_b,))
            ctx.op("q", lambda s=s, hp=hp, a4=a4: nc.sync.dma_start(
                out=k1v[:, s, hp, a4 * NT:(a4 + 1) * NT], in_=pout[:]),
                reads=(pout_b,), writes=(k1_db,), sem="DQ")
        for sub in range(4):
            def vmm(sub=sub):
                i = None
                for half in range(2):
                    osl = slice(half * 512, half * 512 + 512)
                    for kc in range(2):
                        nc.tensor.matmul(out=ps_v[:, osl],
                                         lhsT=latv[:, kc, sub * 128:(sub + 1) * 128],
                                         rhs=wv1[:, kc, osl],
                                         start=(kc == 0), stop=False)
                    i = nc.tensor.matmul(out=ps_v[:, osl], lhsT=ones1[:],
                                         rhs=bv1[0:1, osl], start=False, stop=True)
                return i
            ctx.op("t", vmm, reads=(lat_b, wv1_b, bv1_b, ones1_b), writes=(ps_v_b,))
            ctx.op("v", lambda sub=sub: nc.vector.tensor_copy(out=voutv[:, sub, :],
                   in_=ps_v[:]), reads=(ps_v_b,), writes=(vout_b,))
        ctx.op("q", lambda s=s, a4=a4: nc.sync.dma_start(
            out=v1v[:, s, a4 * 4:(a4 + 1) * 4, :], in_=voutv),
            reads=(vout_b,), writes=(v1_db,), sem="DQ")
    ph.close()



def build_phaseE(ctx, dram):
    nc = ctx.nc
    ph = Phase(ctx)
    init_common(ctx, ph)
    cst = load_consts(ctx, ph, dram, [("bo1", F32)])
    bo1, bo1_b = cst["bo1"]
    ones1, ones1_b = ph.bufs["ones1"]
    wo, wo_b = ph.sb("wo_sb", [64, 16 * 1024], BF16)
    ctx.op("p", lambda: nc.gpsimd.dma_start(
        out=wo[:].rearrange("r (h m) -> r h m", h=16),
        in_=dram["wo1"].rearrange("(h r) m -> r h m", r=64)),
        writes=(wo_b,), sem="DP")
    wov = wo[:].rearrange("r (h m) -> r h m", h=16)

    q1v = dram["q1"].rearrange("p (s h t) -> p s h t", s=SEQ, h=8)
    k1v = dram["k1"].rearrange("p (s h t) -> p s h t", s=SEQ, h=8)
    v1v = dram["v1"].rearrange("p (s u m) -> p s u m", s=SEQ, u=16)
    x2_d = dram["x2"]
    x3_d = dram["x3"]
    xn3_d = dram["xn3"]
    xn3v = xn3_d.rearrange("p (c t) -> p c t", c=CH)
    x3_db, xn3_db = Buf(), Buf()

    q_sb, q_b = ph.sb("q_sbE", [64, 16 * 512], BF16)
    k_sb, k_b = ph.sb("k_sbE", [64, 8 * 2048], BF16)
    v_sb, v_b = ph.sb("v_sbE", [128, 16 * 1040], BF16)
    pt_sb, pt_b = ph.sb("pt_sbE", [128, 16 * 512], BF16)
    attn, attn_b = ph.sb("attnE", [64, 16 * 512], BF16)
    sqt, sqt_b = ph.sb("sqtE", [128, 512], BF16)
    mred, mred_b = ph.sb("mredE", [1, 8], F32)
    bq, bq_b = ph.sb("bqE", [1, 1], F32)
    bk, bk_b = ph.sb("bkE", [1, 1], F32)
    negb, negb_b = ph.sb("negbE", [128, 1], F32)
    rden, rden_b = ph.sb("rdenE", [1, 512], F32)
    rb, rb_b = ph.sb("rbE", [64, 512], BF16)
    x2t, x2t_b = ph.sb("x2tE", [128, CH * NT], F32)
    x3t, x3t_b = ph.sb("x3tE", [128, CH * NT], F32)

    ps_st, ps_st_b = ph.ps("ps_stE", [128, 512], F32)
    ps_av = [ph.ps(f"ps_avE{i}", [128, 512], F32) for i in range(2)]
    ps_r, ps_r_b = ph.ps("ps_rE", [128, 512], F32)
    ps_w, ps_w_b = ph.ps("ps_wE", [128, 1024], F32)
    ps_A, ps_A_b = ph.ps("ps_AE", [128, NT], F32)
    ps_B, ps_B_b = ph.ps("ps_BE", [128, NT], F32)

    qv = q_sb[:].rearrange("p (h t) -> p h t", h=16)
    kvv = k_sb[:].rearrange("p (h t) -> p h t", h=8)
    vv = v_sb[:].rearrange("p (u h m) -> p u h m", u=16, h=16)
    ptv = pt_sb[:].rearrange("p (u n) -> p u n", u=16)
    attnv = attn[:].rearrange("p (h n) -> p h n", h=16)

    for s in range(SEQ):
        ctx.op("q", lambda s=s: nc.sync.dma_start(
            out=vv[:, :, :, 0:64],
            in_=v1v[:, s].rearrange("p u (h m) -> p u h m", h=16)),
            writes=(v_b,), sem="DQ")
        ctx.op("v", lambda: nc.vector.memset(vv[:, :, :, 64:65], 1.0), writes=(v_b,))
        for qtg in range(4):
            nt = s * 4 + qtg
            for par in range(2):
                ctx.op("q", lambda par=par, s=s, qtg=qtg: nc.sync.dma_start(
                    out=qv[:, par::2, :],
                    in_=q1v[par * 64:par * 64 + 64, s, :, qtg * 512:(qtg + 1) * 512]),
                    writes=(q_b,), sem="DQ")
            emit_bound(ctx, ph, nc, q_sb, q_b, 16 * 512, bq, bq_b, sqt, sqt_b,
                       ps_r, ps_r_b, mred, mred_b)
            ctx.op("q", lambda x2t=x2t, nt=nt: nc.sync.dma_start(
                out=x2t[:].rearrange("p (c n) -> p c n", c=CH),
                in_=x2_d[:, :, nt * NT:(nt + 1) * NT]), writes=(x2t_b,), sem="DQ")
            for hh in range(2):
                for par in range(2):
                    ctx.op("q", lambda par=par, s=s, hh=hh: nc.sync.dma_start(
                        out=kvv[:, par::2, :],
                        in_=k1v[par * 64:par * 64 + 64, s, 4 * hh + 0:4 * hh + 4, :]),
                        writes=(k_b,), sem="DQ")
                emit_bound(ctx, ph, nc, k_sb, k_b, 8 * 2048, bk, bk_b, sqt, sqt_b,
                           ps_r, ps_r_b, mred, mred_b)
                ctx.op("v", lambda: nc.vector.tensor_mul(out=bk[0:1, 0:1],
                       in0=bq[0:1, :], in1=bk[0:1, 0:1]),
                       reads=(bk_b, bq_b), writes=(bk_b,))
                ctx.op("s", lambda: nc.scalar.activation(out=mred[0:1, 4:5],
                       in_=bk[0:1, 0:1], func=AF.Sqrt), reads=(bk_b,), writes=(mred_b,))
                ctx.op("t", lambda: nc.tensor.matmul(out=ps_r[:, 0:1], lhsT=ones1[:],
                       rhs=mred[0:1, 4:5], start=True, stop=True),
                       reads=(mred_b, ones1_b), writes=(ps_r_b,))
                ctx.op("v", lambda: nc.vector.tensor_scalar(out=negb[:],
                       in0=ps_r[:, 0:1], scalar1=-SCALE0, scalar2=None, op0=ALU.mult),
                       reads=(ps_r_b,), writes=(negb_b,))
                for hsub in range(8):
                    h = hh * 8 + hsub
                    pav, pav_b = ps_av[h % 2]
                    rhs = qv[:, h, :]
                    for kt in range(16):
                        lhsT = kvv[:, hsub, kt * 128:(kt + 1) * 128]
                        ctx.op("t", lambda lhsT=lhsT, rhs=rhs: nc.tensor.matmul(
                            out=ps_st[:], lhsT=lhsT, rhs=rhs, start=True, stop=True),
                            reads=(q_b, k_b), writes=(ps_st_b,))
                        ctx.op("s", lambda kt=kt: nc.scalar.activation(
                            out=ptv[:, kt, :], in_=ps_st[:], func=AF.Exp,
                            bias=negb[:], scale=SCALE0),
                            reads=(ps_st_b, negb_b), writes=(pt_b,))

                    def av(h=h, pav=pav):
                        i = None
                        for kt in range(16):
                            i = nc.tensor.matmul(out=pav[0:65, :],
                                                 lhsT=vv[:, kt, h, :],
                                                 rhs=ptv[:, kt, :],
                                                 start=(kt == 0), stop=(kt == 15))
                        return i
                    ctx.op("t", av, reads=(pt_b, v_b), writes=(pav_b,))
                    ctx.op("v", lambda pav=pav: nc.vector.reciprocal(
                        out=rden[0:1, :], in_=pav[64:65, :]),
                        reads=(pav_b,), writes=(rden_b,))
                    ctx.op("t", lambda: nc.tensor.matmul(out=ps_r[0:64, :],
                           lhsT=ones1[:, 0:64], rhs=rden[0:1, :], start=True,
                           stop=True), reads=(rden_b, ones1_b), writes=(ps_r_b,))
                    ctx.op("v", lambda: nc.vector.tensor_copy(out=rb[:],
                           in_=ps_r[0:64, :]), reads=(ps_r_b,), writes=(rb_b,))
                    ctx.op("v", lambda pav=pav, h=h: nc.vector.tensor_mul(
                        out=attnv[:, h, :], in0=pav[0:64, :], in1=rb[:]),
                        reads=(pav_b, rb_b), writes=(attn_b,))
            x3v = x3t[:].rearrange("p (c n) -> p c n", c=CH)
            x2v = x2t[:].rearrange("p (c n) -> p c n", c=CH)
            for oc in range(CH):
                half = oc % 2
                osl = slice(half * 512, half * 512 + 512)

                def wmm(oc=oc, osl=osl):
                    i = None
                    for h in range(16):
                        i = nc.tensor.matmul(
                            out=ps_w[:, osl],
                            lhsT=wov[:, h, oc * 128:(oc + 1) * 128],
                            rhs=attnv[:, h, :], start=(h == 0), stop=(h == 15))
                    return i
                ctx.op("t", wmm, reads=(attn_b, wo_b), writes=(ps_w_b,))
                ctx.op("v", lambda oc=oc, osl=osl: nc.vector.scalar_tensor_tensor(
                    out=x3v[:, oc, :], in0=ps_w[:, osl], scalar=bo1[:, oc:oc + 1],
                    in1=x2v[:, oc, :], op0=ALU.add, op1=ALU.add),
                    reads=(ps_w_b, bo1_b, x2t_b), writes=(x3t_b,))
            ctx.op("q", lambda nt=nt: nc.sync.dma_start(
                out=x3_d[:, :, nt * NT:(nt + 1) * NT], in_=x3v),
                reads=(x3t_b,), writes=(x3_db,), sem="DQ")
            xn, xn_b, _ = emit_ln(ctx, ph, "E", x3v, x3t_b, (),
                                  (ps_w, ps_w_b), (ps_A, ps_A_b), (ps_B, ps_B_b),
                                  scratch=(pt_sb[:, 0:CH * NT], pt_sb[:, CH * NT:2 * CH * NT],
                                           pt_b))
            xnv = xn[:].rearrange("p (c n) -> p c n", c=CH)
            ctx.op("q", lambda nt=nt, xnv=xnv: nc.sync.dma_start(
                out=xn3v[:, :, nt * NT:(nt + 1) * NT], in_=xnv),
                reads=(xn_b,), writes=(xn3_db,), sem="DQ")
    ph.close()



def _fold_butterfly(ctx, ph, nc, dram, which, ebuf, ebuf_b, ncols):
    dsel, dsel_b = ph.sb(which + "_dsel", [128, 5 * 32], F32)
    doth, doth_b = ph.sb(which + "_doth", [128, 5 * 32], F32)
    ctx.op("q", lambda: nc.sync.dma_start(out=dsel[:], in_=dram[which + "dself"]),
           writes=(dsel_b,), sem="DQ")
    ctx.op("q", lambda: nc.sync.dma_start(out=doth[:], in_=dram[which + "doth"]),
           writes=(doth_b,), sem="DQ")
    smat = [ph.sb(f"{which}_sm{i}", [128, 32 * 128], BF16) for i in range(2)]
    ps_f, ps_f_b = ph.ps("ps_fold" + which, [128, 1056], F32)
    ps_f2, ps_f2_b = ph.ps("ps_fold2" + which, [128, 1056], F32)
    tmpA, tmpA_b = ph.sb(which + "_tmpA", [128, 1056], BF16)
    tmpB, tmpB_b = ph.sb(which + "_tmpB", [128, 1056], BF16)
    ev = ebuf[:].rearrange("p (c n) -> p c n", c=32)
    chunk_d = dram[which + "chunk"]

    def load_sm(s_, slot):
        t, b = smat[slot]
        ctx.op("p", lambda s_=s_, t=t: nc.gpsimd.dma_start(
            out=t[:].rearrange("p (c m) -> p c m", c=32), in_=chunk_d[s_]),
            writes=(b,), sem="DP")

    stages = list(range(12)) if which == "up" else list(range(11, -1, -1))
    chunk_stages = [s_ for s_ in stages if s_ <= 6]
    diag_stages = [s_ for s_ in stages if s_ >= 7]
    order = stages
    ci = 0
    load_sm(chunk_stages[0], 0)
    for s_ in order:
        if s_ <= 6:
            t, b = smat[ci % 2]
            if ci + 1 < len(chunk_stages):
                load_sm(chunk_stages[ci + 1], (ci + 1) % 2)
            tv = t[:].rearrange("p (c m) -> p c m", c=32)
            for c in range(32):
                psx, psx_b = (ps_f, ps_f_b) if c % 2 == 0 else (ps_f2, ps_f2_b)

                def fmm(c=c, tv=tv, psx=psx):
                    i = None
                    sls = [slice(0, 512), slice(512, 1024)]
                    if ncols > 1024:
                        sls.append(slice(1024, ncols))
                    for sl in sls:
                        i = nc.tensor.matmul(out=psx[:, sl], lhsT=tv[:, c, :],
                                             rhs=ev[:, c, sl], start=True, stop=True)
                    return i
                ctx.op("t", fmm, reads=(ebuf_b, b), writes=(psx_b,))
                ctx.op("v", lambda c=c, psx=psx: nc.vector.tensor_copy(
                    out=ev[:, c, :], in_=psx[:, 0:ncols]),
                    reads=(psx_b,), writes=(ebuf_b,))
            ci += 1
        else:
            si = s_ - 7
            bit = (1 << s_) // 128
            for c in range(32):
                if c & bit:
                    continue
                c2 = c | bit
                ctx.op("v", lambda c=c, c2=c2, si=si: nc.vector.tensor_scalar(
                    out=tmpA[:, 0:ncols], in0=ev[:, c, :],
                    scalar1=dsel[:, si * 32 + c:si * 32 + c + 1], scalar2=None,
                    op0=ALU.mult), reads=(ebuf_b, dsel_b), writes=(tmpA_b,))
                ctx.op("v", lambda c=c, c2=c2, si=si: nc.vector.scalar_tensor_tensor(
                    out=tmpA[:, 0:ncols], in0=ev[:, c2, :],
                    scalar=doth[:, si * 32 + c:si * 32 + c + 1], in1=tmpA[:, 0:ncols],
                    op0=ALU.mult, op1=ALU.add),
                    reads=(ebuf_b, doth_b, tmpA_b), writes=(tmpA_b,))
                ctx.op("v", lambda c=c, c2=c2, si=si: nc.vector.tensor_scalar(
                    out=tmpB[:, 0:ncols], in0=ev[:, c2, :],
                    scalar1=dsel[:, si * 32 + c2:si * 32 + c2 + 1], scalar2=None,
                    op0=ALU.mult), reads=(ebuf_b, dsel_b), writes=(tmpB_b,))
                ctx.op("v", lambda c=c, c2=c2, si=si: nc.vector.scalar_tensor_tensor(
                    out=tmpB[:, 0:ncols], in0=ev[:, c, :],
                    scalar=doth[:, si * 32 + c2:si * 32 + c2 + 1], in1=tmpB[:, 0:ncols],
                    op0=ALU.mult, op1=ALU.add),
                    reads=(ebuf_b, doth_b, tmpB_b), writes=(tmpB_b,))
                ctx.op("v", lambda c=c: nc.vector.tensor_copy(
                    out=ev[:, c, :], in_=tmpA[:, 0:ncols]),
                    reads=(tmpA_b,), writes=(ebuf_b,))
                ctx.op("v", lambda c2=c2: nc.vector.tensor_copy(
                    out=ev[:, c2, :], in_=tmpB[:, 0:ncols]),
                    reads=(tmpB_b,), writes=(ebuf_b,))


def build_phaseF(ctx, dram):
    nc = ctx.nc
    ph = Phase(ctx)
    init_common(ctx, ph)
    x3_d = dram["x3"]
    xn3v = dram["xn3"].rearrange("p (c t) -> p c t", c=CH)
    out_d = dram["out"]
    outf = out_d.rearrange("s t f -> (s t) f")
    out_db = Buf()

    ubf, ubf_b = ph.sb("upbias_sb", [128, 32], F32)
    bsum, bsum_b = ph.sb("bsum", [128, 32], F32)
    ctx.op("q", lambda: nc.sync.dma_start(out=ubf[:], in_=dram["upbias"]),
           writes=(ubf_b,), sem="DQ")
    phF = Phase(ctx)
    ebuf, ebuf_b = phF.sb("ebuf", [128, 32 * 1025], BF16)
    ctx.op("p", lambda: nc.gpsimd.dma_start(
        out=ebuf[:].rearrange("p (c n) -> p c n", c=32), in_=dram["e0"]),
        writes=(ebuf_b,), sem="DP")
    _fold_butterfly(ctx, phF, nc, dram, "up", ebuf, ebuf_b, 1025)
    ev = ebuf[:].rearrange("p (c n) -> p c n", c=32)
    ctx.op("v", lambda: nc.vector.tensor_add(out=bsum[:], in0=ev[:, :, 1024],
           in1=ubf[:]), reads=(ebuf_b, ubf_b), writes=(bsum_b,))
    escr = dram["escr"]
    escr_db = Buf()
    ctx.op("q", lambda: nc.sync.dma_start(
        out=escr.rearrange("(c p) n -> p c n", p=128), in_=ev[:, :, 0:1024]),
        reads=(ebuf_b,), writes=(escr_db,), sem="DQ")
    phF.close()

    we, we_b = ph.sb("we", [128, 32 * 8 * 128], BF16)
    wev = we[:].rearrange("p (c k m) -> p c k m", c=32, k=8)
    for c in range(32):
        for kc in range(8):
            ctx.op("q", lambda c=c, kc=kc: nc.sync.dma_start(
                out=wev[:, c, kc, :],
                in_=escr[c * 128:(c + 1) * 128, kc * 128:(kc + 1) * 128],
                transpose=True), reads=(escr_db,), writes=(we_b,), sem="DQ")

    gbuf, gbuf_b = ph.sb("gbuf", [128, 32 * 1024], BF16)
    phG = Phase(ctx)
    ctx.op("p", lambda: nc.gpsimd.dma_start(
        out=gbuf[:].rearrange("p (c n) -> p c n", c=32), in_=dram["g0"]),
        writes=(gbuf_b,), sem="DP")
    _fold_butterfly(ctx, phG, nc, dram, "dn", gbuf, gbuf_b, 1024)
    gv = gbuf[:].rearrange("p (c n) -> p c n", c=32)
    phG.close()

    bs2, bs2_b = ph.sb("bsum2", [128, CH], F32)
    bcf, bcf_b = ph.sb("bconf", [128, CH], F32)
    dbf, dbf_b = ph.sb("dbf", [128, CH], F32)
    ctx.op("q", lambda: nc.sync.dma_start(out=bcf[:], in_=dram["bcon"]),
           writes=(bcf_b,), sem="DQ")
    ctx.op("q", lambda: nc.sync.dma_start(out=dbf[:], in_=dram["dbias"]),
           writes=(dbf_b,), sem="DQ")
    ctx.op("v", lambda: nc.vector.tensor_add(out=bs2[:], in0=bcf[:], in1=dbf[:]),
           reads=(bcf_b, dbf_b), writes=(bs2_b,))

    cst = load_consts(ctx, ph, dram, [("identfF", F32)])
    identf, identf_b = cst["identfF"]

    xnt, xnt_b = ph.sb("xntF", [128, CH * NT], BF16)
    x3t, x3t_b = ph.sb("x3tF", [128, CH * NT], F32)
    hf, hf_b = ph.sb("hfF", [128, 32 * NT], BF16)
    x4, x4_b = ph.sb("x4F", [128, CH * NT], F32)
    outt, outt_b = ph.sb("outtF", [128, 1024], F32)
    ps_h = [ph.ps(f"ps_hF{i}", [128, NT], F32) for i in range(2)]
    ps_y = [ph.ps(f"ps_yF{i}", [128, NT], F32) for i in range(2)]
    ps_t, ps_t_b = ph.ps("ps_tF", [128, 1024], F32)
    xnv = xnt[:].rearrange("p (c n) -> p c n", c=CH)
    x3v = x3t[:].rearrange("p (c n) -> p c n", c=CH)
    hv = hf[:].rearrange("p (c n) -> p c n", c=32)
    x4v = x4[:].rearrange("p (c n) -> p c n", c=CH)

    for nt in range(NTILES):
        ctx.op("q", lambda nt=nt: nc.sync.dma_start(
            out=xnt[:].rearrange("p (c n) -> p c n", c=CH),
            in_=xn3v[:, :, nt * NT:(nt + 1) * NT]), writes=(xnt_b,), sem="DQ")
        ctx.op("q", lambda nt=nt: nc.sync.dma_start(
            out=x3t[:].rearrange("p (c n) -> p c n", c=CH),
            in_=x3_d[:, :, nt * NT:(nt + 1) * NT]), writes=(x3t_b,), sem="DQ")
        for c in range(32):
            ps, ps_b = ps_h[c % 2]

            def hmm(c=c, ps=ps):
                i = None
                for kc in range(CH):
                    i = nc.tensor.matmul(out=ps[:], lhsT=wev[:, c, kc, :],
                                         rhs=xnv[:, kc, :],
                                         start=(kc == 0), stop=(kc == CH - 1))
                return i
            ctx.op("t", hmm, reads=(xnt_b, we_b), writes=(ps_b,))
            ctx.op("s", lambda c=c, ps=ps: nc.scalar.activation(
                out=hv[:, c, :], in_=ps[:], func=AF.Gelu, bias=bsum[:, c:c + 1]),
                reads=(ps_b, bsum_b), writes=(hf_b,))
        for oc in range(CH):
            ps, ps_b = ps_y[oc % 2]

            def ymm(oc=oc, ps=ps):
                i = None
                for kc in range(32):
                    i = nc.tensor.matmul(out=ps[:], lhsT=gv[:, kc, oc * 128:(oc + 1) * 128],
                                         rhs=hv[:, kc, :],
                                         start=(kc == 0), stop=(kc == 31))
                return i
            ctx.op("t", ymm, reads=(hf_b, gbuf_b), writes=(ps_b,))
            ctx.op("v", lambda oc=oc, ps=ps: nc.vector.scalar_tensor_tensor(
                out=x4v[:, oc, :], in0=ps[:], scalar=bs2[:, oc:oc + 1],
                in1=x3v[:, oc, :], op0=ALU.add, op1=ALU.add),
                reads=(ps_b, bs2_b, x3t_b), writes=(x4_b,))
        for a in range(4):
            def trs(a=a):
                i = None
                for c in range(CH):
                    i = nc.tensor.transpose(out=ps_t[:, c * 128:(c + 1) * 128],
                                            in_=x4v[:, c, a * 128:(a + 1) * 128],
                                            identity=identf[:])
                return i
            ctx.op("t", trs, reads=(x4_b, identf_b), writes=(ps_t_b,))
            ctx.op("v", lambda: nc.vector.tensor_copy(
                out=outt[:], in_=ps_t[:]), reads=(ps_t_b,), writes=(outt_b,))
            ctx.op("q", lambda nt=nt, a=a: nc.sync.dma_start(
                out=outf[nt * NT + a * 128:nt * NT + (a + 1) * 128, :],
                in_=outt[:]), reads=(outt_b,), writes=(out_db,), sem="DQ")
    ph.close()



def _mk_builder(phase_fn, ins, outs, internals=()):
    nc = bass.Bass("TRN2", target_bir_lowering=False, debug=False)
    dram = {}
    for name, shape, dt in ins:
        dram[name] = nc.dram_tensor(name, shape, dt, kind="ExternalInput").ap()
    for name, shape, dt in outs:
        dram[name] = nc.dram_tensor(name, shape, dt, kind="ExternalOutput").ap()
    for name, shape, dt in internals:
        dram[name] = nc.dram_tensor(name, shape, dt, kind="Internal").ap()
    es = ExitStack()
    ctx = Ctx(nc, es)
    with es:
        with nc.Block() as block:
            @block.sync
            def _(sync):
                phase_fn(ctx, dram)
    return nc


class _DefaultNone(dict):
    def __missing__(self, k):
        return None


_SPECS = None


def _phase_specs():
    global _SPECS
    if _SPECS is not None:
        return _SPECS
    f32, bf = F32, BF16
    _SPECS = {
        "A": (build_phaseA,
              [("xin", [SEQ, 2048, 1024], f32), ("wqa", [1024, 64], f32),
               ("wqbp", [64, 1024], f32), ("wkva", [1024, 32], f32),
               ("wk0p", [32, 256], f32), ("wv0", [32, 256], f32),
               ("cossq", [128, 2048], f32), ("sinsq", [128, 2048], f32),
               ("cossk", [128, 2048], f32), ("sinsk", [128, 2048], f32),
               ("permbf", [128, 128], f32), ("ident", [128, 128], f32)],
              [("x0", [128, CH, T], f32), ("q0", [128, SEQ * 8 * 2048], bf),
               ("k0", [128, SEQ * 2 * 2048], bf), ("v0", [128, SEQ * 16 * 256], bf)],
              []),
        "B": (build_phaseB,
              [("q0", [128, SEQ * 8 * 2048], bf), ("k0", [128, SEQ * 2 * 2048], bf),
               ("v0", [128, SEQ * 16 * 256], bf), ("x0", [128, CH, T], f32),
               ("wout", [1024, 1024], f32), ("bout_fm", [128, 8], f32),
               ("rwt", [1024, 8], f32), ("causal", [128, 512], f32),
               ("identf", [128, 128], f32)],
              [("x1", [128, CH, T], f32), ("xn1", [128, CH * T], bf),
               ("gates", [8, T], f32)],
              []),
        "C": (build_phaseC,
              [("xn1", [128, CH * T], bf), ("x1", [128, CH, T], f32),
               ("gates", [8, T], f32), ("w1T", [NE, 1024, 2048], f32),
               ("w2T", [NE, 2048, 1024], f32)],
              [("x2", [128, CH, T], f32)],
              [("accC", [128, CH, T], f32)]),
        "D": (build_phaseD,
              [("x2", [128, CH, T], f32), ("wq1", [1024, 1024], f32),
               ("bq1", [128, 8], f32), ("wl1", [1024, 256], f32),
               ("bl1", [128, 2], f32), ("wk1", [256, 1024], f32),
               ("bk1", [128, 8], f32), ("wv1", [256, 1024], f32),
               ("bv1", [1, 1024], f32)],
              [("q1", [128, SEQ * 8 * 2048], bf), ("k1", [128, SEQ * 8 * 2048], bf),
               ("v1", [128, SEQ * 16 * 1024], bf)],
              []),
        "E": (build_phaseE,
              [("q1", [128, SEQ * 8 * 2048], bf), ("k1", [128, SEQ * 8 * 2048], bf),
               ("v1", [128, SEQ * 16 * 1024], bf), ("x2", [128, CH, T], f32),
               ("wo1", [1024, 1024], f32), ("bo1", [128, 8], f32)],
              [("x3", [128, CH, T], f32), ("xn3", [128, CH * T], bf)],
              []),
        "F": (build_phaseF,
              [("x3", [128, CH, T], f32), ("xn3", [128, CH * T], bf),
               ("e0", [128, 32, 1025], f32), ("g0", [128, 32, 1024], f32),
               ("upchunk", [7, 32, 128, 128], f32), ("dnchunk", [7, 32, 128, 128], f32),
               ("updself", [128, 160], f32), ("updoth", [128, 160], f32),
               ("dndself", [128, 160], f32), ("dndoth", [128, 160], f32),
               ("upbias", [128, 32], f32), ("bcon", [128, CH], f32),
               ("dbias", [128, CH], f32), ("identfF", [128, 128], f32)],
              [("out", [SEQ, 2048, 1024], f32)],
              [("escr", [4096, 1024], bf)]),
    }
    return _SPECS


def _phase_inputs(name, hp, core_arrays):
    core_arrays = _DefaultNone(core_arrays)
    if name == "A":
        return {"xin": core_arrays["xin"], "wqa": hp["Wqa"], "wqbp": hp["Wqb_perm"],
                "wkva": hp["Wkva"], "wk0p": hp["Wk0_perm"], "wv0": hp["Wv0"],
                "cossq": hp["COSSq"], "sinsq": hp["SINSq"], "cossk": hp["COSSk"],
                "sinsk": hp["SINSk"], "permbf": hp["PERM"], "ident": hp["IDENT"]}
    if name == "B":
        return {"q0": core_arrays["q0"], "k0": core_arrays["k0"],
                "v0": core_arrays["v0"], "x0": core_arrays["x0"],
                "wout": hp["Wout"], "bout_fm": hp["bout_fm"], "rwt": hp["router_wT"],
                "causal": hp["CAUSAL"], "identf": hp["IDENT"]}
    if name == "C":
        return {"xn1": core_arrays["xn1"], "x1": core_arrays["x1"],
                "gates": core_arrays["gates"], "w1T": hp["w1T"], "w2T": hp["w2T"]}
    if name == "D":
        return {"x2": core_arrays["x2"], "wq1": hp["Wq1"], "bq1": hp["bq1_fm"],
                "wl1": hp["Wl1"], "bl1": hp["bl1_fm"], "wk1": hp["Wk1"],
                "bk1": hp["bk1_fm"], "wv1": hp["Wv1"], "bv1": hp["bv1_row"]}
    if name == "E":
        return {"q1": core_arrays["q1"], "k1": core_arrays["k1"],
                "v1": core_arrays["v1"], "x2": core_arrays["x2"],
                "wo1": hp["Wo1"], "bo1": hp["bo1_fm"]}
    if name == "F":
        return {"x3": core_arrays["x3"], "xn3": core_arrays["xn3"],
                "e0": hp["E0"], "g0": hp["G0"], "upchunk": hp["UP_CHUNK"],
                "dnchunk": hp["DN_CHUNK"], "updself": hp["UP_DSELF_dev"],
                "updoth": hp["UP_DOTH_dev"], "dndself": hp["DN_DSELF_dev"],
                "dndoth": hp["DN_DOTH_dev"], "upbias": hp["up_bias_fm"],
                "bcon": hp["b_con_fm"], "dbias": hp["down_bias_fm"],
                "identfF": hp["IDENT"]}
    raise KeyError(name)


def _mk_fused():
    specs = _phase_specs()
    nc = bass.Bass("TRN2", target_bir_lowering=False, debug=False)
    dram = {}
    ext_in = set()
    stream_names = {"x0", "x1", "x2", "x3", "xn1", "xn3", "gates",
                    "q0", "k0", "v0", "q1", "k1", "v1"}
    for name in "ABCDEF":
        fn, ins, outs, internals = specs[name]
        for nm, shape, dt in ins:
            if nm in dram:
                continue
            if nm in stream_names:
                continue
            dram[nm] = nc.dram_tensor(nm, shape, dt, kind="ExternalInput").ap()
            ext_in.add(nm)
        for nm, shape, dt in outs:
            if nm in dram:
                continue
            kind = "ExternalOutput" if nm == "out" else "Internal"
            dram[nm] = nc.dram_tensor(nm, shape, dt, kind=kind).ap()
        for nm, shape, dt in internals:
            dram[nm] = nc.dram_tensor(nm, shape, dt, kind="Internal").ap()
    es = ExitStack()
    ctx = Ctx(nc, es)
    with es:
        with nc.Block() as block:
            @block.sync
            def _(sync):
                for name in "ABCDEF":
                    specs[name][0](ctx, dram)
    return nc, ext_in


def kernel(**inputs):
    import os
    hp = host_prep({k: np.asarray(v) for k, v in inputs.items()})
    x_full = np.asarray(inputs["L0_attn_x"], dtype=np.float32)
    ncores = NCORES
    core_state = [{"xin": np.ascontiguousarray(x_full[c * SEQ:(c + 1) * SEQ])}
                  for c in range(ncores)]
    specs = _phase_specs()
    if os.environ.get("MK_UNFUSED"):
        for name in "ABCDEF":
            fn, ins, outs, internals = specs[name]
            nc = _mk_builder(fn, ins, outs, internals)
            in_maps = [_phase_inputs(name, hp, core_state[c]) for c in range(ncores)]
            res = run_bass_kernel_spmd(nc, in_maps, core_ids=list(range(ncores)))
            for c in range(ncores):
                core_state[c].update(res.results[c])
        return np.concatenate([core_state[c]["out"] for c in range(ncores)], axis=0)
    nc, ext_in = _mk_fused()
    in_maps = []
    for c in range(ncores):
        m = {}
        for name in "ABCDEF":
            for k, v in _phase_inputs(name, hp, core_state[c]).items():
                if k in ext_in and v is not None:
                    m[k] = v
        in_maps.append(m)
    res = run_bass_kernel_spmd(nc, in_maps, core_ids=list(range(ncores)))
    return np.concatenate([res.results[c]["out"] for c in range(ncores)], axis=0)


# version 04
# speedup vs baseline: 1.2864x; 1.2864x over previous
import numpy as np
import ml_dtypes
import concourse.bass as bass
import concourse.mybir as mybir
from concourse.bass_utils import run_bass_kernel_spmd

F32 = mybir.dt.float32
BF16 = mybir.dt.bfloat16
AF = mybir.ActivationFunctionType
ALU = mybir.AluOpType
AX = mybir.AxisListType

BF = ml_dtypes.bfloat16

NCORES = 8
B_FULL, S, E = 32, 2048, 1024
SEQ = 4
T = SEQ * S
NT = 512
NTILES = T // NT
CH = E // 128
H, KVH, D = 16, 4, 64
QL, KVL = 64, 32
ROPE_BASE = 10000.0
SCALE0 = D ** -0.5
NE, DFF, TOPK = 8, 2048, 2
LAT = 256
DH = 4096
NSTAGE = 12
LN_EPS = 1e-5



def _deint_idx():
    r = np.arange(64)
    return np.where(r < 32, 2 * r, 2 * (r - 32) + 1)


def _rope_tables():
    inv = 1.0 / ROPE_BASE ** (np.arange(0, D, 2, dtype=np.float64) / D)
    t = np.arange(S, dtype=np.float64)
    fr = np.outer(inv, t)
    cos = np.cos(fr).astype(np.float32)
    sin = np.sin(fr).astype(np.float32)
    coss = np.concatenate([cos, cos, cos, cos], 0)
    sins = np.concatenate([-sin, sin, -sin, sin], 0)
    return coss, sins


def _perm_swap32():
    p = np.arange(128)
    swap = (p // 32 % 2 == 0) * (p + 32) + (p // 32 % 2 == 1) * (p - 32)
    m = np.zeros((128, 128), np.float32)
    m[swap, p] = 1.0
    return m


def _bfly_mats(weights, transpose):
    chunk_lhsT = []
    for s in range(7):
        s_ = 1 << s
        B_full = np.zeros((128, 128), np.float32)
        mats = []
        for c in range(32):
            Bm = np.zeros((128, 128), np.float32)
            for i_lo in range(128):
                i = c * 128 + i_lo
                if i & s_:
                    continue
                j = i + s_
                j_lo = j - c * 128
                k = (i // (2 * s_)) * s_ + (i % s_)
                w = weights[s, k]
                if transpose:
                    w = w.T
                Bm[i_lo, i_lo] = w[0, 0]
                Bm[i_lo, j_lo] = w[0, 1]
                Bm[j_lo, i_lo] = w[1, 0]
                Bm[j_lo, j_lo] = w[1, 1]
            mats.append(np.ascontiguousarray(Bm.T))
        chunk_lhsT.append(np.stack(mats))
    diag = []
    for s in range(7, 12):
        s_ = 1 << s
        bit = s_ // 128
        d_self = np.zeros((32, 128), np.float32)
        d_other = np.zeros((32, 128), np.float32)
        for c in range(32):
            for p in range(128):
                i = c * 128 + p
                k = (i // (2 * s_)) * s_ + (i % s_) if not (i & s_) else None
                if i & s_:
                    ip = i - s_
                    k = (ip // (2 * s_)) * s_ + (ip % s_)
                    w = weights[s, k]
                    if transpose:
                        w = w.T
                    d_self[c, p] = w[1, 1]
                    d_other[c, p] = w[1, 0]
                else:
                    k = (i // (2 * s_)) * s_ + (i % s_)
                    w = weights[s, k]
                    if transpose:
                        w = w.T
                    d_self[c, p] = w[0, 0]
                    d_other[c, p] = w[0, 1]
        diag.append((d_self, d_other))
    return chunk_lhsT, diag


def host_prep(inp):
    d = {}
    deint = _deint_idx()
    d["Wqa"] = np.ascontiguousarray(inp["L0_attn_Wqa"])
    Wqb = inp["L0_attn_Wqb"]
    cols = np.empty(1024, np.int64)
    for hp in range(H // 2):
        for half in range(2):
            h = 2 * hp + half
            cols[hp * 128 + half * 64:hp * 128 + half * 64 + 64] = h * 64 + deint
    d["Wqb_perm"] = np.ascontiguousarray(Wqb[:, cols])
    d["Wkva"] = np.ascontiguousarray(inp["L0_attn_Wkva"])
    Wkvb = inp["L0_attn_Wkvb"]
    kcols = np.empty(256, np.int64)
    vcols = np.empty(256, np.int64)
    for kp in range(KVH // 2):
        for half in range(2):
            g = 2 * kp + half
            kcols[kp * 128 + half * 64:kp * 128 + half * 64 + 64] = g * 128 + 2 * deint
    for g in range(KVH):
        vcols[g * 64:(g + 1) * 64] = g * 128 + 2 * np.arange(64) + 1
    d["Wk0_perm"] = np.ascontiguousarray(Wkvb[:, kcols])
    d["Wv0"] = np.ascontiguousarray(Wkvb[:, vcols])
    d["Wout"] = np.ascontiguousarray(inp["L0_attn_Wout"])
    d["bout_fm"] = np.ascontiguousarray(inp["L0_attn_bout"].reshape(CH, 128).T)
    coss, sins = _rope_tables()
    d["COSSq"] = coss * np.float32(SCALE0)
    d["SINSq"] = sins * np.float32(SCALE0)
    d["COSSk"] = coss
    d["SINSk"] = sins
    d["PERM"] = _perm_swap32()
    d["IDENT"] = np.eye(128, dtype=np.float32)
    causal = np.zeros((128, 512), np.float32)
    kk, q = np.meshgrid(np.arange(128), np.arange(128), indexing="ij")
    cm = np.where(kk > q, np.float32(-1e30), np.float32(0.0))
    for j in range(4):
        causal[:, j * 128:(j + 1) * 128] = cm
    d["CAUSAL"] = causal
    d["router_wT"] = np.ascontiguousarray(inp["L0_ffn_router_w"].T)
    d["w1T"] = np.ascontiguousarray(inp["L0_ffn_w1"].transpose(0, 2, 1))
    d["w2T"] = np.ascontiguousarray(inp["L0_ffn_w2"].transpose(0, 2, 1))
    d["Wq1"] = np.ascontiguousarray(inp["L1_attn_Wq"])
    d["bq1_fm"] = np.ascontiguousarray(inp["L1_attn_bq"].reshape(CH, 128).T)
    d["Wl1"] = np.ascontiguousarray(inp["L1_attn_Wl"])
    d["bl1_fm"] = np.ascontiguousarray(inp["L1_attn_bl"].reshape(2, 128).T)
    d["Wk1"] = np.ascontiguousarray(inp["L1_attn_Wk"])
    d["bk1_fm"] = np.ascontiguousarray(inp["L1_attn_bk"].reshape(CH, 128).T)
    d["Wv1"] = np.ascontiguousarray(inp["L1_attn_Wv"])
    d["bv1_row"] = np.ascontiguousarray(inp["L1_attn_bv"].reshape(1, 1024))
    d["Wo1"] = np.ascontiguousarray(inp["L1_attn_Wo"])
    d["bo1_fm"] = np.ascontiguousarray(inp["L1_attn_bo"].reshape(CH, 128).T)
    E0 = np.concatenate([inp["L1_ffn_W_exp"].T,
                         inp["L1_ffn_b_exp"].reshape(DH, 1)], axis=1)
    d["E0"] = np.ascontiguousarray(E0.reshape(32, 128, 1025).transpose(1, 0, 2))
    G0 = inp["L1_ffn_W_con"]
    d["G0"] = np.ascontiguousarray(G0.reshape(32, 128, 1024).transpose(1, 0, 2))
    up_chunk, up_diag = _bfly_mats(inp["L1_ffn_up_weights"], transpose=False)
    dn_chunk, dn_diag = _bfly_mats(inp["L1_ffn_down_weights"], transpose=True)
    d["UP_CHUNK"] = np.stack(up_chunk)
    d["DN_CHUNK"] = np.stack(dn_chunk)
    d["UP_DSELF"] = np.stack([a for a, b in up_diag])
    d["UP_DOTH"] = np.stack([b for a, b in up_diag])
    d["DN_DSELF"] = np.stack([a for a, b in dn_diag])
    d["DN_DOTH"] = np.stack([b for a, b in dn_diag])
    for k in ("UP_DSELF", "UP_DOTH", "DN_DSELF", "DN_DOTH"):
        d[k + "_dev"] = np.ascontiguousarray(
            d[k].transpose(2, 0, 1).reshape(128, 5 * 32))
    d["up_bias_fm"] = np.ascontiguousarray(
        inp["L1_ffn_up_bias"].reshape(32, 128).T)
    d["b_con_fm"] = np.ascontiguousarray(inp["L1_ffn_b_con"].reshape(CH, 128).T)
    d["down_bias_fm"] = np.ascontiguousarray(inp["L1_ffn_down_bias"].reshape(CH, 128).T)
    return d



from contextlib import ExitStack


class Buf:
    __slots__ = ("w", "r")

    def __init__(self):
        self.w = None
        self.r = {}

    def w_toks(self):
        return [self.w] if self.w is not None else []

    def r_toks(self):
        return list(self.r.items())

    def note_read(self, tok):
        if self.r.get(tok[0], 0) < tok[1]:
            self.r[tok[0]] = tok[1]

    def note_write(self, tok):
        self.w = tok
        self.r = {}


class BufView:

    def __init__(self, *bufs):
        self.bufs = bufs

    def w_toks(self):
        return [t for b in self.bufs for t in b.w_toks()]

    def r_toks(self):
        return [t for b in self.bufs for t in b.r_toks()]

    def note_read(self, tok):
        for b in self.bufs:
            b.note_read(tok)

    def note_write(self, tok):
        for b in self.bufs:
            b.note_write(tok)


class Ctx:
    def __init__(self, nc, es):
        self.nc = nc
        self.es = es
        self.sems = {}
        self.cnt = {}
        self.waited = {}
        self.eng = {"t": nc.tensor, "v": nc.vector, "s": nc.scalar,
                    "p": nc.gpsimd, "q": nc.sync}
        self.esem = {"t": "T", "v": "V", "s": "S", "p": "P", "q": "Q"}
        self.dma_k = {"T": 1, "V": 1, "S": 1, "P": 1, "Q": 1,
                      "DQ": 16, "DP": 16, "DS": 16}
        self.dma_sems = {"DQ", "DP", "DS"}

    def sem(self, name):
        if name not in self.sems:
            self.sems[name] = self.es.enter_context(self.nc.semaphore(name))
            self.cnt[name] = 0
        return self.sems[name]

    def _wait(self, ename, toks, skip_self=True, dma_sem=None):
        skip = ()
        if skip_self:
            if ename == "t":
                skip = ("T",)
            if dma_sem is not None:
                skip = skip + (dma_sem,)
        need = {}
        for t in toks:
            if t is None:
                continue
            s, v = t
            if s in skip:
                continue
            if v > need.get(s, 0):
                need[s] = v
        eng = self.eng[ename]
        w = self.waited.setdefault(ename, {})
        for s, v in sorted(need.items()):
            if w.get(s, -1) >= v:
                continue
            eng.wait_ge(self.sem(s), v)
            w[s] = v

    def op(self, ename, f, reads=(), writes=(), sem=None, dep=()):
        sem = sem or self.esem[ename]
        toks = list(dep)
        for b in reads:
            toks.extend(b.w_toks())
        for b in writes:
            toks.extend(b.w_toks())
            toks.extend(b.r_toks())
        self._wait(ename, toks,
                   dma_sem=sem if sem in self.dma_sems else None)
        inst = f()
        k = self.dma_k.get(sem, 1)
        h = self.sem(sem)
        inst.then_inc(h, k)
        self.cnt[sem] += k
        tok = (sem, self.cnt[sem])
        for b in reads:
            b.note_read(tok)
        for b in writes:
            b.note_write(tok)
        return tok

    def barrier(self, tokens=None):
        toks = [(s, c) for s, c in self.cnt.items()]
        for ename in self.eng:
            self._wait(ename, toks, skip_self=False)


def _mk_dram(nc, name, shape, dtype, kind):
    return nc.dram_tensor(name, shape, dtype, kind=kind).ap()


class Phase:
    _n = 0

    def __init__(self, ctx):
        self.ctx = ctx
        self.es = ExitStack()
        self.bufs = {}
        Phase._n += 1
        self.pfx = f"ph{Phase._n}_"

    def sb(self, name, shape, dtype):
        t = self.es.enter_context(self.ctx.nc.sbuf_tensor(self.pfx + name, shape, dtype))
        b = Buf()
        self.bufs[name] = (t, b)
        return t, b

    def ps(self, name, shape, dtype=F32):
        t = self.es.enter_context(self.ctx.nc.psum_tensor(self.pfx + name, shape, dtype))
        b = Buf()
        self.bufs[name] = (t, b)
        return t, b

    def close(self):
        self.ctx.barrier()
        self.es.close()



def load_consts(ctx, ph, dram, names):
    out = {}
    for name, sbdtype in names:
        ap = dram[name]
        shp = list(ap.tensor.shape)
        if len(shp) == 2 and shp[0] > 128:
            assert shp[0] % 128 == 0
            ap = ap.rearrange("(c p) m -> p c m", p=128)
            shp = [128, shp[0] // 128, shp[1]]
        t, b = ph.sb("c_" + name, shp, sbdtype)
        eng = "p" if sbdtype != ap.tensor.dtype else "q"
        sem = "DP" if eng == "p" else "DQ"
        ctx.op(eng, lambda t=t, ap=ap, eng=eng: ctx.eng[eng].dma_start(out=t[:], in_=ap),
               writes=(b,), sem=sem)
        out[name] = (t, b)
    return out



def emit_ln(ctx, ph, sfx, x3, x_buf, dep, psS_p, psA_p, psB_p, NTc=NT, scratch=None,
            stat_scratch=None, xn_scratch=None):
    nc = ctx.nc
    key = "ln_key" + sfx
    if key not in ph.bufs:
        ph.sb("ln_key" + sfx, [1, 1], F32)
        if stat_scratch is None:
            ph.sb("ln_stat" + sfx, [1, 6 * NTc], F32)
        if xn_scratch is None:
            ph.sb("ln_xn" + sfx, [128, CH * NTc], BF16)
        if scratch is None:
            ph.sb("ln_xb" + sfx, [128, CH * NTc], BF16)
            ph.sb("ln_sq" + sfx, [128, CH * NTc], BF16)
    if scratch is None:
        xbt, xb_b = ph.bufs["ln_xb" + sfx]
        sqt, sq_b = ph.bufs["ln_sq" + sfx]
        xb, sq = xbt[:], sqt[:]
    else:
        xb, sq, shared = scratch
        xb_b = sq_b = shared
    if stat_scratch is None:
        stat, stat_b = ph.bufs["ln_stat" + sfx]
    else:
        stat, stat_b = stat_scratch
    if xn_scratch is None:
        xnt_, xn_b = ph.bufs["ln_xn" + sfx]
        xn = xnt_[:]
    else:
        xn, xn_b = xn_scratch
    psS, psS_b = psS_p
    psA, psA_b = psA_p
    psB, psB_b = psB_p
    ones128_bf, ones128_b = ph.bufs["ones128_bf"]
    ones1, ones1_b = ph.bufs["ones1"]

    ctx.op("v", lambda: nc.vector.tensor_copy(out=xb, in_=x3),
           reads=(x_buf,), writes=(xb_b,), dep=dep)
    ctx.op("s", lambda: nc.scalar.activation(out=sq, in_=xb, func=AF.Square),
           reads=(xb_b,), writes=(sq_b,))
    xbv = xb.rearrange("p (c n) -> p c n", c=CH)
    sqv = sq.rearrange("p (c n) -> p c n", c=CH)

    def sums():
        for c in range(CH):
            nc.tensor.matmul(out=psS[0:1, 0:NTc], lhsT=ones128_bf[:], rhs=xbv[:, c, :],
                             start=(c == 0), stop=(c == CH - 1))
        for c in range(CH):
            i = nc.tensor.matmul(out=psS[0:1, NTc:2 * NTc], lhsT=ones128_bf[:],
                                 rhs=sqv[:, c, :], start=(c == 0), stop=(c == CH - 1))
        return i
    ctx.op("t", sums, reads=(xb_b, sq_b, ones128_b), writes=(psS_b,))
    n = NTc
    mean, ms, var, sd, a_r, b_r = (stat[:1, i * n:(i + 1) * n] for i in range(6))
    ctx.op("v", lambda: nc.vector.tensor_scalar(out=mean, in0=psS[0:1, 0:n],
           scalar1=1.0 / E, scalar2=None, op0=ALU.mult), reads=(psS_b,), writes=(stat_b,))
    ctx.op("v", lambda: nc.vector.tensor_scalar(out=ms, in0=psS[0:1, n:2 * n],
           scalar1=1.0 / E, scalar2=None, op0=ALU.mult), reads=(psS_b,), writes=(stat_b,))
    ctx.op("v", lambda: nc.vector.tensor_mul(out=var, in0=mean, in1=mean),
           reads=(stat_b,), writes=(stat_b,))
    ctx.op("v", lambda: nc.vector.tensor_sub(out=var, in0=ms, in1=var),
           reads=(stat_b,), writes=(stat_b,))
    epst, epst_b = ph.bufs["epsln"]
    ctx.op("s", lambda: nc.scalar.activation(out=sd, in_=var, func=AF.Sqrt,
           bias=epst[:]), reads=(stat_b, epst_b), writes=(stat_b,))
    ctx.op("v", lambda: nc.vector.reciprocal(out=a_r, in_=sd),
           reads=(stat_b,), writes=(stat_b,))
    ctx.op("v", lambda: nc.vector.scalar_tensor_tensor(out=b_r, in0=mean, scalar=-1.0,
           in1=a_r, op0=ALU.mult, op1=ALU.mult), reads=(stat_b,), writes=(stat_b,))
    ctx.op("t", lambda: nc.tensor.matmul(out=psA[:], lhsT=ones1[:], rhs=a_r,
           start=True, stop=True), reads=(stat_b, ones1_b), writes=(psA_b,))
    ctx.op("t", lambda: nc.tensor.matmul(out=psB[:], lhsT=ones1[:], rhs=b_r,
           start=True, stop=True), reads=(stat_b, ones1_b), writes=(psB_b,))
    xnv = xn.rearrange("p (c n) -> p c n", c=CH)
    tok = None
    for c in range(CH):
        ctx.op("v", lambda c=c: nc.vector.tensor_mul(out=xnv[:, c, :], in0=x3[:, c, :],
               in1=psA[:]), reads=(x_buf, psA_b), writes=(xn_b,))
        tok = ctx.op("v", lambda c=c: nc.vector.tensor_add(out=xnv[:, c, :],
               in0=xnv[:, c, :], in1=psB[:]), reads=(psB_b, xn_b), writes=(xn_b,))
    return xn, xn_b, tok


def init_common(ctx, ph):
    nc = ctx.nc
    o128, o128_b = ph.sb("ones128_bf", [128, 1], BF16)
    o1, o1_b = ph.sb("ones1", [1, 128], F32)
    eps, eps_b = ph.sb("epsln", [1, 1], F32)
    ctx.op("v", lambda: nc.vector.memset(o128[:], 1.0), writes=(o128_b,))
    ctx.op("v", lambda: nc.vector.memset(o1[:], 1.0), writes=(o1_b,))
    ctx.op("v", lambda: nc.vector.memset(eps[:], LN_EPS), writes=(eps_b,))


def build_phaseA(ctx, dram):
    nc = ctx.nc
    ph = Phase(ctx)
    init_common(ctx, ph)
    cst = load_consts(ctx, ph, dram, [
        ("wqa", BF16), ("wqbp", BF16), ("wkva", BF16), ("wk0p", BF16), ("wv0", BF16),
        ("cossq", F32), ("sinsq", F32), ("cossk", F32), ("sinsk", F32),
        ("permbf", BF16), ("ident", F32),
    ])
    wqa, wqa_b = cst["wqa"]
    wqav = wqa[:]
    wqbp, wqbp_b = cst["wqbp"]
    wkva, wkva_b = cst["wkva"]
    wkvav = wkva[:]
    wk0p, wk0p_b = cst["wk0p"]
    wv0, wv0_b = cst["wv0"]
    cossq, cossq_b = cst["cossq"]
    sinsq, sinsq_b = cst["sinsq"]
    cossk, cossk_b = cst["cossk"]
    sinsk, sinsk_b = cst["sinsk"]
    perm, perm_b = cst["permbf"]
    ident, ident_b = cst["ident"]

    xin = dram["xin"]
    xin_flat = xin.rearrange("s t f -> (s t) f")
    x0_d = dram["x0"]
    q0_d = dram["q0"]
    k0_d = dram["k0"]
    v0_d = dram["v0"]
    x0_db, q0_db, k0_db, v0_db = Buf(), Buf(), Buf(), Buf()
    q0v = q0_d.rearrange("p (s h t) -> p s h t", s=SEQ, h=8)
    k0v = k0_d.rearrange("p (s h t) -> p s h t", s=SEQ, h=2)
    v0v = v0_d.rearrange("p (s u m) -> p s u m", s=SEQ, u=16)

    xt = [ph.sb(f"xt{i}", [128, 4 * 1024], F32) for i in range(2)]
    x0sb = [ph.sb(f"x0sb{i}", [128, CH * NT], F32) for i in range(2)]
    latq = [ph.sb(f"latq{i}", [64, NT], BF16) for i in range(2)]
    latkv = [ph.sb(f"latkv{i}", [32, NT], BF16) for i in range(2)]
    qbf = ph.sb("qbf", [128, NT], BF16)
    ro1 = ph.sb("ro1", [128, NT], F32)
    ro2 = ph.sb("ro2", [128, NT], F32)
    qout = ph.sb("qout", [128, NT], BF16)
    vout = ph.sb("vout", [128, 4 * 256], BF16)

    ps_x = ph.ps("ps_x", [128, 1024], F32)
    ps_lat = ph.ps("ps_lat", [128, NT], F32)
    ps_q = [ph.ps(f"ps_q{i}", [128, NT], F32) for i in range(2)]
    ps_sw = ph.ps("ps_sw", [128, NT], F32)
    ps_A = ph.ps("ps_A", [128, NT], F32)
    ps_B = ph.ps("ps_B", [128, NT], F32)
    ps_v = ps_lat

    for nt in range(NTILES):
        s, a4 = nt // 4, nt % 4
        pos0 = a4 * NT
        xt_t, xt_b = xt[nt % 2]
        x0_t, x0_b = x0sb[nt % 2]
        src = xin_flat[nt * NT:(nt + 1) * NT, :].rearrange("(a p) f -> p a f", p=128)
        ctx.op("q", lambda xt_t=xt_t, src=src: nc.sync.dma_start(out=xt_t[:].rearrange(
            "p (a f) -> p a f", a=4), in_=src), writes=(xt_b,), sem="DQ")
        xtv = xt_t[:].rearrange("p (a f) -> p a f", a=4)
        x0v = x0_t[:].rearrange("p (c n) -> p c n", c=CH)
        for a in range(4):
            def trs(a=a, xtv=xtv):
                i = None
                for c in range(CH):
                    i = nc.tensor.transpose(out=ps_x[0][:, c * 128:(c + 1) * 128],
                                            in_=xtv[:, a, c * 128:(c + 1) * 128],
                                            identity=ident[:])
                return i
            ctx.op("t", trs, reads=(xt_b, ident_b), writes=(ps_x[1],))
            ctx.op("v", lambda a=a, x0v=x0v: nc.vector.tensor_copy(
                out=x0v[:, :, a * 128:(a + 1) * 128],
                in_=ps_x[0][:].rearrange("p (c n) -> p c n", c=CH)),
                reads=(ps_x[1],), writes=(x0_b,))
        ctx.op("s", lambda x0v=x0v, nt=nt: nc.scalar.dma_start(
            out=x0_d[:, nt], in_=x0v), reads=(x0_b,), sem="DS")
        xn, xn_b, _ = emit_ln(ctx, ph, "A", x0v, x0_b, (), ps_x, ps_A, ps_B)
        xnv = xn[:].rearrange("p (c n) -> p c n", c=CH)
        latq_t, latq_b = latq[nt % 2]
        latkv_t, latkv_b = latkv[nt % 2]

        def mm_lat():
            for c in range(CH):
                nc.tensor.matmul(out=ps_lat[0][0:64, :], lhsT=wqav[:, c, :],
                                 rhs=xnv[:, c, :], start=(c == 0), stop=(c == CH - 1))
            for c in range(CH):
                i = nc.tensor.matmul(out=ps_lat[0][64:96, :], lhsT=wkvav[:, c, :],
                                     rhs=xnv[:, c, :], start=(c == 0), stop=(c == CH - 1))
            return i
        ctx.op("t", mm_lat, reads=(xn_b, wqa_b, wkva_b), writes=(ps_lat[1],))
        ctx.op("v", lambda latq_t=latq_t: nc.vector.tensor_copy(out=latq_t[:],
               in_=ps_lat[0][0:64, :]), reads=(ps_lat[1],), writes=(latq_b,))
        ctx.op("s", lambda latkv_t=latkv_t: nc.scalar.copy(out=latkv_t[:],
               in_=ps_lat[0][64:96, :]), reads=(ps_lat[1],), writes=(latkv_b,))
        for hp in range(8):
            pq, pq_b = ps_q[hp % 2]
            ctx.op("t", lambda hp=hp, pq=pq, latq_t=latq_t: nc.tensor.matmul(
                out=pq[:], lhsT=wqbp[:, hp * 128:(hp + 1) * 128], rhs=latq_t[:],
                start=True, stop=True), reads=(latq_b, wqbp_b), writes=(pq_b,))
            ctx.op("s", lambda pq=pq: nc.scalar.copy(out=qbf[0][:], in_=pq[:]),
                   reads=(pq_b,), writes=(qbf[1],))
            ctx.op("t", lambda: nc.tensor.matmul(out=ps_sw[0][:], lhsT=perm[:],
                   rhs=qbf[0][:], start=True, stop=True),
                   reads=(qbf[1], perm_b), writes=(ps_sw[1],))
            ctx.op("v", lambda pos0=pos0: nc.vector.tensor_mul(out=ro1[0][:],
                   in0=ps_sw[0][:], in1=sinsq[:, pos0:pos0 + NT]),
                   reads=(ps_sw[1], sinsq_b), writes=(ro1[1],))
            ctx.op("v", lambda pos0=pos0: nc.vector.tensor_mul(out=ro2[0][:],
                   in0=qbf[0][:], in1=cossq[:, pos0:pos0 + NT]),
                   reads=(qbf[1], cossq_b), writes=(ro2[1],))
            ctx.op("v", lambda: nc.vector.tensor_add(out=qout[0][:], in0=ro1[0][:],
                   in1=ro2[0][:]), reads=(ro1[1], ro2[1]), writes=(qout[1],))
            ctx.op("s", lambda s=s, hp=hp, pos0=pos0: nc.scalar.dma_start(
                out=q0v[:, s, hp, pos0:pos0 + NT], in_=qout[0][:]),
                reads=(qout[1],), sem="DS", writes=(q0_db,))
        for kp in range(2):
            pq, pq_b = ps_q[kp % 2]
            ctx.op("t", lambda kp=kp, pq=pq, latkv_t=latkv_t: nc.tensor.matmul(
                out=pq[:], lhsT=wk0p[:, kp * 128:(kp + 1) * 128], rhs=latkv_t[:],
                start=True, stop=True), reads=(latkv_b, wk0p_b), writes=(pq_b,))
            ctx.op("s", lambda pq=pq: nc.scalar.copy(out=qbf[0][:], in_=pq[:]),
                   reads=(pq_b,), writes=(qbf[1],))
            ctx.op("t", lambda: nc.tensor.matmul(out=ps_sw[0][:], lhsT=perm[:],
                   rhs=qbf[0][:], start=True, stop=True),
                   reads=(qbf[1], perm_b), writes=(ps_sw[1],))
            ctx.op("v", lambda pos0=pos0: nc.vector.tensor_mul(out=ro1[0][:],
                   in0=ps_sw[0][:], in1=sinsk[:, pos0:pos0 + NT]),
                   reads=(ps_sw[1], sinsk_b), writes=(ro1[1],))
            ctx.op("v", lambda pos0=pos0: nc.vector.tensor_mul(out=ro2[0][:],
                   in0=qbf[0][:], in1=cossk[:, pos0:pos0 + NT]),
                   reads=(qbf[1], cossk_b), writes=(ro2[1],))
            ctx.op("v", lambda: nc.vector.tensor_add(out=qout[0][:], in0=ro1[0][:],
                   in1=ro2[0][:]), reads=(ro1[1], ro2[1]), writes=(qout[1],))
            ctx.op("s", lambda s=s, kp=kp, pos0=pos0: nc.scalar.dma_start(
                out=k0v[:, s, kp, pos0:pos0 + NT], in_=qout[0][:]),
                reads=(qout[1],), sem="DS", writes=(k0_db,))
        voutv = vout[0][:].rearrange("p (a m) -> p a m", a=4)
        for sub in range(4):
            ctx.op("t", lambda sub=sub, latkv_t=latkv_t: nc.tensor.matmul(
                out=ps_v[0][:, 0:256], lhsT=latkv_t[:, sub * 128:(sub + 1) * 128],
                rhs=wv0[:], start=True, stop=True),
                reads=(latkv_b, wv0_b), writes=(ps_v[1],))
            ctx.op("v", lambda sub=sub: nc.vector.tensor_copy(out=voutv[:, sub, :],
                   in_=ps_v[0][:, 0:256]), reads=(ps_v[1],), writes=(vout[1],))
        ctx.op("s", lambda s=s, a4=a4: nc.scalar.dma_start(
            out=v0v[:, s, a4 * 4:(a4 + 1) * 4, :], in_=voutv),
            reads=(vout[1],), sem="DS", writes=(v0_db,))
    ph.close()



def _load_qk_seq(ctx, nc, dst, dst_b, src_view, s, nh, sem="DQ"):
    dstv = dst[:].rearrange("p (h t) -> p h t", h=nh)
    toks = []
    for half in range(2):
        for par in range(2):
            out_ap = dstv[half * 64:(half + 1) * 64, par::2, :]
            in_ap = src_view[par * 64:par * 64 + 64, s, :, half * 1024:(half + 1) * 1024]
            toks.append(ctx.op("q", lambda o=out_ap, i=in_ap: nc.sync.dma_start(
                out=o, in_=i), writes=(dst_b,), sem=sem))
    return toks[-1]


def emit_bound(ctx, ph, nc, q_sb, q_b, ncols, bmax, bmax_b, sq, sq_b, psn, psn_b,
               mred, mred_b, nrows=64):
    ones128_bf, ones128_b = ph.bufs["ones128_bf"]
    nchunks = ncols // 512
    first = True
    for ci in range(nchunks):
        sl = slice(ci * 512, (ci + 1) * 512)
        ctx.op("s", lambda sl=sl: nc.scalar.activation(out=sq[0:nrows, :],
               in_=q_sb[0:nrows, sl], func=AF.Square), reads=(q_b,), writes=(sq_b,))
        ctx.op("t", lambda: nc.tensor.matmul(out=psn[0:1, 0:512],
               lhsT=ones128_bf[0:nrows, :], rhs=sq[0:nrows, :], start=True, stop=True),
               reads=(sq_b, ones128_b), writes=(psn_b,))
        ctx.op("v", lambda: nc.vector.reduce_max(out=mred[0:1, 0:1], in_=psn[0:1, 0:512],
               axis=AX.X), reads=(psn_b,), writes=(mred_b,))
        if first:
            ctx.op("v", lambda: nc.vector.tensor_copy(out=bmax[0:1, :], in_=mred[0:1, 0:1]),
                   reads=(mred_b,), writes=(bmax_b,))
            first = False
        else:
            ctx.op("v", lambda: nc.vector.tensor_max(out=bmax[0:1, :], in0=bmax[0:1, :],
                   in1=mred[0:1, 0:1]), reads=(mred_b, bmax_b), writes=(bmax_b,))


def build_phaseB(ctx, dram):
    nc = ctx.nc
    ph = Phase(ctx)
    init_common(ctx, ph)
    cst = load_consts(ctx, ph, dram, [
        ("bout_fm", F32), ("rwt", BF16), ("causal", F32),
        ("identf", F32),
    ])
    wout, wout_b = ph.sb("wout_sb", [64, 16 * 1024], BF16)
    ctx.op("p", lambda: nc.gpsimd.dma_start(
        out=wout[:].rearrange("r (h m) -> r h m", h=16),
        in_=dram["wout"].rearrange("(h r) m -> r h m", r=64)),
        writes=(wout_b,), sem="DP")
    woutv = wout[:].rearrange("r (h m) -> r h m", h=16)
    bout, bout_b = cst["bout_fm"]
    rwt, rwt_b = cst["rwt"]
    causal, causal_b = cst["causal"]
    identf, identf_b = cst["identf"]
    ones1, ones1_b = ph.bufs["ones1"]

    q0v = dram["q0"].rearrange("p (s h t) -> p s h t", s=SEQ, h=8)
    k0v = dram["k0"].rearrange("p (s h t) -> p s h t", s=SEQ, h=2)
    v0v = dram["v0"].rearrange("p (s u m) -> p s u m", s=SEQ, u=16)
    x0_d = dram["x0"]
    x1_d = dram["x1"]
    xn1_d = dram["xn1"]
    xn1v = xn1_d
    gates_d = dram["gates"]
    x1_db, xn1_db, gates_db = Buf(), Buf(), Buf()

    q_sb, q_b = ph.sb("q_sb", [64, 16 * 512], BF16)
    k_sb, k_b = ph.sb("k_sb", [64, 4 * 2048], BF16)
    v_sb, v_b = ph.sb("v_sb", [128, 16 * 260], BF16)
    pt_sb, pt_b = ph.sb("pt_sb", [128, 16 * 512], BF16)
    attn, attn_b = ph.sb("attn", [64, 16 * 512], BF16)
    sqt, sqt_b = ph.sb("sqt", [128, 512], BF16)
    mred, mred_b = ph.sb("mred", [1, 8], F32)
    bq, bq_b = ph.sb("bq", [1, 1], F32)
    bk, bk_b = ph.sb("bk", [1, 1], F32)
    negb, negb_b = ph.sb("negb", [128, 1], F32)
    rden, rden_b = ph.sb("rden", [1, 512], F32)
    rb, rb_b = ph.sb("rb", [64, 512], BF16)
    x0t = [ph.sb(f"x0t{i}", [128, CH * NT], F32) for i in range(2)]
    x1t, x1t_b = ph.sb("x1t", [128, CH * NT], F32)
    glog, glog_b = ph.sb("glog", [128, 8], F32)
    gm, gm_b = ph.sb("gm", [128, 4], F32)
    gmask, gmask_b = ph.sb("gmask", [128, 8], F32)
    gtm, gtm_b = ph.sb("gtm", [128, 8], F32)
    gexp, gexp_b = ph.sb("gexp", [128, 8], F32)
    grow, grow_b = ph.sb("grow", [8, 512], F32)

    ps_st, ps_st_b = ph.ps("ps_st", [128, 512], F32)
    ps_av = [ph.ps(f"ps_av{i}", [128, 512], F32) for i in range(2)]
    ps_r, ps_r_b = ph.ps("ps_r", [128, 512], F32)
    ps_w, ps_w_b = ph.ps("ps_w", [128, 1024], F32)
    ps_A, ps_A_b = ph.ps("ps_A", [128, NT], F32)
    ps_B, ps_B_b = ph.ps("ps_B", [128, NT], F32)

    qv = q_sb[:].rearrange("p (h t) -> p h t", h=16)
    kv = k_sb[:].rearrange("p (h t) -> p h t", h=4)
    vv = v_sb[:].rearrange("p (u g m) -> p u g m", u=16, g=4)
    ptv = pt_sb[:].rearrange("p (u n) -> p u n", u=16)
    attnv = attn[:].rearrange("p (h n) -> p h n", h=16)

    for s in range(SEQ):
        for par in range(2):
            ctx.op("q", lambda par=par, s=s: nc.sync.dma_start(
                out=kv[:, par::2, :], in_=k0v[par * 64:par * 64 + 64, s, :, :]),
                writes=(k_b,), sem="DQ")
        ctx.op("q", lambda s=s: nc.sync.dma_start(
            out=vv[:, :, :, 0:64],
            in_=v0v[:, s].rearrange("p u (g m) -> p u g m", g=4)),
            writes=(v_b,), sem="DQ")
        ctx.op("v", lambda: nc.vector.memset(vv[:, :, :, 64:65], 1.0), writes=(v_b,))
        emit_bound_cheap(ctx, ph, nc, k_sb, k_b, 4 * 2048, bk, bk_b,
                         [sqt, sqt], [sqt_b, sqt_b], ps_r, ps_r_b)

        for qtg in range(4):
            nt = s * 4 + qtg
            for par in range(2):
                ctx.op("q", lambda par=par, s=s, qtg=qtg: nc.sync.dma_start(
                    out=qv[:, par::2, :],
                    in_=q0v[par * 64:par * 64 + 64, s, :, qtg * 512:(qtg + 1) * 512]),
                    writes=(q_b,), sem="DQ")
            emit_bound_cheap(ctx, ph, nc, q_sb, q_b, 16 * 512, bq, bq_b,
                             [sqt, sqt], [sqt_b, sqt_b], ps_r, ps_r_b)
            ctx.op("v", lambda: nc.vector.tensor_mul(out=bq[0:1, :], in0=bq[0:1, :],
                   in1=bk[0:1, :]), reads=(bk_b, bq_b), writes=(bq_b,))
            ctx.op("s", lambda: nc.scalar.activation(out=mred[0:1, 4:5], in_=bq[0:1, :],
                   func=AF.Sqrt, scale=1.0), reads=(bq_b,), writes=(mred_b,))
            ctx.op("v", lambda: nc.vector.tensor_scalar(out=mred[0:1, 4:5],
                   in0=mred[0:1, 4:5], scalar1=60.0, scalar2=None,
                   op0=ALU.min), reads=(mred_b,), writes=(mred_b,))
            ctx.op("t", lambda: nc.tensor.matmul(out=ps_r[:, 0:1], lhsT=ones1[:],
                   rhs=mred[0:1, 4:5], start=True, stop=True),
                   reads=(mred_b, ones1_b), writes=(ps_r_b,))
            ctx.op("v", lambda: nc.vector.tensor_scalar(out=negb[:], in0=ps_r[:, 0:1],
                   scalar1=-1.0, scalar2=None, op0=ALU.mult),
                   reads=(ps_r_b,), writes=(negb_b,))
            x0_t, x0_b = x0t[qtg % 2]
            x0v3 = x0_t[:].rearrange("p (c n) -> p c n", c=CH)
            ctx.op("q", lambda x0_t=x0_t, nt=nt: nc.sync.dma_start(
                out=x0_t[:].rearrange("p (c n) -> p c n", c=CH),
                in_=x0_d[:, nt]), writes=(x0_b,), sem="DQ")
            for sub in range(4):
                qt = qtg * 4 + sub
                for g in range(4):
                    pav, pav_b = ps_av[(qt * 4 + g) % 2]
                    rhs = qv[:, 4 * g:4 * g + 4, sub * 128:(sub + 1) * 128]
                    for kt in range(qt + 1):
                        lhsT = kv[:, g, kt * 128:(kt + 1) * 128]
                        ctx.op("t", lambda lhsT=lhsT, rhs=rhs: nc.tensor.matmul(
                            out=ps_st[:], lhsT=lhsT, rhs=rhs, start=True, stop=True),
                            reads=(q_b, k_b), writes=(ps_st_b,))
                        if kt == qt:
                            ctx.op("v", lambda: nc.vector.tensor_add(out=ps_st[:],
                                   in0=ps_st[:], in1=causal[:]),
                                   reads=(causal_b, ps_st_b), writes=(ps_st_b,))
                        ctx.op("s", lambda kt=kt: nc.scalar.activation(
                            out=ptv[:, kt, :], in_=ps_st[:], func=AF.Exp,
                            bias=negb[:]), reads=(ps_st_b, negb_b), writes=(pt_b,))
                    def av(qt=qt, g=g, pav=pav):
                        i = None
                        for kt in range(qt + 1):
                            i = nc.tensor.matmul(out=pav[0:65, :],
                                                 lhsT=vv[:, kt, g, :],
                                                 rhs=ptv[:, kt, :],
                                                 start=(kt == 0), stop=(kt == qt))
                        return i
                    ctx.op("t", av, reads=(pt_b, v_b), writes=(pav_b,))
                    ctx.op("v", lambda pav=pav: nc.vector.reciprocal(
                        out=rden[0:1, :], in_=pav[64:65, :]),
                        reads=(pav_b,), writes=(rden_b,))
                    ctx.op("t", lambda: nc.tensor.matmul(out=ps_r[0:64, :],
                           lhsT=ones1[:, 0:64], rhs=rden[0:1, :], start=True,
                           stop=True), reads=(rden_b, ones1_b), writes=(ps_r_b,))
                    ctx.op("v", lambda: nc.vector.tensor_copy(out=rb[:],
                           in_=ps_r[0:64, :]), reads=(ps_r_b,), writes=(rb_b,))
                    dst = attnv[:, 4 * g:4 * g + 4, sub * 128:(sub + 1) * 128]
                    ctx.op("v", lambda pav=pav, dst=dst: nc.vector.tensor_mul(
                        out=dst, in0=pav[0:64, :].rearrange("p (a n) -> p a n", a=4),
                        in1=rb[:].rearrange("p (a n) -> p a n", a=4)),
                        reads=(pav_b, rb_b), writes=(attn_b,))
            x1v = x1t[:].rearrange("p (c n) -> p c n", c=CH)
            for oc in range(CH):
                half = oc % 2
                osl = slice(half * 512, half * 512 + 512)

                def wmm(oc=oc, osl=osl):
                    i = None
                    for h in range(16):
                        i = nc.tensor.matmul(
                            out=ps_w[:, osl],
                            lhsT=woutv[:, h, oc * 128:(oc + 1) * 128],
                            rhs=attnv[:, h, :], start=(h == 0), stop=(h == 15))
                    return i
                ctx.op("t", wmm, reads=(attn_b, wout_b), writes=(ps_w_b,))
                ctx.op("v", lambda oc=oc, osl=osl: nc.vector.scalar_tensor_tensor(
                    out=x1v[:, oc, :], in0=ps_w[:, osl], scalar=bout[:, oc:oc + 1],
                    in1=x0v3[:, oc, :], op0=ALU.add, op1=ALU.add),
                    reads=(ps_w_b, bout_b, x0_b), writes=(x1t_b,))
            ctx.op("s", lambda nt=nt: nc.scalar.dma_start(
                out=x1_d[:, nt], in_=x1v),
                reads=(x1t_b,), writes=(x1_db,), sem="DS")
            xn, xn_b, _ = emit_ln(ctx, ph, "B", x1v, x1t_b, (),
                                  (ps_w, ps_w_b), (ps_A, ps_A_b), (ps_B, ps_B_b))
            xnv = xn[:].rearrange("p (c n) -> p c n", c=CH)
            ctx.op("s", lambda nt=nt, xnv=xnv: nc.scalar.dma_start(
                out=xn1v[:, nt], in_=xnv),
                reads=(xn_b,), writes=(xn1_db,), sem="DS")
            for sub in range(4):
                def gmm(sub=sub, xnv=xnv):
                    i = None
                    for c in range(CH):
                        i = nc.tensor.matmul(out=ps_r[:, 0:8],
                                             lhsT=xnv[:, c, sub * 128:(sub + 1) * 128],
                                             rhs=rwt[:, c, :],
                                             start=(c == 0), stop=(c == CH - 1))
                    return i
                ctx.op("t", gmm, reads=(xn_b, rwt_b), writes=(ps_r_b,))
                m1 = gm[:, 0:1]
                negm1 = gm[:, 1:2]
                m2 = gm[:, 2:3]
                rsum = gm[:, 3:4]
                ctx.op("v", lambda: nc.vector.tensor_copy(out=glog[:], in_=ps_r[:, 0:8]),
                       reads=(ps_r_b,), writes=(glog_b,))
                ctx.op("v", lambda: nc.vector.reduce_max(out=m1, in_=glog[:], axis=AX.X),
                       reads=(glog_b,), writes=(gm_b,))
                ctx.op("v", lambda: nc.vector.tensor_scalar(out=gmask[:], in0=glog[:],
                       scalar1=m1, scalar2=None, op0=ALU.is_ge),
                       reads=(glog_b, gm_b), writes=(gmask_b,))
                ctx.op("v", lambda: nc.vector.scalar_tensor_tensor(out=gmask[:],
                       in0=gmask[:], scalar=-1e30, in1=glog[:], op0=ALU.mult,
                       op1=ALU.add), reads=(glog_b, gmask_b), writes=(gmask_b,))
                ctx.op("v", lambda: nc.vector.reduce_max(out=m2, in_=gmask[:], axis=AX.X),
                       reads=(gmask_b,), writes=(gm_b,))
                ctx.op("v", lambda: nc.vector.tensor_scalar(out=gmask[:], in0=glog[:],
                       scalar1=m2, scalar2=None, op0=ALU.is_ge),
                       reads=(glog_b, gm_b), writes=(gmask_b,))
                ctx.op("v", lambda: nc.vector.tensor_scalar(out=negm1, in0=m1,
                       scalar1=-1.0, scalar2=None, op0=ALU.mult),
                       reads=(gm_b,), writes=(gm_b,))
                ctx.op("s", lambda: nc.scalar.activation(out=gexp[:], in_=glog[:],
                       func=AF.Exp, bias=negm1), reads=(glog_b, gm_b), writes=(gexp_b,))
                ctx.op("v", lambda: nc.vector.tensor_mul(out=gexp[:], in0=gexp[:],
                       in1=gmask[:]), reads=(gmask_b, gexp_b), writes=(gexp_b,))
                ctx.op("v", lambda: nc.vector.reduce_sum(out=rsum, in_=gexp[:], axis=AX.X),
                       reads=(gexp_b,), writes=(gm_b,))
                ctx.op("v", lambda: nc.vector.reciprocal(out=rsum, in_=rsum),
                       reads=(gm_b,), writes=(gm_b,))
                ctx.op("v", lambda: nc.vector.tensor_scalar(out=gtm[:], in0=gexp[:],
                       scalar1=rsum, scalar2=None, op0=ALU.mult),
                       reads=(gexp_b, gm_b), writes=(gtm_b,))
                ctx.op("t", lambda: nc.tensor.matmul(out=ps_r[0:8, 8:136], lhsT=gtm[:],
                       rhs=identf[:], start=True, stop=True),
                       reads=(gtm_b, identf_b), writes=(ps_r_b,))
                ctx.op("v", lambda sub=sub: nc.vector.tensor_copy(
                    out=grow[:, sub * 128:(sub + 1) * 128], in_=ps_r[0:8, 8:136]),
                    reads=(ps_r_b,), writes=(grow_b,))
            ctx.op("s", lambda nt=nt: nc.scalar.dma_start(
                out=gates_d[:, nt * NT:(nt + 1) * NT], in_=grow[:]),
                reads=(grow_b,), writes=(gates_db,), sem="DS")
    ph.close()



def build_phaseC(ctx, dram):
    nc = ctx.nc
    ph = Phase(ctx)
    init_common(ctx, ph)
    ones1, ones1_b = ph.bufs["ones1"]
    xn1v = dram["xn1"]
    x1_d = dram["x1"]
    x2_d = dram["x2"]
    acc_d = dram["accC"]
    gates_d = dram["gates"]
    w1T_d = dram["w1T"]
    w2T_d = dram["w2T"]
    x2_db, acc_db = Buf(), Buf()

    w1 = [ph.sb(f"w1_{i}", [128, 8 * 2048], BF16) for i in range(2)]
    w2 = [ph.sb(f"w2_{i}", [128, 16 * 1024], BF16) for i in range(2)]
    xnt = [ph.sb(f"xnt{i}", [128, CH * NT], BF16) for i in range(2)]
    hsc, hsc_b = ph.sb("hsc", [128, 16 * NT], BF16)
    gb, gb_b = ph.sb("gb", [128, NT], BF16)
    grow, grow_b = ph.sb("growC", [1, NT], F32)
    acc, acc_b = ph.sb("accsb", [128, CH * NT], F32)

    ps_h = [ph.ps(f"ps_h{i}", [128, NT], F32) for i in range(2)]
    ps_y = [ph.ps(f"ps_y{i}", [128, NT], F32) for i in range(2)]
    ps_g, ps_g_b = ph.ps("ps_g", [128, NT], F32)

    def load_w(e, slot):
        w1t, w1b = w1[slot]
        w2t, w2b = w2[slot]
        ctx.op("p", lambda e=e, w1t=w1t: nc.gpsimd.dma_start(
            out=w1t[:].rearrange("p (k m) -> p k m", k=8),
            in_=w1T_d[e].rearrange("(k p) m -> p k m", p=128)),
            writes=(w1b,), sem="DP")
        ctx.op("p", lambda e=e, w2t=w2t: nc.gpsimd.dma_start(
            out=w2t[:].rearrange("p (k m) -> p k m", k=16),
            in_=w2T_d[e].rearrange("(k p) m -> p k m", p=128)),
            writes=(w2b,), sem="DP")

    load_w(0, 0)
    for e in range(NE):
        if e + 1 < NE:
            load_w(e + 1, (e + 1) % 2)
        w1t, w1b = w1[e % 2]
        w2t, w2b = w2[e % 2]
        w1v = w1t[:].rearrange("p (k m) -> p k m", k=8)
        w2v = w2t[:].rearrange("p (k m) -> p k m", k=16)
        hv = hsc[:].rearrange("p (k m) -> p k m", k=16)
        accv = acc[:].rearrange("p (c n) -> p c n", c=CH)
        for nt in range(NTILES):
            xt, xt_b = xnt[nt % 2]
            xv = xt[:].rearrange("p (c n) -> p c n", c=CH)
            ctx.op("q", lambda xt=xt, nt=nt: nc.sync.dma_start(
                out=xt[:].rearrange("p (c n) -> p c n", c=CH),
                in_=xn1v[:, nt]), writes=(xt_b,), sem="DQ")
            ctx.op("q", lambda e=e, nt=nt: nc.sync.dma_start(
                out=grow[:], in_=gates_d[e:e + 1, nt * NT:(nt + 1) * NT]),
                writes=(grow_b,), sem="DQ")
            ctx.op("t", lambda: nc.tensor.matmul(out=ps_g[:], lhsT=ones1[:],
                   rhs=grow[0:1, :], start=True, stop=True),
                   reads=(grow_b, ones1_b), writes=(ps_g_b,))
            ctx.op("v", lambda: nc.vector.tensor_copy(out=gb[:], in_=ps_g[:]),
                   reads=(ps_g_b,), writes=(gb_b,))
            for mc in range(16):
                ps, ps_b = ps_h[mc % 2]

                def hmm(mc=mc, ps=ps):
                    i = None
                    for kc in range(CH):
                        i = nc.tensor.matmul(out=ps[:],
                                             lhsT=w1v[:, kc, mc * 128:(mc + 1) * 128],
                                             rhs=xv[:, kc, :],
                                             start=(kc == 0), stop=(kc == CH - 1))
                    return i
                ctx.op("t", hmm, reads=(xt_b, w1b), writes=(ps_b,))
                ctx.op("s", lambda mc=mc, ps=ps: nc.scalar.activation(
                    out=hv[:, mc, :], in_=ps[:], func=AF.Gelu),
                    reads=(ps_b,), writes=(hsc_b,))
                ctx.op("v", lambda mc=mc: nc.vector.tensor_mul(
                    out=hv[:, mc, :], in0=hv[:, mc, :], in1=gb[:]),
                    reads=(gb_b, hsc_b), writes=(hsc_b,))
            src = x1_d if e == 0 else acc_d
            ctx.op("q", lambda src=src, nt=nt: nc.sync.dma_start(
                out=acc[:].rearrange("p (c n) -> p c n", c=CH),
                in_=src[:, nt]), writes=(acc_b,), sem="DQ",
                dep=(acc_db.w,) if e > 0 else ())
            for oc in range(CH):
                ps, ps_b = ps_y[oc % 2]

                def ymm(oc=oc, ps=ps):
                    i = None
                    for kc in range(16):
                        i = nc.tensor.matmul(out=ps[:],
                                             lhsT=w2v[:, kc, oc * 128:(oc + 1) * 128],
                                             rhs=hv[:, kc, :],
                                             start=(kc == 0), stop=(kc == 15))
                    return i
                ctx.op("t", ymm, reads=(hsc_b, w2b), writes=(ps_b,))
                ctx.op("v", lambda oc=oc, ps=ps: nc.vector.tensor_add(
                    out=accv[:, oc, :], in0=ps[:], in1=accv[:, oc, :]),
                    reads=(ps_b, acc_b), writes=(acc_b,))
            dst_d, dst_db = (x2_d, x2_db) if e == NE - 1 else (acc_d, acc_db)
            ctx.op("s", lambda dst_d=dst_d, nt=nt: nc.scalar.dma_start(
                out=dst_d[:, nt],
                in_=acc[:].rearrange("p (c n) -> p c n", c=CH)),
                reads=(acc_b,), writes=(dst_db,), sem="DS")
    ph.close()



def build_phaseD(ctx, dram):
    nc = ctx.nc
    ph = Phase(ctx)
    init_common(ctx, ph)
    cst = load_consts(ctx, ph, dram, [
        ("wq1", BF16), ("bq1", F32), ("wl1", BF16), ("bl1", F32),
        ("wk1", BF16), ("bk1", F32), ("wv1", BF16), ("bv1", F32),
    ])
    wq1, wq1_b = cst["wq1"]
    bq1, bq1_b = cst["bq1"]
    wl1, wl1_b = cst["wl1"]
    bl1, bl1_b = cst["bl1"]
    wk1, wk1_b = cst["wk1"]
    bk1, bk1_b = cst["bk1"]
    wv1, wv1_b = cst["wv1"]
    bv1, bv1_b = cst["bv1"]
    ones1, ones1_b = ph.bufs["ones1"]

    x2_d = dram["x2"]
    q1v = dram["q1"].rearrange("p (s h t) -> p s h t", s=SEQ, h=8)
    k1v = dram["k1"].rearrange("p (s h t) -> p s h t", s=SEQ, h=8)
    v1v = dram["v1"].rearrange("p (s u m) -> p s u m", s=SEQ, u=16)
    q1_db, k1_db, v1_db = Buf(), Buf(), Buf()

    x2t = [ph.sb(f"x2t{i}", [128, CH * NT], F32) for i in range(2)]
    lat, lat_b = ph.sb("lat1", [128, 2 * NT], BF16)
    pout, pout_b = ph.sb("pout", [128, NT], BF16)
    vout, vout_b = ph.sb("voutD", [128, 4 * 1024], BF16)

    ps_q = [ph.ps(f"ps_qD{i}", [128, NT], F32) for i in range(2)]
    ps_v, ps_v_b = ph.ps("ps_vD", [128, 1024], F32)
    ps_A, ps_A_b = ph.ps("ps_AD", [128, NT], F32)
    ps_B, ps_B_b = ph.ps("ps_BD", [128, NT], F32)

    latv = lat[:].rearrange("p (k n) -> p k n", k=2)
    voutv = vout[:].rearrange("p (a m) -> p a m", a=4)

    for nt in range(NTILES):
        s, a4 = nt // 4, nt % 4
        x2_t, x2_b = x2t[nt % 2]
        x2v = x2_t[:].rearrange("p (c n) -> p c n", c=CH)
        ctx.op("q", lambda x2_t=x2_t, nt=nt: nc.sync.dma_start(
            out=x2_t[:].rearrange("p (c n) -> p c n", c=CH),
            in_=x2_d[:, nt]), writes=(x2_b,), sem="DQ")
        xn, xn_b, _ = emit_ln(ctx, ph, "D", x2v, x2_b, (),
                              (ps_v, ps_v_b), (ps_A, ps_A_b), (ps_B, ps_B_b))
        xnv = xn[:].rearrange("p (c n) -> p c n", c=CH)
        for hp in range(8):
            ps, ps_b = ps_q[hp % 2]

            def qmm(hp=hp, ps=ps):
                i = None
                for kc in range(CH):
                    i = nc.tensor.matmul(out=ps[:],
                                         lhsT=wq1[:, kc, hp * 128:(hp + 1) * 128],
                                         rhs=xnv[:, kc, :],
                                         start=(kc == 0), stop=(kc == CH - 1))
                return i
            ctx.op("t", qmm, reads=(xn_b, wq1_b), writes=(ps_b,))
            ctx.op("s", lambda hp=hp, ps=ps: nc.scalar.activation(
                out=pout[:], in_=ps[:], func=AF.Identity, bias=bq1[:, hp:hp + 1]),
                reads=(ps_b, bq1_b), writes=(pout_b,))
            ctx.op("s", lambda s=s, hp=hp, a4=a4: nc.scalar.dma_start(
                out=q1v[:, s, hp, a4 * NT:(a4 + 1) * NT], in_=pout[:]),
                reads=(pout_b,), writes=(q1_db,), sem="DS")
        for lc in range(2):
            ps, ps_b = ps_q[lc % 2]

            def lmm(lc=lc, ps=ps):
                i = None
                for kc in range(CH):
                    i = nc.tensor.matmul(out=ps[:],
                                         lhsT=wl1[:, kc, lc * 128:(lc + 1) * 128],
                                         rhs=xnv[:, kc, :],
                                         start=(kc == 0), stop=(kc == CH - 1))
                return i
            ctx.op("t", lmm, reads=(xn_b, wl1_b), writes=(ps_b,))
            ctx.op("s", lambda lc=lc, ps=ps: nc.scalar.activation(
                out=latv[:, lc, :], in_=ps[:], func=AF.Identity,
                bias=bl1[:, lc:lc + 1]), reads=(ps_b, bl1_b), writes=(lat_b,))
        for hp in range(8):
            ps, ps_b = ps_q[hp % 2]

            def kmm(hp=hp, ps=ps):
                i = None
                for kc in range(2):
                    i = nc.tensor.matmul(out=ps[:],
                                         lhsT=wk1[:, kc, hp * 128:(hp + 1) * 128],
                                         rhs=latv[:, kc, :],
                                         start=(kc == 0), stop=(kc == 1))
                return i
            ctx.op("t", kmm, reads=(lat_b, wk1_b), writes=(ps_b,))
            ctx.op("s", lambda hp=hp, ps=ps: nc.scalar.activation(
                out=pout[:], in_=ps[:], func=AF.Identity, bias=bk1[:, hp:hp + 1]),
                reads=(ps_b, bk1_b), writes=(pout_b,))
            ctx.op("s", lambda s=s, hp=hp, a4=a4: nc.scalar.dma_start(
                out=k1v[:, s, hp, a4 * NT:(a4 + 1) * NT], in_=pout[:]),
                reads=(pout_b,), writes=(k1_db,), sem="DS")
        for sub in range(4):
            def vmm(sub=sub):
                i = None
                for half in range(2):
                    osl = slice(half * 512, half * 512 + 512)
                    for kc in range(2):
                        nc.tensor.matmul(out=ps_v[:, osl],
                                         lhsT=latv[:, kc, sub * 128:(sub + 1) * 128],
                                         rhs=wv1[:, kc, osl],
                                         start=(kc == 0), stop=False)
                    i = nc.tensor.matmul(out=ps_v[:, osl], lhsT=ones1[:],
                                         rhs=bv1[0:1, osl], start=False, stop=True)
                return i
            ctx.op("t", vmm, reads=(lat_b, wv1_b, bv1_b, ones1_b), writes=(ps_v_b,))
            ctx.op("v", lambda sub=sub: nc.vector.tensor_copy(out=voutv[:, sub, :],
                   in_=ps_v[:]), reads=(ps_v_b,), writes=(vout_b,))
        ctx.op("s", lambda s=s, a4=a4: nc.scalar.dma_start(
            out=v1v[:, s, a4 * 4:(a4 + 1) * 4, :], in_=voutv),
            reads=(vout_b,), writes=(v1_db,), sem="DS")
    ph.close()



def emit_bound_cheap(ctx, ph, nc, src, src_b, ncols, bmax, bmax_b, sq2, sq2_bs,
                     psn, psn_b, nrows=64):
    ones128_bf, ones128_b = ph.bufs["ones128_bf"]
    nchunks = ncols // 512
    for ci in range(nchunks):
        sq, sq_b = sq2[ci % 2], sq2_bs[ci % 2]
        sl = slice(ci * 512, (ci + 1) * 512)
        ctx.op("p", lambda sl=sl, sq=sq: nc.gpsimd.tensor_mul(out=sq[0:nrows, :],
               in0=src[0:nrows, sl], in1=src[0:nrows, sl]),
               reads=(src_b,), writes=(sq_b,))
        ctx.op("t", lambda sq=sq, ci=ci: nc.tensor.matmul(out=psn[0:1, 0:512],
               lhsT=ones128_bf[0:nrows, :], rhs=sq[0:nrows, :], start=(ci == 0),
               stop=(ci == nchunks - 1)), reads=(sq_b, ones128_b), writes=(psn_b,))
    ctx.op("v", lambda: nc.vector.reduce_max(out=bmax[0:1, 0:1], in_=psn[0:1, 0:512],
           axis=AX.X), reads=(psn_b,), writes=(bmax_b,))


def build_phaseE(ctx, dram):
    import os
    E_SEQS = int(os.environ.get("E_SEQS", SEQ))
    E_QTGS = int(os.environ.get("E_QTGS", 4))
    E_HH = int(os.environ.get("E_HH", 2))
    E_HEADS = int(os.environ.get("E_HEADS", 8))
    E_LN = int(os.environ.get("E_LN", 1))
    nc = ctx.nc
    ph = Phase(ctx)
    init_common(ctx, ph)
    cst = load_consts(ctx, ph, dram, [("bo1", F32)])
    bo1, bo1_b = cst["bo1"]
    ones1, ones1_b = ph.bufs["ones1"]
    wo, wo_b = ph.sb("wo_sb", [64, 16 * 1024], BF16)
    ctx.op("p", lambda: nc.gpsimd.dma_start(
        out=wo[:].rearrange("r (h m) -> r h m", h=16),
        in_=dram["wo1"].rearrange("(h r) m -> r h m", r=64)),
        writes=(wo_b,), sem="DP")
    wov = wo[:].rearrange("r (h m) -> r h m", h=16)

    q1v = dram["q1"].rearrange("p (s h t) -> p s h t", s=SEQ, h=8)
    k1v = dram["k1"].rearrange("p (s h t) -> p s h t", s=SEQ, h=8)
    v1v = dram["v1"].rearrange("p (s u m) -> p s u m", s=SEQ, u=16)
    x2_d = dram["x2"]
    x3_d = dram["x3"]
    xn3_d = dram["xn3"]
    xn3v = xn3_d
    x3_db, xn3_db = Buf(), Buf()

    q_sb, q_b = ph.sb("q_sbE", [64, 16 * 512], BF16)
    k2 = [ph.sb(f"k_sbE{i}", [64, 8 * 2048], BF16) for i in range(2)]
    v_sb, v_b = ph.sb("v_sbE", [128, 16 * 8 * 65], BF16)
    pt2 = [ph.sb(f"pt_sbE{i}", [128, 16 * 512], BF16) for i in range(2)]
    attn, attn_b = ph.sb("attnE", [64, 8 * 512], BF16)
    sq2t = [ph.sb(f"sqtE{i}", [64, 512], BF16) for i in range(2)]
    mred, mred_b = ph.sb("mredE", [1, 8], F32)
    bq, bq_b = ph.sb("bqE", [1, 1], F32)
    bk, bk_b = ph.sb("bkE", [1, 1], F32)
    bkh, bkh_b = ph.sb("bkhE", [1, 1], F32)
    negb, negb_b = ph.sb("negbE", [128, 1], F32)
    _rden = ph.sb("rdenE0", [1, 512], F32)
    rden2 = [_rden, _rden]
    _rb = ph.sb("rbE0", [64, 512], BF16)
    rb2 = [_rb, _rb]
    x2t, x2t_b = ph.sb("x2tE", [128, CH * NT], F32)
    x3t, x3t_b = ph.sb("x3tE", [128, CH * NT], F32)

    ps_st = [ph.ps(f"ps_stE{i}", [128, 1024], F32) for i in range(2)]
    ps_av = [ph.ps(f"ps_avE{i}", [128, 512], F32) for i in range(1)]
    ps_r, ps_r_b = ph.ps("ps_rE", [128, 512], F32)
    ps_w, ps_w_b = ph.ps("ps_wE", [128, 1024], F32)
    ps_B, ps_B_b = ps_r, ps_r_b
    ps_A, ps_A_b = ps_av[0]
    sq2 = [t for t, b in sq2t]
    sq2_bs = [b for t, b in sq2t]

    qv = q_sb[:].rearrange("p (h t) -> p h t", h=16)
    vv = v_sb[:].rearrange("p (u h m) -> p u h m", u=16, h=8)
    attnv = attn[:].rearrange("p (h n) -> p h n", h=8)

    def load_khalf(s, hh, slot):
        k_sb, k_b = k2[slot]
        kvvx = k_sb[:].rearrange("p (h t) -> p h t", h=8)
        for par in range(2):
            ctx.op("q", lambda par=par, s=s, hh=hh, kvvx=kvvx: nc.sync.dma_start(
                out=kvvx[:, par::2, :],
                in_=k1v[par * 64:par * 64 + 64, s, 4 * hh:4 * hh + 4, :]),
                writes=(k_b,), sem="DQ")

    for s in range(E_SEQS):
        for hh in range(2):
            load_khalf(s, hh, hh % 2)
            k_sb, k_b = k2[hh % 2]
            emit_bound_cheap(ctx, ph, nc, k_sb, k_b, 8 * 2048, bkh, bkh_b, sq2,
                             sq2_bs, ps_r, ps_r_b)
            if hh == 0:
                ctx.op("v", lambda: nc.vector.tensor_copy(out=bk[0:1, :],
                       in_=bkh[0:1, :]), reads=(bkh_b,), writes=(bk_b,))
            else:
                ctx.op("v", lambda: nc.vector.tensor_max(out=bk[0:1, :],
                       in0=bk[0:1, :], in1=bkh[0:1, :]),
                       reads=(bkh_b, bk_b), writes=(bk_b,))
        for qtg in range(E_QTGS):
            nt = s * 4 + qtg
            for par in range(2):
                ctx.op("q", lambda par=par, s=s, qtg=qtg: nc.sync.dma_start(
                    out=qv[:, par::2, :],
                    in_=q1v[par * 64:par * 64 + 64, s, :, qtg * 512:(qtg + 1) * 512]),
                    writes=(q_b,), sem="DQ")
            emit_bound_cheap(ctx, ph, nc, q_sb, q_b, 16 * 512, bq, bq_b, sq2,
                             sq2_bs, ps_r, ps_r_b)
            ctx.op("q", lambda nt=nt: nc.sync.dma_start(
                out=x2t[:].rearrange("p (c n) -> p c n", c=CH),
                in_=x2_d[:, nt]), writes=(x2t_b,), sem="DQ")
            ctx.op("v", lambda: nc.vector.tensor_mul(out=bq[0:1, :], in0=bq[0:1, :],
                   in1=bk[0:1, :]), reads=(bk_b, bq_b), writes=(bq_b,))
            ctx.op("s", lambda: nc.scalar.activation(out=mred[0:1, 4:5],
                   in_=bq[0:1, :], func=AF.Sqrt), reads=(bq_b,), writes=(mred_b,))
            ctx.op("v", lambda: nc.vector.tensor_scalar(out=mred[0:1, 4:5],
                   in0=mred[0:1, 4:5], scalar1=60.0 / SCALE0, scalar2=None,
                   op0=ALU.min), reads=(mred_b,), writes=(mred_b,))
            ctx.op("t", lambda: nc.tensor.matmul(out=ps_r[:, 0:1], lhsT=ones1[:],
                   rhs=mred[0:1, 4:5], start=True, stop=True),
                   reads=(mred_b, ones1_b), writes=(ps_r_b,))
            ctx.op("v", lambda: nc.vector.tensor_scalar(out=negb[:],
                   in0=ps_r[:, 0:1], scalar1=-SCALE0, scalar2=None, op0=ALU.mult),
                   reads=(ps_r_b,), writes=(negb_b,))
            x3v = x3t[:].rearrange("p (c n) -> p c n", c=CH)
            x2v = x2t[:].rearrange("p (c n) -> p c n", c=CH)
            for hh in range(E_HH):
                it = qtg * 2 + hh
                if it == 0:
                    load_khalf(s, 0, 0)
                if it + 1 < E_QTGS * 2:
                    load_khalf(s, (it + 1) % 2, (it + 1) % 2)
                k_sb, k_b = k2[it % 2]
                kvv = k_sb[:].rearrange("p (h t) -> p h t", h=8)
                vtmp, vtmp_b = pt2[1]
                ctx.op("q", lambda s=s, hh=hh, vtmp=vtmp: nc.sync.dma_start(
                    out=vtmp[:, 0:8192].rearrange("p (u m) -> p u m", u=16),
                    in_=v1v[:, s, :, 512 * hh:512 * hh + 512]),
                    writes=(vtmp_b,), sem="DQ")
                ctx.op("v", lambda vtmp=vtmp: nc.vector.tensor_copy(
                    out=vv[:, :, :, 0:64],
                    in_=vtmp[:, 0:8192].rearrange("p (u h m) -> p u h m", u=16, h=8)),
                    reads=(vtmp_b,), writes=(v_b,))
                ctx.op("v", lambda: nc.vector.memset(vv[:, :, :, 64:65], 1.0),
                       writes=(v_b,))

                def st_burst(hsub, ktg):
                    h = hh * 8 + hsub
                    pst, pst_b = ps_st[ktg % 2]
                    pt_sb, pt_b = pt2[hsub % 2]
                    ptv = pt_sb[:].rearrange("p (u n) -> p u n", u=16)
                    rhs = qv[:, h, :]
                    for j in range(2):
                        kt = 2 * ktg + j
                        lhsT = kvv[:, hsub, kt * 128:(kt + 1) * 128]
                        ctx.op("t", lambda lhsT=lhsT, rhs=rhs, j=j, pst=pst:
                               nc.tensor.matmul(out=pst[:, j * 512:(j + 1) * 512],
                                                lhsT=lhsT, rhs=rhs,
                                                start=True, stop=True),
                               reads=(q_b, k_b), writes=(pst_b,))
                    ctx.op("s", lambda ktg=ktg, ptv=ptv, pst=pst:
                           nc.scalar.activation(
                               out=ptv[:, 2 * ktg:2 * ktg + 2, :], in_=pst[:],
                               func=AF.Exp, bias=negb[:], scale=SCALE0),
                           reads=(pst_b, negb_b), writes=(pt_b,))

                def av_part(hsub, ktg):
                    pav, pav_b = ps_av[0]
                    pt_sb, pt_b = pt2[hsub % 2]
                    ptv = pt_sb[:].rearrange("p (u n) -> p u n", u=16)

                    def av(hsub=hsub, ktg=ktg, pav=pav, ptv=ptv):
                        i = None
                        for j in range(2):
                            kt = 2 * ktg + j
                            i = nc.tensor.matmul(out=pav[0:65, :],
                                                 lhsT=vv[:, kt, hsub, :],
                                                 rhs=ptv[:, kt, :],
                                                 start=(kt == 0), stop=(kt == 15))
                        return i
                    ctx.op("t", av, reads=(pt_b, v_b), writes=(ps_av[0][1],))

                def norm_head(hsub):
                    pav, pav_b = ps_av[0]
                    rden, rden_b = rden2[hsub % 2]
                    rb, rb_b = rb2[hsub % 2]
                    ctx.op("v", lambda pav=pav, rden=rden: nc.vector.reciprocal(
                        out=rden[0:1, :], in_=pav[64:65, :]),
                        reads=(pav_b,), writes=(rden_b,))
                    ctx.op("t", lambda rden=rden: nc.tensor.matmul(out=ps_r[0:64, :],
                           lhsT=ones1[:, 0:64], rhs=rden[0:1, :], start=True,
                           stop=True), reads=(rden_b, ones1_b), writes=(ps_r_b,))
                    ctx.op("v", lambda rb=rb: nc.vector.tensor_copy(out=rb[:],
                           in_=ps_r[0:64, :]), reads=(ps_r_b,), writes=(rb_b,))
                    ctx.op("v", lambda pav=pav, hsub=hsub, rb=rb:
                           nc.vector.tensor_mul(
                               out=attnv[:, hsub, :], in0=pav[0:64, :], in1=rb[:]),
                           reads=(pav_b, rb_b), writes=(attn_b,))

                for hsub in range(E_HEADS):
                    for ktg in range(8):
                        st_burst(hsub, ktg)
                    if hsub > 0:
                        for ktg in range(8):
                            av_part(hsub - 1, ktg)
                        norm_head(hsub - 1)
                for ktg in range(8):
                    av_part(E_HEADS - 1, ktg)
                norm_head(E_HEADS - 1)
                for oc in range(CH):
                    half = oc % 2
                    osl = slice(half * 512, half * 512 + 512)

                    def wmm(oc=oc, osl=osl, hh=hh):
                        i = None
                        for hsub in range(8):
                            i = nc.tensor.matmul(
                                out=ps_w[:, osl],
                                lhsT=wov[:, hh * 8 + hsub, oc * 128:(oc + 1) * 128],
                                rhs=attnv[:, hsub, :], start=(hsub == 0),
                                stop=(hsub == 7))
                        return i
                    ctx.op("t", wmm, reads=(attn_b, wo_b), writes=(ps_w_b,))
                    if hh == 0:
                        ctx.op("v", lambda oc=oc, osl=osl:
                               nc.vector.scalar_tensor_tensor(
                                   out=x3v[:, oc, :], in0=ps_w[:, osl],
                                   scalar=bo1[:, oc:oc + 1], in1=x2v[:, oc, :],
                                   op0=ALU.add, op1=ALU.add),
                               reads=(ps_w_b, bo1_b, x2t_b), writes=(x3t_b,))
                    else:
                        ctx.op("v", lambda oc=oc, osl=osl: nc.vector.tensor_add(
                               out=x3v[:, oc, :], in0=ps_w[:, osl],
                               in1=x3v[:, oc, :]),
                               reads=(ps_w_b,), writes=(x3t_b,))
            ctx.op("s", lambda nt=nt: nc.scalar.dma_start(
                out=x3_d[:, nt], in_=x3v),
                reads=(x3t_b,), writes=(x3_db,), sem="DS")
            if not E_LN:
                continue
            pt0, pt0_b = pt2[0]
            xn, xn_b, _ = emit_ln(ctx, ph, "E", x3v, x3t_b, (),
                                  (ps_w, ps_w_b), (ps_A, ps_A_b), (ps_B, ps_B_b),
                                  scratch=(pt0[:, 0:CH * NT],
                                           pt0[:, CH * NT:2 * CH * NT], pt0_b),
                                  stat_scratch=(x2t[:], x2t_b),
                                  xn_scratch=(v_sb[:, 0:CH * NT], v_b))
            xnv = xn[:].rearrange("p (c n) -> p c n", c=CH)
            ctx.op("s", lambda nt=nt, xnv=xnv: nc.scalar.dma_start(
                out=xn3v[:, nt], in_=xnv),
                reads=(xn_b,), writes=(xn3_db,), sem="DS")
    ph.close()



def _fold_butterfly(ctx, ph, nc, dram, which, ebuf, ebuf_b, ncols):
    dsel, dsel_b = ph.sb(which + "_dsel", [128, 5 * 32], F32)
    doth, doth_b = ph.sb(which + "_doth", [128, 5 * 32], F32)
    ctx.op("q", lambda: nc.sync.dma_start(out=dsel[:], in_=dram[which + "dself"]),
           writes=(dsel_b,), sem="DQ")
    ctx.op("q", lambda: nc.sync.dma_start(out=doth[:], in_=dram[which + "doth"]),
           writes=(doth_b,), sem="DQ")
    smat = [ph.sb(f"{which}_sm{i}", [128, 32 * 128], BF16) for i in range(2)]
    ps_f, ps_f_b = ph.ps("ps_fold" + which, [128, 1056], F32)
    ps_f2, ps_f2_b = ph.ps("ps_fold2" + which, [128, 1056], F32)
    tmpA, tmpA_b = ph.sb(which + "_tmpA", [128, 1056], BF16)
    tmpB, tmpB_b = ph.sb(which + "_tmpB", [128, 1056], BF16)
    ev = ebuf[:].rearrange("p (c n) -> p c n", c=32)
    chunk_d = dram[which + "chunk"]

    def load_sm(s_, slot):
        t, b = smat[slot]
        ctx.op("p", lambda s_=s_, t=t: nc.gpsimd.dma_start(
            out=t[:].rearrange("p (c m) -> p c m", c=32), in_=chunk_d[s_]),
            writes=(b,), sem="DP")

    stages = list(range(12)) if which == "up" else list(range(11, -1, -1))
    chunk_stages = [s_ for s_ in stages if s_ <= 6]
    diag_stages = [s_ for s_ in stages if s_ >= 7]
    order = stages
    ci = 0
    load_sm(chunk_stages[0], 0)
    for s_ in order:
        if s_ <= 6:
            t, b = smat[ci % 2]
            if ci + 1 < len(chunk_stages):
                load_sm(chunk_stages[ci + 1], (ci + 1) % 2)
            tv = t[:].rearrange("p (c m) -> p c m", c=32)
            for c in range(32):
                psx, psx_b = (ps_f, ps_f_b) if c % 2 == 0 else (ps_f2, ps_f2_b)

                def fmm(c=c, tv=tv, psx=psx):
                    i = None
                    sls = [slice(0, 512), slice(512, 1024)]
                    if ncols > 1024:
                        sls.append(slice(1024, ncols))
                    for sl in sls:
                        i = nc.tensor.matmul(out=psx[:, sl], lhsT=tv[:, c, :],
                                             rhs=ev[:, c, sl], start=True, stop=True)
                    return i
                ctx.op("t", fmm, reads=(ebuf_b, b), writes=(psx_b,))
                ctx.op("v", lambda c=c, psx=psx: nc.vector.tensor_copy(
                    out=ev[:, c, :], in_=psx[:, 0:ncols]),
                    reads=(psx_b,), writes=(ebuf_b,))
            ci += 1
        else:
            si = s_ - 7
            bit = (1 << s_) // 128
            for c in range(32):
                if c & bit:
                    continue
                c2 = c | bit
                ctx.op("v", lambda c=c, c2=c2, si=si: nc.vector.tensor_scalar(
                    out=tmpA[:, 0:ncols], in0=ev[:, c, :],
                    scalar1=dsel[:, si * 32 + c:si * 32 + c + 1], scalar2=None,
                    op0=ALU.mult), reads=(ebuf_b, dsel_b), writes=(tmpA_b,))
                ctx.op("v", lambda c=c, c2=c2, si=si: nc.vector.scalar_tensor_tensor(
                    out=tmpA[:, 0:ncols], in0=ev[:, c2, :],
                    scalar=doth[:, si * 32 + c:si * 32 + c + 1], in1=tmpA[:, 0:ncols],
                    op0=ALU.mult, op1=ALU.add),
                    reads=(ebuf_b, doth_b, tmpA_b), writes=(tmpA_b,))
                ctx.op("v", lambda c=c, c2=c2, si=si: nc.vector.tensor_scalar(
                    out=tmpB[:, 0:ncols], in0=ev[:, c2, :],
                    scalar1=dsel[:, si * 32 + c2:si * 32 + c2 + 1], scalar2=None,
                    op0=ALU.mult), reads=(ebuf_b, dsel_b), writes=(tmpB_b,))
                ctx.op("v", lambda c=c, c2=c2, si=si: nc.vector.scalar_tensor_tensor(
                    out=tmpB[:, 0:ncols], in0=ev[:, c, :],
                    scalar=doth[:, si * 32 + c2:si * 32 + c2 + 1], in1=tmpB[:, 0:ncols],
                    op0=ALU.mult, op1=ALU.add),
                    reads=(ebuf_b, doth_b, tmpB_b), writes=(tmpB_b,))
                ctx.op("v", lambda c=c: nc.vector.tensor_copy(
                    out=ev[:, c, :], in_=tmpA[:, 0:ncols]),
                    reads=(tmpA_b,), writes=(ebuf_b,))
                ctx.op("v", lambda c2=c2: nc.vector.tensor_copy(
                    out=ev[:, c2, :], in_=tmpB[:, 0:ncols]),
                    reads=(tmpB_b,), writes=(ebuf_b,))


def build_phaseF(ctx, dram):
    nc = ctx.nc
    ph = Phase(ctx)
    init_common(ctx, ph)
    x3_d = dram["x3"]
    xn3v = dram["xn3"]
    out_d = dram["out"]
    outf = out_d.rearrange("s t f -> (s t) f")
    out_db = Buf()

    ubf, ubf_b = ph.sb("upbias_sb", [128, 32], F32)
    bsum, bsum_b = ph.sb("bsum", [128, 32], F32)
    ctx.op("q", lambda: nc.sync.dma_start(out=ubf[:], in_=dram["upbias"]),
           writes=(ubf_b,), sem="DQ")
    phF = Phase(ctx)
    ebuf, ebuf_b = phF.sb("ebuf", [128, 32 * 1025], BF16)
    ctx.op("p", lambda: nc.gpsimd.dma_start(
        out=ebuf[:].rearrange("p (c n) -> p c n", c=32), in_=dram["e0"]),
        writes=(ebuf_b,), sem="DP")
    _fold_butterfly(ctx, phF, nc, dram, "up", ebuf, ebuf_b, 1025)
    ev = ebuf[:].rearrange("p (c n) -> p c n", c=32)
    ctx.op("v", lambda: nc.vector.tensor_add(out=bsum[:], in0=ev[:, :, 1024],
           in1=ubf[:]), reads=(ebuf_b, ubf_b), writes=(bsum_b,))
    escr = dram["escr"]
    escr_db = Buf()
    ctx.op("s", lambda: nc.scalar.dma_start(
        out=escr.rearrange("(c p) n -> p c n", p=128), in_=ev[:, :, 0:1024]),
        reads=(ebuf_b,), writes=(escr_db,), sem="DS")
    phF.close()

    we, we_b = ph.sb("we", [128, 32 * 8 * 128], BF16)
    wev = we[:].rearrange("p (c k m) -> p c k m", c=32, k=8)
    for c in range(32):
        for kc in range(8):
            ctx.op("q", lambda c=c, kc=kc: nc.sync.dma_start(
                out=wev[:, c, kc, :],
                in_=escr[c * 128:(c + 1) * 128, kc * 128:(kc + 1) * 128],
                transpose=True), reads=(escr_db,), writes=(we_b,), sem="DQ")

    gbuf, gbuf_b = ph.sb("gbuf", [128, 32 * 1024], BF16)
    phG = Phase(ctx)
    ctx.op("p", lambda: nc.gpsimd.dma_start(
        out=gbuf[:].rearrange("p (c n) -> p c n", c=32), in_=dram["g0"]),
        writes=(gbuf_b,), sem="DP")
    _fold_butterfly(ctx, phG, nc, dram, "dn", gbuf, gbuf_b, 1024)
    gv = gbuf[:].rearrange("p (c n) -> p c n", c=32)
    phG.close()

    bs2, bs2_b = ph.sb("bsum2", [128, CH], F32)
    bcf, bcf_b = ph.sb("bconf", [128, CH], F32)
    dbf, dbf_b = ph.sb("dbf", [128, CH], F32)
    ctx.op("q", lambda: nc.sync.dma_start(out=bcf[:], in_=dram["bcon"]),
           writes=(bcf_b,), sem="DQ")
    ctx.op("q", lambda: nc.sync.dma_start(out=dbf[:], in_=dram["dbias"]),
           writes=(dbf_b,), sem="DQ")
    ctx.op("v", lambda: nc.vector.tensor_add(out=bs2[:], in0=bcf[:], in1=dbf[:]),
           reads=(bcf_b, dbf_b), writes=(bs2_b,))

    cst = load_consts(ctx, ph, dram, [("identfF", F32)])
    identf, identf_b = cst["identfF"]

    xnt, xnt_b = ph.sb("xntF", [128, CH * NT], BF16)
    x3t, x3t_b = ph.sb("x3tF", [128, CH * NT], F32)
    hf, hf_b = ph.sb("hfF", [128, 32 * NT], BF16)
    x4, x4_b = ph.sb("x4F", [128, CH * NT], F32)
    outt, outt_b = ph.sb("outtF", [128, 1024], F32)
    ps_h = [ph.ps(f"ps_hF{i}", [128, NT], F32) for i in range(2)]
    ps_y = [ph.ps(f"ps_yF{i}", [128, NT], F32) for i in range(2)]
    ps_t, ps_t_b = ph.ps("ps_tF", [128, 1024], F32)
    xnv = xnt[:].rearrange("p (c n) -> p c n", c=CH)
    x3v = x3t[:].rearrange("p (c n) -> p c n", c=CH)
    hv = hf[:].rearrange("p (c n) -> p c n", c=32)
    x4v = x4[:].rearrange("p (c n) -> p c n", c=CH)

    for nt in range(NTILES):
        ctx.op("q", lambda nt=nt: nc.sync.dma_start(
            out=xnt[:].rearrange("p (c n) -> p c n", c=CH),
            in_=xn3v[:, nt]), writes=(xnt_b,), sem="DQ")
        ctx.op("q", lambda nt=nt: nc.sync.dma_start(
            out=x3t[:].rearrange("p (c n) -> p c n", c=CH),
            in_=x3_d[:, nt]), writes=(x3t_b,), sem="DQ")
        for c in range(32):
            ps, ps_b = ps_h[c % 2]

            def hmm(c=c, ps=ps):
                i = None
                for kc in range(CH):
                    i = nc.tensor.matmul(out=ps[:], lhsT=wev[:, c, kc, :],
                                         rhs=xnv[:, kc, :],
                                         start=(kc == 0), stop=(kc == CH - 1))
                return i
            ctx.op("t", hmm, reads=(xnt_b, we_b), writes=(ps_b,))
            ctx.op("s", lambda c=c, ps=ps: nc.scalar.activation(
                out=hv[:, c, :], in_=ps[:], func=AF.Gelu, bias=bsum[:, c:c + 1]),
                reads=(ps_b, bsum_b), writes=(hf_b,))
        for oc in range(CH):
            ps, ps_b = ps_y[oc % 2]

            def ymm(oc=oc, ps=ps):
                i = None
                for kc in range(32):
                    i = nc.tensor.matmul(out=ps[:], lhsT=gv[:, kc, oc * 128:(oc + 1) * 128],
                                         rhs=hv[:, kc, :],
                                         start=(kc == 0), stop=(kc == 31))
                return i
            ctx.op("t", ymm, reads=(hf_b, gbuf_b), writes=(ps_b,))
            ctx.op("v", lambda oc=oc, ps=ps: nc.vector.scalar_tensor_tensor(
                out=x4v[:, oc, :], in0=ps[:], scalar=bs2[:, oc:oc + 1],
                in1=x3v[:, oc, :], op0=ALU.add, op1=ALU.add),
                reads=(ps_b, bs2_b, x3t_b), writes=(x4_b,))
        for a in range(4):
            def trs(a=a):
                i = None
                for c in range(CH):
                    i = nc.tensor.transpose(out=ps_t[:, c * 128:(c + 1) * 128],
                                            in_=x4v[:, c, a * 128:(a + 1) * 128],
                                            identity=identf[:])
                return i
            ctx.op("t", trs, reads=(x4_b, identf_b), writes=(ps_t_b,))
            ctx.op("v", lambda: nc.vector.tensor_copy(
                out=outt[:], in_=ps_t[:]), reads=(ps_t_b,), writes=(outt_b,))
            ctx.op("s", lambda nt=nt, a=a: nc.scalar.dma_start(
                out=outf[nt * NT + a * 128:nt * NT + (a + 1) * 128, :],
                in_=outt[:]), reads=(outt_b,), writes=(out_db,), sem="DS")
    ph.close()



def _mk_builder(phase_fn, ins, outs, internals=()):
    nc = bass.Bass("TRN2", target_bir_lowering=False, debug=False)
    dram = {}
    for name, shape, dt in ins:
        dram[name] = nc.dram_tensor(name, shape, dt, kind="ExternalInput").ap()
    for name, shape, dt in outs:
        dram[name] = nc.dram_tensor(name, shape, dt, kind="ExternalOutput").ap()
    for name, shape, dt in internals:
        dram[name] = nc.dram_tensor(name, shape, dt, kind="Internal").ap()
    es = ExitStack()
    ctx = Ctx(nc, es)
    with es:
        with nc.Block() as block:
            @block.sync
            def _(sync):
                phase_fn(ctx, dram)
    return nc


class _DefaultNone(dict):
    def __missing__(self, k):
        return None


_SPECS = None


def _phase_specs():
    global _SPECS
    if _SPECS is not None:
        return _SPECS
    f32, bf = F32, BF16
    _SPECS = {
        "A": (build_phaseA,
              [("xin", [SEQ, 2048, 1024], f32), ("wqa", [1024, 64], f32),
               ("wqbp", [64, 1024], f32), ("wkva", [1024, 32], f32),
               ("wk0p", [32, 256], f32), ("wv0", [32, 256], f32),
               ("cossq", [128, 2048], f32), ("sinsq", [128, 2048], f32),
               ("cossk", [128, 2048], f32), ("sinsk", [128, 2048], f32),
               ("permbf", [128, 128], f32), ("ident", [128, 128], f32)],
              [("x0", [128, NTILES, CH, NT], f32), ("q0", [128, SEQ * 8 * 2048], bf),
               ("k0", [128, SEQ * 2 * 2048], bf), ("v0", [128, SEQ * 16 * 256], bf)],
              []),
        "B": (build_phaseB,
              [("q0", [128, SEQ * 8 * 2048], bf), ("k0", [128, SEQ * 2 * 2048], bf),
               ("v0", [128, SEQ * 16 * 256], bf), ("x0", [128, NTILES, CH, NT], f32),
               ("wout", [1024, 1024], f32), ("bout_fm", [128, 8], f32),
               ("rwt", [1024, 8], f32), ("causal", [128, 512], f32),
               ("identf", [128, 128], f32)],
              [("x1", [128, NTILES, CH, NT], f32), ("xn1", [128, NTILES, CH, NT], bf),
               ("gates", [8, T], f32)],
              []),
        "C": (build_phaseC,
              [("xn1", [128, NTILES, CH, NT], bf), ("x1", [128, NTILES, CH, NT], f32),
               ("gates", [8, T], f32), ("w1T", [NE, 1024, 2048], f32),
               ("w2T", [NE, 2048, 1024], f32)],
              [("x2", [128, NTILES, CH, NT], f32)],
              [("accC", [128, NTILES, CH, NT], f32)]),
        "D": (build_phaseD,
              [("x2", [128, NTILES, CH, NT], f32), ("wq1", [1024, 1024], f32),
               ("bq1", [128, 8], f32), ("wl1", [1024, 256], f32),
               ("bl1", [128, 2], f32), ("wk1", [256, 1024], f32),
               ("bk1", [128, 8], f32), ("wv1", [256, 1024], f32),
               ("bv1", [1, 1024], f32)],
              [("q1", [128, SEQ * 8 * 2048], bf), ("k1", [128, SEQ * 8 * 2048], bf),
               ("v1", [128, SEQ * 16 * 1024], bf)],
              []),
        "E": (build_phaseE,
              [("q1", [128, SEQ * 8 * 2048], bf), ("k1", [128, SEQ * 8 * 2048], bf),
               ("v1", [128, SEQ * 16 * 1024], bf), ("x2", [128, NTILES, CH, NT], f32),
               ("wo1", [1024, 1024], f32), ("bo1", [128, 8], f32)],
              [("x3", [128, NTILES, CH, NT], f32), ("xn3", [128, NTILES, CH, NT], bf)],
              []),
        "F": (build_phaseF,
              [("x3", [128, NTILES, CH, NT], f32), ("xn3", [128, NTILES, CH, NT], bf),
               ("e0", [128, 32, 1025], f32), ("g0", [128, 32, 1024], f32),
               ("upchunk", [7, 32, 128, 128], f32), ("dnchunk", [7, 32, 128, 128], f32),
               ("updself", [128, 160], f32), ("updoth", [128, 160], f32),
               ("dndself", [128, 160], f32), ("dndoth", [128, 160], f32),
               ("upbias", [128, 32], f32), ("bcon", [128, CH], f32),
               ("dbias", [128, CH], f32), ("identfF", [128, 128], f32)],
              [("out", [SEQ, 2048, 1024], f32)],
              [("escr", [4096, 1024], bf)]),
    }
    return _SPECS


def _phase_inputs(name, hp, core_arrays):
    core_arrays = _DefaultNone(core_arrays)
    if name == "A":
        return {"xin": core_arrays["xin"], "wqa": hp["Wqa"], "wqbp": hp["Wqb_perm"],
                "wkva": hp["Wkva"], "wk0p": hp["Wk0_perm"], "wv0": hp["Wv0"],
                "cossq": hp["COSSq"], "sinsq": hp["SINSq"], "cossk": hp["COSSk"],
                "sinsk": hp["SINSk"], "permbf": hp["PERM"], "ident": hp["IDENT"]}
    if name == "B":
        return {"q0": core_arrays["q0"], "k0": core_arrays["k0"],
                "v0": core_arrays["v0"], "x0": core_arrays["x0"],
                "wout": hp["Wout"], "bout_fm": hp["bout_fm"], "rwt": hp["router_wT"],
                "causal": hp["CAUSAL"], "identf": hp["IDENT"]}
    if name == "C":
        return {"xn1": core_arrays["xn1"], "x1": core_arrays["x1"],
                "gates": core_arrays["gates"], "w1T": hp["w1T"], "w2T": hp["w2T"]}
    if name == "D":
        return {"x2": core_arrays["x2"], "wq1": hp["Wq1"], "bq1": hp["bq1_fm"],
                "wl1": hp["Wl1"], "bl1": hp["bl1_fm"], "wk1": hp["Wk1"],
                "bk1": hp["bk1_fm"], "wv1": hp["Wv1"], "bv1": hp["bv1_row"]}
    if name == "E":
        return {"q1": core_arrays["q1"], "k1": core_arrays["k1"],
                "v1": core_arrays["v1"], "x2": core_arrays["x2"],
                "wo1": hp["Wo1"], "bo1": hp["bo1_fm"]}
    if name == "F":
        return {"x3": core_arrays["x3"], "xn3": core_arrays["xn3"],
                "e0": hp["E0"], "g0": hp["G0"], "upchunk": hp["UP_CHUNK"],
                "dnchunk": hp["DN_CHUNK"], "updself": hp["UP_DSELF_dev"],
                "updoth": hp["UP_DOTH_dev"], "dndself": hp["DN_DSELF_dev"],
                "dndoth": hp["DN_DOTH_dev"], "upbias": hp["up_bias_fm"],
                "bcon": hp["b_con_fm"], "dbias": hp["down_bias_fm"],
                "identfF": hp["IDENT"]}
    raise KeyError(name)


def _mk_fused():
    specs = _phase_specs()
    nc = bass.Bass("TRN2", target_bir_lowering=False, debug=False)
    dram = {}
    ext_in = set()
    stream_names = {"x0", "x1", "x2", "x3", "xn1", "xn3", "gates",
                    "q0", "k0", "v0", "q1", "k1", "v1"}
    for name in "ABCDEF":
        fn, ins, outs, internals = specs[name]
        for nm, shape, dt in ins:
            if nm in dram:
                continue
            if nm in stream_names:
                continue
            dram[nm] = nc.dram_tensor(nm, shape, dt, kind="ExternalInput").ap()
            ext_in.add(nm)
        for nm, shape, dt in outs:
            if nm in dram:
                continue
            kind = "ExternalOutput" if nm == "out" else "Internal"
            dram[nm] = nc.dram_tensor(nm, shape, dt, kind=kind).ap()
        for nm, shape, dt in internals:
            dram[nm] = nc.dram_tensor(nm, shape, dt, kind="Internal").ap()
    es = ExitStack()
    ctx = Ctx(nc, es)
    with es:
        with nc.Block() as block:
            @block.sync
            def _(sync):
                for name in "ABCDEF":
                    specs[name][0](ctx, dram)
    return nc, ext_in


def kernel(**inputs):
    import os
    hp = host_prep({k: np.asarray(v) for k, v in inputs.items()})
    x_full = np.asarray(inputs["L0_attn_x"], dtype=np.float32)
    ncores = NCORES
    core_state = [{"xin": np.ascontiguousarray(x_full[c * SEQ:(c + 1) * SEQ])}
                  for c in range(ncores)]
    specs = _phase_specs()
    if os.environ.get("MK_UNFUSED"):
        for name in "ABCDEF":
            fn, ins, outs, internals = specs[name]
            nc = _mk_builder(fn, ins, outs, internals)
            in_maps = [_phase_inputs(name, hp, core_state[c]) for c in range(ncores)]
            res = run_bass_kernel_spmd(nc, in_maps, core_ids=list(range(ncores)))
            for c in range(ncores):
                core_state[c].update(res.results[c])
        return np.concatenate([core_state[c]["out"] for c in range(ncores)], axis=0)
    nc, ext_in = _mk_fused()
    in_maps = []
    for c in range(ncores):
        m = {}
        for name in "ABCDEF":
            for k, v in _phase_inputs(name, hp, core_state[c]).items():
                if k in ext_in and v is not None:
                    m[k] = v
        in_maps.append(m)
    res = run_bass_kernel_spmd(nc, in_maps, core_ids=list(range(ncores)))
    return np.concatenate([res.results[c]["out"] for c in range(ncores)], axis=0)


# version 05
# speedup vs baseline: 1.4165x; 1.1012x over previous
import numpy as np
import ml_dtypes
import concourse.bass as bass
import concourse.mybir as mybir
from concourse.bass_utils import run_bass_kernel_spmd

F32 = mybir.dt.float32
BF16 = mybir.dt.bfloat16
AF = mybir.ActivationFunctionType
ALU = mybir.AluOpType
AX = mybir.AxisListType

BF = ml_dtypes.bfloat16

NCORES = 8
B_FULL, S, E = 32, 2048, 1024
SEQ = 4
T = SEQ * S
NT = 512
NTILES = T // NT
CH = E // 128
H, KVH, D = 16, 4, 64
QL, KVL = 64, 32
ROPE_BASE = 10000.0
SCALE0 = D ** -0.5
NE, DFF, TOPK = 8, 2048, 2
LAT = 256
DH = 4096
NSTAGE = 12
LN_EPS = 1e-5



def _deint_idx():
    r = np.arange(64)
    return np.where(r < 32, 2 * r, 2 * (r - 32) + 1)


def _rope_tables():
    inv = 1.0 / ROPE_BASE ** (np.arange(0, D, 2, dtype=np.float64) / D)
    t = np.arange(S, dtype=np.float64)
    fr = np.outer(inv, t)
    cos = np.cos(fr).astype(np.float32)
    sin = np.sin(fr).astype(np.float32)
    coss = np.concatenate([cos, cos, cos, cos], 0)
    sins = np.concatenate([-sin, sin, -sin, sin], 0)
    return coss, sins


def _perm_swap32():
    p = np.arange(128)
    swap = (p // 32 % 2 == 0) * (p + 32) + (p // 32 % 2 == 1) * (p - 32)
    m = np.zeros((128, 128), np.float32)
    m[swap, p] = 1.0
    return m


def _bfly_mats(weights, transpose):
    chunk_lhsT = []
    for s in range(7):
        s_ = 1 << s
        B_full = np.zeros((128, 128), np.float32)
        mats = []
        for c in range(32):
            Bm = np.zeros((128, 128), np.float32)
            for i_lo in range(128):
                i = c * 128 + i_lo
                if i & s_:
                    continue
                j = i + s_
                j_lo = j - c * 128
                k = (i // (2 * s_)) * s_ + (i % s_)
                w = weights[s, k]
                if transpose:
                    w = w.T
                Bm[i_lo, i_lo] = w[0, 0]
                Bm[i_lo, j_lo] = w[0, 1]
                Bm[j_lo, i_lo] = w[1, 0]
                Bm[j_lo, j_lo] = w[1, 1]
            mats.append(np.ascontiguousarray(Bm.T))
        chunk_lhsT.append(np.stack(mats))
    diag = []
    for s in range(7, 12):
        s_ = 1 << s
        bit = s_ // 128
        d_self = np.zeros((32, 128), np.float32)
        d_other = np.zeros((32, 128), np.float32)
        for c in range(32):
            for p in range(128):
                i = c * 128 + p
                k = (i // (2 * s_)) * s_ + (i % s_) if not (i & s_) else None
                if i & s_:
                    ip = i - s_
                    k = (ip // (2 * s_)) * s_ + (ip % s_)
                    w = weights[s, k]
                    if transpose:
                        w = w.T
                    d_self[c, p] = w[1, 1]
                    d_other[c, p] = w[1, 0]
                else:
                    k = (i // (2 * s_)) * s_ + (i % s_)
                    w = weights[s, k]
                    if transpose:
                        w = w.T
                    d_self[c, p] = w[0, 0]
                    d_other[c, p] = w[0, 1]
        diag.append((d_self, d_other))
    return chunk_lhsT, diag


def host_prep(inp):
    d = {}
    deint = _deint_idx()
    d["Wqa"] = np.ascontiguousarray(inp["L0_attn_Wqa"])
    Wqb = inp["L0_attn_Wqb"]
    cols = np.empty(1024, np.int64)
    for hp in range(H // 2):
        for half in range(2):
            h = 2 * hp + half
            cols[hp * 128 + half * 64:hp * 128 + half * 64 + 64] = h * 64 + deint
    d["Wqb_perm"] = np.ascontiguousarray(Wqb[:, cols])
    d["Wkva"] = np.ascontiguousarray(inp["L0_attn_Wkva"])
    Wkvb = inp["L0_attn_Wkvb"]
    kcols = np.empty(256, np.int64)
    vcols = np.empty(256, np.int64)
    for kp in range(KVH // 2):
        for half in range(2):
            g = 2 * kp + half
            kcols[kp * 128 + half * 64:kp * 128 + half * 64 + 64] = g * 128 + 2 * deint
    for g in range(KVH):
        vcols[g * 64:(g + 1) * 64] = g * 128 + 2 * np.arange(64) + 1
    d["Wk0_perm"] = np.ascontiguousarray(Wkvb[:, kcols])
    d["Wv0"] = np.ascontiguousarray(Wkvb[:, vcols])
    d["Wout"] = np.ascontiguousarray(inp["L0_attn_Wout"])
    d["bout_fm"] = np.ascontiguousarray(inp["L0_attn_bout"].reshape(CH, 128).T)
    coss, sins = _rope_tables()
    d["COSSq"] = coss * np.float32(SCALE0)
    d["SINSq"] = sins * np.float32(SCALE0)
    d["COSSk"] = coss
    d["SINSk"] = sins
    d["PERM"] = _perm_swap32()
    d["IDENT"] = np.eye(128, dtype=np.float32)
    causal = np.zeros((128, 512), np.float32)
    kk, q = np.meshgrid(np.arange(128), np.arange(128), indexing="ij")
    cm = np.where(kk > q, np.float32(-1e30), np.float32(0.0))
    for j in range(4):
        causal[:, j * 128:(j + 1) * 128] = cm
    d["CAUSAL"] = causal
    d["router_wT"] = np.ascontiguousarray(inp["L0_ffn_router_w"].T)
    d["w1T"] = np.ascontiguousarray(inp["L0_ffn_w1"].transpose(0, 2, 1))
    d["w2T"] = np.ascontiguousarray(inp["L0_ffn_w2"].transpose(0, 2, 1))
    d["Wq1"] = np.ascontiguousarray(inp["L1_attn_Wq"])
    d["bq1_fm"] = np.ascontiguousarray(inp["L1_attn_bq"].reshape(CH, 128).T)
    d["Wl1"] = np.ascontiguousarray(inp["L1_attn_Wl"])
    d["bl1_fm"] = np.ascontiguousarray(inp["L1_attn_bl"].reshape(2, 128).T)
    d["Wk1"] = np.ascontiguousarray(inp["L1_attn_Wk"])
    d["bk1_fm"] = np.ascontiguousarray(inp["L1_attn_bk"].reshape(CH, 128).T)
    d["Wv1"] = np.ascontiguousarray(inp["L1_attn_Wv"])
    d["bv1_row"] = np.ascontiguousarray(inp["L1_attn_bv"].reshape(1, 1024))
    d["Wo1"] = np.ascontiguousarray(inp["L1_attn_Wo"])
    d["bo1_fm"] = np.ascontiguousarray(inp["L1_attn_bo"].reshape(CH, 128).T)
    E0 = np.concatenate([inp["L1_ffn_W_exp"].T,
                         inp["L1_ffn_b_exp"].reshape(DH, 1)], axis=1)
    d["E0"] = np.ascontiguousarray(E0.reshape(32, 128, 1025).transpose(1, 0, 2))
    G0 = inp["L1_ffn_W_con"]
    d["G0"] = np.ascontiguousarray(G0.reshape(32, 128, 1024).transpose(1, 0, 2))
    up_chunk, up_diag = _bfly_mats(inp["L1_ffn_up_weights"], transpose=False)
    dn_chunk, dn_diag = _bfly_mats(inp["L1_ffn_down_weights"], transpose=True)
    d["UP_CHUNK"] = np.stack(up_chunk)
    d["DN_CHUNK"] = np.stack(dn_chunk)
    d["UP_DSELF"] = np.stack([a for a, b in up_diag])
    d["UP_DOTH"] = np.stack([b for a, b in up_diag])
    d["DN_DSELF"] = np.stack([a for a, b in dn_diag])
    d["DN_DOTH"] = np.stack([b for a, b in dn_diag])
    for k in ("UP_DSELF", "UP_DOTH", "DN_DSELF", "DN_DOTH"):
        d[k + "_dev"] = np.ascontiguousarray(
            d[k].transpose(2, 0, 1).reshape(128, 5 * 32))
    d["up_bias_fm"] = np.ascontiguousarray(
        inp["L1_ffn_up_bias"].reshape(32, 128).T)
    d["b_con_fm"] = np.ascontiguousarray(inp["L1_ffn_b_con"].reshape(CH, 128).T)
    d["down_bias_fm"] = np.ascontiguousarray(inp["L1_ffn_down_bias"].reshape(CH, 128).T)
    return d



from contextlib import ExitStack


class Buf:
    __slots__ = ("w", "r")

    def __init__(self):
        self.w = None
        self.r = {}

    def w_toks(self):
        return [self.w] if self.w is not None else []

    def r_toks(self):
        return list(self.r.items())

    def note_read(self, tok):
        if self.r.get(tok[0], 0) < tok[1]:
            self.r[tok[0]] = tok[1]

    def note_write(self, tok):
        self.w = tok
        self.r = {}


class BufView:

    def __init__(self, *bufs):
        self.bufs = bufs

    def w_toks(self):
        return [t for b in self.bufs for t in b.w_toks()]

    def r_toks(self):
        return [t for b in self.bufs for t in b.r_toks()]

    def note_read(self, tok):
        for b in self.bufs:
            b.note_read(tok)

    def note_write(self, tok):
        for b in self.bufs:
            b.note_write(tok)


class Ctx:
    def __init__(self, nc, es):
        self.nc = nc
        self.es = es
        self.sems = {}
        self.cnt = {}
        self.waited = {}
        self.eng = {"t": nc.tensor, "v": nc.vector, "s": nc.scalar,
                    "p": nc.gpsimd, "q": nc.sync}
        self.esem = {"t": "T", "v": "V", "s": "S", "p": "P", "q": "Q"}
        self.dma_k = {"T": 1, "V": 1, "S": 1, "P": 1, "Q": 1,
                      "DQ": 16, "DP": 16, "DS": 16}
        self.dma_sems = {"DQ", "DP", "DS"}

    def sem(self, name):
        if name not in self.sems:
            self.sems[name] = self.es.enter_context(self.nc.semaphore(name))
            self.cnt[name] = 0
        return self.sems[name]

    def _wait(self, ename, toks, skip_self=True, dma_sem=None):
        skip = ()
        if skip_self:
            if ename == "t":
                skip = ("T",)
            if dma_sem is not None:
                skip = skip + (dma_sem,)
        need = {}
        for t in toks:
            if t is None:
                continue
            s, v = t
            if s in skip:
                continue
            if v > need.get(s, 0):
                need[s] = v
        eng = self.eng[ename]
        w = self.waited.setdefault(ename, {})
        for s, v in sorted(need.items()):
            if w.get(s, -1) >= v:
                continue
            eng.wait_ge(self.sem(s), v)
            w[s] = v

    def op(self, ename, f, reads=(), writes=(), sem=None, dep=()):
        sem = sem or self.esem[ename]
        toks = list(dep)
        for b in reads:
            toks.extend(b.w_toks())
        for b in writes:
            toks.extend(b.w_toks())
            toks.extend(b.r_toks())
        self._wait(ename, toks,
                   dma_sem=sem if sem in self.dma_sems else None)
        inst = f()
        k = self.dma_k.get(sem, 1)
        h = self.sem(sem)
        inst.then_inc(h, k)
        self.cnt[sem] += k
        tok = (sem, self.cnt[sem])
        for b in reads:
            b.note_read(tok)
        for b in writes:
            b.note_write(tok)
        return tok

    def barrier(self, tokens=None):
        toks = [(s, c) for s, c in self.cnt.items()]
        for ename in self.eng:
            self._wait(ename, toks, skip_self=False)


def _mk_dram(nc, name, shape, dtype, kind):
    return nc.dram_tensor(name, shape, dtype, kind=kind).ap()


class Phase:
    _n = 0

    def __init__(self, ctx):
        self.ctx = ctx
        self.es = ExitStack()
        self.bufs = {}
        Phase._n += 1
        self.pfx = f"ph{Phase._n}_"

    def sb(self, name, shape, dtype):
        t = self.es.enter_context(self.ctx.nc.sbuf_tensor(self.pfx + name, shape, dtype))
        b = Buf()
        self.bufs[name] = (t, b)
        return t, b

    def ps(self, name, shape, dtype=F32):
        t = self.es.enter_context(self.ctx.nc.psum_tensor(self.pfx + name, shape, dtype))
        b = Buf()
        self.bufs[name] = (t, b)
        return t, b

    def close(self):
        self.ctx.barrier()
        self.es.close()



def load_consts(ctx, ph, dram, names):
    out = {}
    for name, sbdtype in names:
        ap = dram[name]
        shp = list(ap.tensor.shape)
        if len(shp) == 2 and shp[0] > 128:
            assert shp[0] % 128 == 0
            ap = ap.rearrange("(c p) m -> p c m", p=128)
            shp = [128, shp[0] // 128, shp[1]]
        t, b = ph.sb("c_" + name, shp, sbdtype)
        eng = "p" if sbdtype != ap.tensor.dtype else "q"
        sem = "DP" if eng == "p" else "DQ"
        ctx.op(eng, lambda t=t, ap=ap, eng=eng: ctx.eng[eng].dma_start(out=t[:], in_=ap),
               writes=(b,), sem=sem)
        out[name] = (t, b)
    return out



def emit_ln(ctx, ph, sfx, x3, x_buf, dep, psS_p, psA_p, psB_p, NTc=NT, scratch=None,
            stat_scratch=None, xn_scratch=None):
    nc = ctx.nc
    key = "ln_key" + sfx
    if key not in ph.bufs:
        ph.sb("ln_key" + sfx, [1, 1], F32)
        if stat_scratch is None:
            ph.sb("ln_stat" + sfx, [1, 6 * NTc], F32)
        if xn_scratch is None:
            ph.sb("ln_xn" + sfx, [128, CH * NTc], BF16)
        if scratch is None:
            ph.sb("ln_xb" + sfx, [128, CH * NTc], BF16)
            ph.sb("ln_sq" + sfx, [128, CH * NTc], BF16)
    if scratch is None:
        xbt, xb_b = ph.bufs["ln_xb" + sfx]
        sqt, sq_b = ph.bufs["ln_sq" + sfx]
        xb, sq = xbt[:], sqt[:]
    else:
        xb, sq, shared = scratch
        xb_b = sq_b = shared
    if stat_scratch is None:
        stat, stat_b = ph.bufs["ln_stat" + sfx]
    else:
        stat, stat_b = stat_scratch
    if xn_scratch is None:
        xnt_, xn_b = ph.bufs["ln_xn" + sfx]
        xn = xnt_[:]
    else:
        xn, xn_b = xn_scratch
    psS, psS_b = psS_p
    psA, psA_b = psA_p
    psB, psB_b = psB_p
    ones128_bf, ones128_b = ph.bufs["ones128_bf"]
    ones1, ones1_b = ph.bufs["ones1"]

    ctx.op("v", lambda: nc.vector.tensor_copy(out=xb, in_=x3),
           reads=(x_buf,), writes=(xb_b,), dep=dep)
    ctx.op("s", lambda: nc.scalar.activation(out=sq, in_=xb, func=AF.Square),
           reads=(xb_b,), writes=(sq_b,))
    xbv = xb.rearrange("p (c n) -> p c n", c=CH)
    sqv = sq.rearrange("p (c n) -> p c n", c=CH)

    def sums():
        for c in range(CH):
            nc.tensor.matmul(out=psS[0:1, 0:NTc], lhsT=ones128_bf[:], rhs=xbv[:, c, :],
                             start=(c == 0), stop=(c == CH - 1))
        for c in range(CH):
            i = nc.tensor.matmul(out=psS[0:1, NTc:2 * NTc], lhsT=ones128_bf[:],
                                 rhs=sqv[:, c, :], start=(c == 0), stop=(c == CH - 1))
        return i
    ctx.op("t", sums, reads=(xb_b, sq_b, ones128_b), writes=(psS_b,))
    n = NTc
    mean, ms, var, sd, a_r, b_r = (stat[:1, i * n:(i + 1) * n] for i in range(6))
    ctx.op("v", lambda: nc.vector.tensor_scalar(out=mean, in0=psS[0:1, 0:n],
           scalar1=1.0 / E, scalar2=None, op0=ALU.mult), reads=(psS_b,), writes=(stat_b,))
    ctx.op("v", lambda: nc.vector.tensor_scalar(out=ms, in0=psS[0:1, n:2 * n],
           scalar1=1.0 / E, scalar2=None, op0=ALU.mult), reads=(psS_b,), writes=(stat_b,))
    ctx.op("v", lambda: nc.vector.tensor_mul(out=var, in0=mean, in1=mean),
           reads=(stat_b,), writes=(stat_b,))
    ctx.op("v", lambda: nc.vector.tensor_sub(out=var, in0=ms, in1=var),
           reads=(stat_b,), writes=(stat_b,))
    epst, epst_b = ph.bufs["epsln"]
    ctx.op("s", lambda: nc.scalar.activation(out=sd, in_=var, func=AF.Sqrt,
           bias=epst[:]), reads=(stat_b, epst_b), writes=(stat_b,))
    ctx.op("v", lambda: nc.vector.reciprocal(out=a_r, in_=sd),
           reads=(stat_b,), writes=(stat_b,))
    ctx.op("v", lambda: nc.vector.scalar_tensor_tensor(out=b_r, in0=mean, scalar=-1.0,
           in1=a_r, op0=ALU.mult, op1=ALU.mult), reads=(stat_b,), writes=(stat_b,))
    ctx.op("t", lambda: nc.tensor.matmul(out=psA[:], lhsT=ones1[:], rhs=a_r,
           start=True, stop=True), reads=(stat_b, ones1_b), writes=(psA_b,))
    ctx.op("t", lambda: nc.tensor.matmul(out=psB[:], lhsT=ones1[:], rhs=b_r,
           start=True, stop=True), reads=(stat_b, ones1_b), writes=(psB_b,))
    xnv = xn.rearrange("p (c n) -> p c n", c=CH)
    tok = None
    for c in range(CH):
        ctx.op("v", lambda c=c: nc.vector.tensor_mul(out=xnv[:, c, :], in0=x3[:, c, :],
               in1=psA[:]), reads=(x_buf, psA_b), writes=(xn_b,))
        tok = ctx.op("v", lambda c=c: nc.vector.tensor_add(out=xnv[:, c, :],
               in0=xnv[:, c, :], in1=psB[:]), reads=(psB_b, xn_b), writes=(xn_b,))
    return xn, xn_b, tok


def init_common(ctx, ph):
    nc = ctx.nc
    o128, o128_b = ph.sb("ones128_bf", [128, 1], BF16)
    o1, o1_b = ph.sb("ones1", [1, 128], F32)
    eps, eps_b = ph.sb("epsln", [1, 1], F32)
    ctx.op("v", lambda: nc.vector.memset(o128[:], 1.0), writes=(o128_b,))
    ctx.op("v", lambda: nc.vector.memset(o1[:], 1.0), writes=(o1_b,))
    ctx.op("v", lambda: nc.vector.memset(eps[:], LN_EPS), writes=(eps_b,))


def build_phaseA(ctx, dram):
    nc = ctx.nc
    ph = Phase(ctx)
    init_common(ctx, ph)
    cst = load_consts(ctx, ph, dram, [
        ("wqa", BF16), ("wqbp", BF16), ("wkva", BF16), ("wk0p", BF16), ("wv0", BF16),
        ("cossq", F32), ("sinsq", F32), ("cossk", F32), ("sinsk", F32),
        ("permbf", BF16), ("ident", F32),
    ])
    wqa, wqa_b = cst["wqa"]
    wqav = wqa[:]
    wqbp, wqbp_b = cst["wqbp"]
    wkva, wkva_b = cst["wkva"]
    wkvav = wkva[:]
    wk0p, wk0p_b = cst["wk0p"]
    wv0, wv0_b = cst["wv0"]
    cossq, cossq_b = cst["cossq"]
    sinsq, sinsq_b = cst["sinsq"]
    cossk, cossk_b = cst["cossk"]
    sinsk, sinsk_b = cst["sinsk"]
    perm, perm_b = cst["permbf"]
    ident, ident_b = cst["ident"]

    xin = dram["xin"]
    xin_flat = xin.rearrange("s t f -> (s t) f")
    x0_d = dram["x0"]
    q0_d = dram["q0"]
    k0_d = dram["k0"]
    v0_d = dram["v0"]
    x0_db, q0_db, k0_db, v0_db = Buf(), Buf(), Buf(), Buf()
    q0v = q0_d.rearrange("p (s h t) -> p s h t", s=SEQ, h=8)
    k0v = k0_d.rearrange("p (s h t) -> p s h t", s=SEQ, h=2)
    v0v = v0_d.rearrange("p (s u m) -> p s u m", s=SEQ, u=16)

    xt = [ph.sb(f"xt{i}", [128, 4 * 1024], F32) for i in range(2)]
    x0sb = [ph.sb(f"x0sb{i}", [128, CH * NT], F32) for i in range(2)]
    latq = [ph.sb(f"latq{i}", [64, NT], BF16) for i in range(2)]
    latkv = [ph.sb(f"latkv{i}", [32, NT], BF16) for i in range(2)]
    qbf2 = [ph.sb(f"qbf{i}", [128, NT], BF16) for i in range(2)]
    ro12 = [ph.sb(f"ro1{i}", [128, NT], F32) for i in range(2)]
    ro22 = [ph.sb(f"ro2{i}", [128, NT], F32) for i in range(2)]
    qout2 = [ph.sb(f"qout{i}", [128, NT], BF16) for i in range(2)]
    vout = ph.sb("vout", [128, 4 * 256], BF16)

    ps_x = ph.ps("ps_x", [128, 1024], F32)
    ps_lat = ph.ps("ps_lat", [128, NT], F32)
    ps_q = [ph.ps(f"ps_q{i}", [128, NT], F32) for i in range(2)]
    ps_sw = ph.ps("ps_sw", [128, NT], F32)
    ps_A = ph.ps("ps_AA", [128, NT], F32)
    ps_B = ph.ps("ps_BA", [128, NT], F32)
    ps_v = ps_lat

    for nt in range(NTILES):
        s, a4 = nt // 4, nt % 4
        pos0 = a4 * NT
        xt_t, xt_b = xt[nt % 2]
        x0_t, x0_b = x0sb[nt % 2]
        src = xin_flat[nt * NT:(nt + 1) * NT, :].rearrange("(a p) f -> p a f", p=128)
        ctx.op("q", lambda xt_t=xt_t, src=src: nc.sync.dma_start(out=xt_t[:].rearrange(
            "p (a f) -> p a f", a=4), in_=src), writes=(xt_b,), sem="DQ")
        xtv = xt_t[:].rearrange("p (a f) -> p a f", a=4)
        x0v = x0_t[:].rearrange("p (c n) -> p c n", c=CH)
        for a in range(4):
            def trs(a=a, xtv=xtv):
                i = None
                for c in range(CH):
                    i = nc.tensor.transpose(out=ps_x[0][:, c * 128:(c + 1) * 128],
                                            in_=xtv[:, a, c * 128:(c + 1) * 128],
                                            identity=ident[:])
                return i
            ctx.op("t", trs, reads=(xt_b, ident_b), writes=(ps_x[1],))
            ctx.op("v", lambda a=a, x0v=x0v: nc.vector.tensor_copy(
                out=x0v[:, :, a * 128:(a + 1) * 128],
                in_=ps_x[0][:].rearrange("p (c n) -> p c n", c=CH)),
                reads=(ps_x[1],), writes=(x0_b,))
        ctx.op("s", lambda x0v=x0v, nt=nt: nc.scalar.dma_start(
            out=x0_d[:, nt], in_=x0v), reads=(x0_b,), sem="DS")
        xn, xn_b, _ = emit_ln(ctx, ph, "A", x0v, x0_b, (), ps_x, ps_A, ps_B)
        xnv = xn[:].rearrange("p (c n) -> p c n", c=CH)
        latq_t, latq_b = latq[nt % 2]
        latkv_t, latkv_b = latkv[nt % 2]

        def mm_lat():
            for c in range(CH):
                nc.tensor.matmul(out=ps_lat[0][0:64, :], lhsT=wqav[:, c, :],
                                 rhs=xnv[:, c, :], start=(c == 0), stop=(c == CH - 1))
            for c in range(CH):
                i = nc.tensor.matmul(out=ps_lat[0][64:96, :], lhsT=wkvav[:, c, :],
                                     rhs=xnv[:, c, :], start=(c == 0), stop=(c == CH - 1))
            return i
        ctx.op("t", mm_lat, reads=(xn_b, wqa_b, wkva_b), writes=(ps_lat[1],))
        ctx.op("v", lambda latq_t=latq_t: nc.vector.tensor_copy(out=latq_t[:],
               in_=ps_lat[0][0:64, :]), reads=(ps_lat[1],), writes=(latq_b,))
        ctx.op("s", lambda latkv_t=latkv_t: nc.scalar.copy(out=latkv_t[:],
               in_=ps_lat[0][64:96, :]), reads=(ps_lat[1],), writes=(latkv_b,))
        for hp in range(8):
            pq, pq_b = ps_q[hp % 2]
            qbf = qbf2[hp % 2]
            ro1 = ro12[hp % 2]
            ro2 = ro22[hp % 2]
            qout = qout2[hp % 2]
            ctx.op("t", lambda hp=hp, pq=pq, latq_t=latq_t: nc.tensor.matmul(
                out=pq[:], lhsT=wqbp[:, hp * 128:(hp + 1) * 128], rhs=latq_t[:],
                start=True, stop=True), reads=(latq_b, wqbp_b), writes=(pq_b,))
            ctx.op("s", lambda pq=pq: nc.scalar.copy(out=qbf[0][:], in_=pq[:]),
                   reads=(pq_b,), writes=(qbf[1],))
            ctx.op("t", lambda: nc.tensor.matmul(out=ps_sw[0][:], lhsT=perm[:],
                   rhs=qbf[0][:], start=True, stop=True),
                   reads=(qbf[1], perm_b), writes=(ps_sw[1],))
            ctx.op("v", lambda pos0=pos0: nc.vector.tensor_mul(out=ro1[0][:],
                   in0=ps_sw[0][:], in1=sinsq[:, pos0:pos0 + NT]),
                   reads=(ps_sw[1], sinsq_b), writes=(ro1[1],))
            ctx.op("v", lambda pos0=pos0: nc.vector.tensor_mul(out=ro2[0][:],
                   in0=qbf[0][:], in1=cossq[:, pos0:pos0 + NT]),
                   reads=(qbf[1], cossq_b), writes=(ro2[1],))
            ctx.op("v", lambda: nc.vector.tensor_add(out=qout[0][:], in0=ro1[0][:],
                   in1=ro2[0][:]), reads=(ro1[1], ro2[1]), writes=(qout[1],))
            ctx.op("s", lambda s=s, hp=hp, pos0=pos0: nc.scalar.dma_start(
                out=q0v[:, s, hp, pos0:pos0 + NT], in_=qout[0][:]),
                reads=(qout[1],), sem="DS", writes=(q0_db,))
        for kp in range(2):
            pq, pq_b = ps_q[kp % 2]
            qbf = qbf2[kp % 2]
            ro1 = ro12[kp % 2]
            ro2 = ro22[kp % 2]
            qout = qout2[kp % 2]
            ctx.op("t", lambda kp=kp, pq=pq, latkv_t=latkv_t: nc.tensor.matmul(
                out=pq[:], lhsT=wk0p[:, kp * 128:(kp + 1) * 128], rhs=latkv_t[:],
                start=True, stop=True), reads=(latkv_b, wk0p_b), writes=(pq_b,))
            ctx.op("s", lambda pq=pq: nc.scalar.copy(out=qbf[0][:], in_=pq[:]),
                   reads=(pq_b,), writes=(qbf[1],))
            ctx.op("t", lambda: nc.tensor.matmul(out=ps_sw[0][:], lhsT=perm[:],
                   rhs=qbf[0][:], start=True, stop=True),
                   reads=(qbf[1], perm_b), writes=(ps_sw[1],))
            ctx.op("v", lambda pos0=pos0: nc.vector.tensor_mul(out=ro1[0][:],
                   in0=ps_sw[0][:], in1=sinsk[:, pos0:pos0 + NT]),
                   reads=(ps_sw[1], sinsk_b), writes=(ro1[1],))
            ctx.op("v", lambda pos0=pos0: nc.vector.tensor_mul(out=ro2[0][:],
                   in0=qbf[0][:], in1=cossk[:, pos0:pos0 + NT]),
                   reads=(qbf[1], cossk_b), writes=(ro2[1],))
            ctx.op("v", lambda: nc.vector.tensor_add(out=qout[0][:], in0=ro1[0][:],
                   in1=ro2[0][:]), reads=(ro1[1], ro2[1]), writes=(qout[1],))
            ctx.op("s", lambda s=s, kp=kp, pos0=pos0: nc.scalar.dma_start(
                out=k0v[:, s, kp, pos0:pos0 + NT], in_=qout[0][:]),
                reads=(qout[1],), sem="DS", writes=(k0_db,))
        voutv = vout[0][:].rearrange("p (a m) -> p a m", a=4)
        for sub in range(4):
            ctx.op("t", lambda sub=sub, latkv_t=latkv_t: nc.tensor.matmul(
                out=ps_v[0][:, 0:256], lhsT=latkv_t[:, sub * 128:(sub + 1) * 128],
                rhs=wv0[:], start=True, stop=True),
                reads=(latkv_b, wv0_b), writes=(ps_v[1],))
            ctx.op("v", lambda sub=sub: nc.vector.tensor_copy(out=voutv[:, sub, :],
                   in_=ps_v[0][:, 0:256]), reads=(ps_v[1],), writes=(vout[1],))
        ctx.op("s", lambda s=s, a4=a4: nc.scalar.dma_start(
            out=v0v[:, s, a4 * 4:(a4 + 1) * 4, :], in_=voutv),
            reads=(vout[1],), sem="DS", writes=(v0_db,))
    ph.close()



def _load_qk_seq(ctx, nc, dst, dst_b, src_view, s, nh, sem="DQ"):
    dstv = dst[:].rearrange("p (h t) -> p h t", h=nh)
    toks = []
    for half in range(2):
        for par in range(2):
            out_ap = dstv[half * 64:(half + 1) * 64, par::2, :]
            in_ap = src_view[par * 64:par * 64 + 64, s, :, half * 1024:(half + 1) * 1024]
            toks.append(ctx.op("q", lambda o=out_ap, i=in_ap: nc.sync.dma_start(
                out=o, in_=i), writes=(dst_b,), sem=sem))
    return toks[-1]


def emit_bound(ctx, ph, nc, q_sb, q_b, ncols, bmax, bmax_b, sq, sq_b, psn, psn_b,
               mred, mred_b, nrows=64):
    ones128_bf, ones128_b = ph.bufs["ones128_bf"]
    nchunks = ncols // 512
    first = True
    for ci in range(nchunks):
        sl = slice(ci * 512, (ci + 1) * 512)
        ctx.op("s", lambda sl=sl: nc.scalar.activation(out=sq[0:nrows, :],
               in_=q_sb[0:nrows, sl], func=AF.Square), reads=(q_b,), writes=(sq_b,))
        ctx.op("t", lambda: nc.tensor.matmul(out=psn[0:1, 0:512],
               lhsT=ones128_bf[0:nrows, :], rhs=sq[0:nrows, :], start=True, stop=True),
               reads=(sq_b, ones128_b), writes=(psn_b,))
        ctx.op("v", lambda: nc.vector.reduce_max(out=mred[0:1, 0:1], in_=psn[0:1, 0:512],
               axis=AX.X), reads=(psn_b,), writes=(mred_b,))
        if first:
            ctx.op("v", lambda: nc.vector.tensor_copy(out=bmax[0:1, :], in_=mred[0:1, 0:1]),
                   reads=(mred_b,), writes=(bmax_b,))
            first = False
        else:
            ctx.op("v", lambda: nc.vector.tensor_max(out=bmax[0:1, :], in0=bmax[0:1, :],
                   in1=mred[0:1, 0:1]), reads=(mred_b, bmax_b), writes=(bmax_b,))


def build_phaseB(ctx, dram):
    nc = ctx.nc
    ph = Phase(ctx)
    init_common(ctx, ph)
    cst = load_consts(ctx, ph, dram, [
        ("bout_fm", F32), ("rwt", BF16), ("causal", F32),
        ("identf", F32),
    ])
    wout, wout_b = ph.sb("wout_sb", [64, 16 * 1024], BF16)
    ctx.op("p", lambda: nc.gpsimd.dma_start(
        out=wout[:].rearrange("r (h m) -> r h m", h=16),
        in_=dram["wout"].rearrange("(h r) m -> r h m", r=64)),
        writes=(wout_b,), sem="DP")
    woutv = wout[:].rearrange("r (h m) -> r h m", h=16)
    bout, bout_b = cst["bout_fm"]
    rwt, rwt_b = cst["rwt"]
    causal, causal_b = cst["causal"]
    identf, identf_b = cst["identf"]
    ones1, ones1_b = ph.bufs["ones1"]

    q0v = dram["q0"].rearrange("p (s h t) -> p s h t", s=SEQ, h=8)
    k0v = dram["k0"].rearrange("p (s h t) -> p s h t", s=SEQ, h=2)
    v0v = dram["v0"].rearrange("p (s u m) -> p s u m", s=SEQ, u=16)
    x0_d = dram["x0"]
    x1_d = dram["x1"]
    xn1_d = dram["xn1"]
    xn1v = xn1_d
    gates_d = dram["gates"]
    x1_db, xn1_db, gates_db = Buf(), Buf(), Buf()

    q_sb, q_b = ph.sb("q_sb", [64, 16 * 512], BF16)
    k_sb, k_b = ph.sb("k_sb", [64, 4 * 2048], BF16)
    v_sb, v_b = ph.sb("v_sb", [128, 16 * 260], BF16)
    pt2 = [ph.sb(f"pt_sb{i}", [128, 16 * 512], BF16) for i in range(2)]
    attn, attn_b = ph.sb("attn", [64, 16 * 512], BF16)
    sqt, sqt_b = ph.sb("sqt", [128, 512], BF16)
    mred, mred_b = ph.sb("mred", [1, 8], F32)
    bq, bq_b = ph.sb("bq", [1, 1], F32)
    bk, bk_b = ph.sb("bk", [1, 1], F32)
    negb, negb_b = ph.sb("negb", [128, 1], F32)
    rden, rden_b = ph.sb("rden", [1, 512], F32)
    rb, rb_b = ph.sb("rb", [64, 512], BF16)
    x0t = [ph.sb(f"x0t{i}", [128, CH * NT], F32) for i in range(2)]
    x1t, x1t_b = ph.sb("x1t", [128, CH * NT], F32)
    glog, glog_b = ph.sb("glog", [128, 8], F32)
    gm, gm_b = ph.sb("gm", [128, 4], F32)
    gmask, gmask_b = ph.sb("gmask", [128, 8], F32)
    gtm, gtm_b = ph.sb("gtm", [128, 8], F32)
    gexp, gexp_b = ph.sb("gexp", [128, 8], F32)
    grow, grow_b = ph.sb("grow", [8, 512], F32)

    ps_st = [ph.ps(f"ps_st{i}", [128, 1024], F32) for i in range(2)]
    ps_av = [ph.ps(f"ps_av{i}", [128, 512], F32) for i in range(1)]
    ps_r, ps_r_b = ph.ps("ps_r", [128, 512], F32)
    ps_w, ps_w_b = ph.ps("ps_w", [128, 1024], F32)
    ps_A, ps_A_b = ps_av[0]
    ps_B, ps_B_b = ps_r, ps_r_b

    qv = q_sb[:].rearrange("p (h t) -> p h t", h=16)
    kv = k_sb[:].rearrange("p (h t) -> p h t", h=4)
    vv = v_sb[:].rearrange("p (u g m) -> p u g m", u=16, g=4)
    attnv = attn[:].rearrange("p (h n) -> p h n", h=16)

    for s in range(SEQ):
        for par in range(2):
            ctx.op("q", lambda par=par, s=s: nc.sync.dma_start(
                out=kv[:, par::2, :], in_=k0v[par * 64:par * 64 + 64, s, :, :]),
                writes=(k_b,), sem="DQ")
        vtmp, vtmp_b = pt2[0]
        ctx.op("q", lambda s=s, vtmp=vtmp: nc.sync.dma_start(
            out=vtmp[:, 0:4096].rearrange("p (u m) -> p u m", u=16),
            in_=v0v[:, s]), writes=(vtmp_b,), sem="DQ")
        ctx.op("v", lambda vtmp=vtmp: nc.vector.tensor_copy(
            out=vv[:, :, :, 0:64],
            in_=vtmp[:, 0:4096].rearrange("p (u g m) -> p u g m", u=16, g=4)),
            reads=(vtmp_b,), writes=(v_b,))
        ctx.op("v", lambda: nc.vector.memset(vv[:, :, :, 64:65], 1.0), writes=(v_b,))
        emit_bound_cheap(ctx, ph, nc, k_sb, k_b, 4 * 2048, bk, bk_b,
                         [sqt, sqt], [sqt_b, sqt_b], ps_r, ps_r_b)

        for qtg in range(4):
            nt = s * 4 + qtg
            for par in range(2):
                ctx.op("q", lambda par=par, s=s, qtg=qtg: nc.sync.dma_start(
                    out=qv[:, par::2, :],
                    in_=q0v[par * 64:par * 64 + 64, s, :, qtg * 512:(qtg + 1) * 512]),
                    writes=(q_b,), sem="DQ")
            emit_bound_cheap(ctx, ph, nc, q_sb, q_b, 16 * 512, bq, bq_b,
                             [sqt, sqt], [sqt_b, sqt_b], ps_r, ps_r_b)
            ctx.op("v", lambda: nc.vector.tensor_mul(out=bq[0:1, :], in0=bq[0:1, :],
                   in1=bk[0:1, :]), reads=(bk_b, bq_b), writes=(bq_b,))
            ctx.op("s", lambda: nc.scalar.activation(out=mred[0:1, 4:5], in_=bq[0:1, :],
                   func=AF.Sqrt, scale=1.0), reads=(bq_b,), writes=(mred_b,))
            ctx.op("v", lambda: nc.vector.tensor_scalar(out=mred[0:1, 4:5],
                   in0=mred[0:1, 4:5], scalar1=60.0, scalar2=None,
                   op0=ALU.min), reads=(mred_b,), writes=(mred_b,))
            ctx.op("t", lambda: nc.tensor.matmul(out=ps_r[:, 0:1], lhsT=ones1[:],
                   rhs=mred[0:1, 4:5], start=True, stop=True),
                   reads=(mred_b, ones1_b), writes=(ps_r_b,))
            ctx.op("v", lambda: nc.vector.tensor_scalar(out=negb[:], in0=ps_r[:, 0:1],
                   scalar1=-1.0, scalar2=None, op0=ALU.mult),
                   reads=(ps_r_b,), writes=(negb_b,))
            x0_t, x0_b = x0t[qtg % 2]
            x0v3 = x0_t[:].rearrange("p (c n) -> p c n", c=CH)
            ctx.op("q", lambda x0_t=x0_t, nt=nt: nc.sync.dma_start(
                out=x0_t[:].rearrange("p (c n) -> p c n", c=CH),
                in_=x0_d[:, nt]), writes=(x0_b,), sem="DQ")
            def st_chain_B(g, qt, sub):
                pt_sb, pt_b = pt2[(qt * 4 + g) % 2]
                ptv = pt_sb[:].rearrange("p (u n) -> p u n", u=16)
                rhs = qv[:, 4 * g:4 * g + 4, sub * 128:(sub + 1) * 128]
                nkt = qt + 1
                for ktg in range((nkt + 1) // 2):
                    pst, pst_b = ps_st[ktg % 2]
                    nj = min(2, nkt - 2 * ktg)
                    for j in range(nj):
                        kt = 2 * ktg + j
                        lhsT = kv[:, g, kt * 128:(kt + 1) * 128]
                        ctx.op("t", lambda lhsT=lhsT, rhs=rhs, pst=pst, j=j:
                               nc.tensor.matmul(out=pst[:, j * 512:(j + 1) * 512],
                                                lhsT=lhsT, rhs=rhs,
                                                start=True, stop=True),
                               reads=(q_b, k_b), writes=(pst_b,))
                        if kt == qt:
                            ctx.op("v", lambda pst=pst, j=j: nc.vector.tensor_add(
                                   out=pst[:, j * 512:(j + 1) * 512],
                                   in0=pst[:, j * 512:(j + 1) * 512], in1=causal[:]),
                                   reads=(causal_b,), writes=(pst_b,))
                    ctx.op("s", lambda ktg=ktg, nj=nj, pst=pst, ptv=ptv:
                           nc.scalar.activation(
                               out=ptv[:, 2 * ktg:2 * ktg + nj, :],
                               in_=pst[:, 0:nj * 512], func=AF.Exp, bias=negb[:]),
                           reads=(pst_b, negb_b), writes=(pt_b,))

            def av_norm_B(g, qt, sub):
                pav, pav_b = ps_av[0]
                pt_sb, pt_b = pt2[(qt * 4 + g) % 2]
                ptv = pt_sb[:].rearrange("p (u n) -> p u n", u=16)

                def av(g=g, qt=qt, pav=pav, ptv=ptv):
                    i = None
                    for kt in range(qt + 1):
                        i = nc.tensor.matmul(out=pav[0:65, :],
                                             lhsT=vv[:, kt, g, :],
                                             rhs=ptv[:, kt, :],
                                             start=(kt == 0), stop=(kt == qt))
                    return i
                ctx.op("t", av, reads=(pt_b, v_b), writes=(pav_b,))
                ctx.op("v", lambda pav=pav: nc.vector.reciprocal(
                    out=rden[0:1, :], in_=pav[64:65, :]),
                    reads=(pav_b,), writes=(rden_b,))
                ctx.op("t", lambda: nc.tensor.matmul(out=ps_r[0:64, :],
                       lhsT=ones1[:, 0:64], rhs=rden[0:1, :], start=True,
                       stop=True), reads=(rden_b, ones1_b), writes=(ps_r_b,))
                ctx.op("v", lambda: nc.vector.tensor_copy(out=rb[:],
                       in_=ps_r[0:64, :]), reads=(ps_r_b,), writes=(rb_b,))
                dst = attnv[:, 4 * g:4 * g + 4, sub * 128:(sub + 1) * 128]
                ctx.op("v", lambda pav=pav, dst=dst: nc.vector.tensor_mul(
                    out=dst, in0=pav[0:64, :].rearrange("p (a n) -> p a n", a=4),
                    in1=rb[:].rearrange("p (a n) -> p a n", a=4)),
                    reads=(pav_b, rb_b), writes=(attn_b,))

            prev = None
            for sub in range(4):
                qt = qtg * 4 + sub
                for g in range(4):
                    st_chain_B(g, qt, sub)
                    if prev is not None:
                        av_norm_B(*prev)
                    prev = (g, qt, sub)
            av_norm_B(*prev)
            x1v = x1t[:].rearrange("p (c n) -> p c n", c=CH)
            for oc in range(CH):
                half = oc % 2
                osl = slice(half * 512, half * 512 + 512)

                def wmm(oc=oc, osl=osl):
                    i = None
                    for h in range(16):
                        i = nc.tensor.matmul(
                            out=ps_w[:, osl],
                            lhsT=woutv[:, h, oc * 128:(oc + 1) * 128],
                            rhs=attnv[:, h, :], start=(h == 0), stop=(h == 15))
                    return i
                ctx.op("t", wmm, reads=(attn_b, wout_b), writes=(ps_w_b,))
                ctx.op("v", lambda oc=oc, osl=osl: nc.vector.scalar_tensor_tensor(
                    out=x1v[:, oc, :], in0=ps_w[:, osl], scalar=bout[:, oc:oc + 1],
                    in1=x0v3[:, oc, :], op0=ALU.add, op1=ALU.add),
                    reads=(ps_w_b, bout_b, x0_b), writes=(x1t_b,))
            ctx.op("s", lambda nt=nt: nc.scalar.dma_start(
                out=x1_d[:, nt], in_=x1v),
                reads=(x1t_b,), writes=(x1_db,), sem="DS")
            pt0B, pt0B_b = pt2[0]
            xn, xn_b, _ = emit_ln(ctx, ph, "B", x1v, x1t_b, (),
                                  (ps_w, ps_w_b), (ps_A, ps_A_b), (ps_B, ps_B_b),
                                  scratch=(pt0B[:, 0:CH * NT],
                                           pt0B[:, CH * NT:2 * CH * NT], pt0B_b))
            xnv = xn[:].rearrange("p (c n) -> p c n", c=CH)
            ctx.op("s", lambda nt=nt, xnv=xnv: nc.scalar.dma_start(
                out=xn1v[:, nt], in_=xnv),
                reads=(xn_b,), writes=(xn1_db,), sem="DS")
            for sub in range(4):
                def gmm(sub=sub, xnv=xnv):
                    i = None
                    for c in range(CH):
                        i = nc.tensor.matmul(out=ps_r[:, 0:8],
                                             lhsT=xnv[:, c, sub * 128:(sub + 1) * 128],
                                             rhs=rwt[:, c, :],
                                             start=(c == 0), stop=(c == CH - 1))
                    return i
                ctx.op("t", gmm, reads=(xn_b, rwt_b), writes=(ps_r_b,))
                m1 = gm[:, 0:1]
                negm1 = gm[:, 1:2]
                m2 = gm[:, 2:3]
                rsum = gm[:, 3:4]
                ctx.op("v", lambda: nc.vector.tensor_copy(out=glog[:], in_=ps_r[:, 0:8]),
                       reads=(ps_r_b,), writes=(glog_b,))
                ctx.op("v", lambda: nc.vector.reduce_max(out=m1, in_=glog[:], axis=AX.X),
                       reads=(glog_b,), writes=(gm_b,))
                ctx.op("v", lambda: nc.vector.tensor_scalar(out=gmask[:], in0=glog[:],
                       scalar1=m1, scalar2=None, op0=ALU.is_ge),
                       reads=(glog_b, gm_b), writes=(gmask_b,))
                ctx.op("v", lambda: nc.vector.scalar_tensor_tensor(out=gmask[:],
                       in0=gmask[:], scalar=-1e30, in1=glog[:], op0=ALU.mult,
                       op1=ALU.add), reads=(glog_b, gmask_b), writes=(gmask_b,))
                ctx.op("v", lambda: nc.vector.reduce_max(out=m2, in_=gmask[:], axis=AX.X),
                       reads=(gmask_b,), writes=(gm_b,))
                ctx.op("v", lambda: nc.vector.tensor_scalar(out=gmask[:], in0=glog[:],
                       scalar1=m2, scalar2=None, op0=ALU.is_ge),
                       reads=(glog_b, gm_b), writes=(gmask_b,))
                ctx.op("v", lambda: nc.vector.tensor_scalar(out=negm1, in0=m1,
                       scalar1=-1.0, scalar2=None, op0=ALU.mult),
                       reads=(gm_b,), writes=(gm_b,))
                ctx.op("s", lambda: nc.scalar.activation(out=gexp[:], in_=glog[:],
                       func=AF.Exp, bias=negm1), reads=(glog_b, gm_b), writes=(gexp_b,))
                ctx.op("v", lambda: nc.vector.tensor_mul(out=gexp[:], in0=gexp[:],
                       in1=gmask[:]), reads=(gmask_b, gexp_b), writes=(gexp_b,))
                ctx.op("v", lambda: nc.vector.reduce_sum(out=rsum, in_=gexp[:], axis=AX.X),
                       reads=(gexp_b,), writes=(gm_b,))
                ctx.op("v", lambda: nc.vector.reciprocal(out=rsum, in_=rsum),
                       reads=(gm_b,), writes=(gm_b,))
                ctx.op("v", lambda: nc.vector.tensor_scalar(out=gtm[:], in0=gexp[:],
                       scalar1=rsum, scalar2=None, op0=ALU.mult),
                       reads=(gexp_b, gm_b), writes=(gtm_b,))
                ctx.op("t", lambda: nc.tensor.matmul(out=ps_r[0:8, 8:136], lhsT=gtm[:],
                       rhs=identf[:], start=True, stop=True),
                       reads=(gtm_b, identf_b), writes=(ps_r_b,))
                ctx.op("v", lambda sub=sub: nc.vector.tensor_copy(
                    out=grow[:, sub * 128:(sub + 1) * 128], in_=ps_r[0:8, 8:136]),
                    reads=(ps_r_b,), writes=(grow_b,))
            ctx.op("s", lambda nt=nt: nc.scalar.dma_start(
                out=gates_d[:, nt * NT:(nt + 1) * NT], in_=grow[:]),
                reads=(grow_b,), writes=(gates_db,), sem="DS")
    ph.close()



def build_phaseC(ctx, dram):
    nc = ctx.nc
    ph = Phase(ctx)
    init_common(ctx, ph)
    ones1, ones1_b = ph.bufs["ones1"]
    xn1v = dram["xn1"]
    x1_d = dram["x1"]
    x2_d = dram["x2"]
    acc_d = dram["accC"]
    gates_d = dram["gates"]
    w1T_d = dram["w1T"]
    w2T_d = dram["w2T"]
    x2_db, acc_db = Buf(), Buf()

    w1 = [ph.sb(f"w1_{i}", [128, 8 * 2048], BF16) for i in range(2)]
    w2 = [ph.sb(f"w2_{i}", [128, 16 * 1024], BF16) for i in range(2)]
    xnt = [ph.sb(f"xnt{i}", [128, CH * NT], BF16) for i in range(2)]
    hsc, hsc_b = ph.sb("hsc", [128, 16 * NT], BF16)
    gb, gb_b = ph.sb("gb", [128, NT], BF16)
    grow, grow_b = ph.sb("growC", [1, NT], F32)
    acc, acc_b = ph.sb("accsb", [128, CH * NT], F32)

    ps_h = [ph.ps(f"ps_h{i}", [128, NT], F32) for i in range(2)]
    ps_y = [ph.ps(f"ps_y{i}", [128, NT], F32) for i in range(2)]
    ps_g, ps_g_b = ph.ps("ps_g", [128, NT], F32)

    def load_w(e, slot):
        w1t, w1b = w1[slot]
        w2t, w2b = w2[slot]
        ctx.op("p", lambda e=e, w1t=w1t: nc.gpsimd.dma_start(
            out=w1t[:].rearrange("p (k m) -> p k m", k=8),
            in_=w1T_d[e].rearrange("(k p) m -> p k m", p=128)),
            writes=(w1b,), sem="DP")
        ctx.op("p", lambda e=e, w2t=w2t: nc.gpsimd.dma_start(
            out=w2t[:].rearrange("p (k m) -> p k m", k=16),
            in_=w2T_d[e].rearrange("(k p) m -> p k m", p=128)),
            writes=(w2b,), sem="DP")

    load_w(0, 0)
    for e in range(NE):
        if e + 1 < NE:
            load_w(e + 1, (e + 1) % 2)
        w1t, w1b = w1[e % 2]
        w2t, w2b = w2[e % 2]
        w1v = w1t[:].rearrange("p (k m) -> p k m", k=8)
        w2v = w2t[:].rearrange("p (k m) -> p k m", k=16)
        hv = hsc[:].rearrange("p (k m) -> p k m", k=16)
        accv = acc[:].rearrange("p (c n) -> p c n", c=CH)
        for nt in range(NTILES):
            xt, xt_b = xnt[nt % 2]
            xv = xt[:].rearrange("p (c n) -> p c n", c=CH)
            ctx.op("q", lambda xt=xt, nt=nt: nc.sync.dma_start(
                out=xt[:].rearrange("p (c n) -> p c n", c=CH),
                in_=xn1v[:, nt]), writes=(xt_b,), sem="DQ")
            ctx.op("q", lambda e=e, nt=nt: nc.sync.dma_start(
                out=grow[:], in_=gates_d[e:e + 1, nt * NT:(nt + 1) * NT]),
                writes=(grow_b,), sem="DQ")
            ctx.op("t", lambda: nc.tensor.matmul(out=ps_g[:], lhsT=ones1[:],
                   rhs=grow[0:1, :], start=True, stop=True),
                   reads=(grow_b, ones1_b), writes=(ps_g_b,))
            ctx.op("v", lambda: nc.vector.tensor_copy(out=gb[:], in_=ps_g[:]),
                   reads=(ps_g_b,), writes=(gb_b,))
            for mc in range(16):
                ps, ps_b = ps_h[mc % 2]

                def hmm(mc=mc, ps=ps):
                    i = None
                    for kc in range(CH):
                        i = nc.tensor.matmul(out=ps[:],
                                             lhsT=w1v[:, kc, mc * 128:(mc + 1) * 128],
                                             rhs=xv[:, kc, :],
                                             start=(kc == 0), stop=(kc == CH - 1))
                    return i
                ctx.op("t", hmm, reads=(xt_b, w1b), writes=(ps_b,))
                ctx.op("s", lambda mc=mc, ps=ps: nc.scalar.activation(
                    out=hv[:, mc, :], in_=ps[:], func=AF.Gelu),
                    reads=(ps_b,), writes=(hsc_b,))
                ctx.op("v", lambda mc=mc: nc.vector.tensor_mul(
                    out=hv[:, mc, :], in0=hv[:, mc, :], in1=gb[:]),
                    reads=(gb_b, hsc_b), writes=(hsc_b,))
            src = x1_d if e == 0 else acc_d
            ctx.op("q", lambda src=src, nt=nt: nc.sync.dma_start(
                out=acc[:].rearrange("p (c n) -> p c n", c=CH),
                in_=src[:, nt]), writes=(acc_b,), sem="DQ",
                dep=(acc_db.w,) if e > 0 else ())
            for oc in range(CH):
                ps, ps_b = ps_y[oc % 2]

                def ymm(oc=oc, ps=ps):
                    i = None
                    for kc in range(16):
                        i = nc.tensor.matmul(out=ps[:],
                                             lhsT=w2v[:, kc, oc * 128:(oc + 1) * 128],
                                             rhs=hv[:, kc, :],
                                             start=(kc == 0), stop=(kc == 15))
                    return i
                ctx.op("t", ymm, reads=(hsc_b, w2b), writes=(ps_b,))
                ctx.op("v", lambda oc=oc, ps=ps: nc.vector.tensor_add(
                    out=accv[:, oc, :], in0=ps[:], in1=accv[:, oc, :]),
                    reads=(ps_b, acc_b), writes=(acc_b,))
            dst_d, dst_db = (x2_d, x2_db) if e == NE - 1 else (acc_d, acc_db)
            ctx.op("s", lambda dst_d=dst_d, nt=nt: nc.scalar.dma_start(
                out=dst_d[:, nt],
                in_=acc[:].rearrange("p (c n) -> p c n", c=CH)),
                reads=(acc_b,), writes=(dst_db,), sem="DS")
    ph.close()



def build_phaseD(ctx, dram):
    nc = ctx.nc
    ph = Phase(ctx)
    init_common(ctx, ph)
    cst = load_consts(ctx, ph, dram, [
        ("wq1", BF16), ("bq1", F32), ("wl1", BF16), ("bl1", F32),
        ("wk1", BF16), ("bk1", F32), ("wv1", BF16), ("bv1", F32),
    ])
    wq1, wq1_b = cst["wq1"]
    bq1, bq1_b = cst["bq1"]
    wl1, wl1_b = cst["wl1"]
    bl1, bl1_b = cst["bl1"]
    wk1, wk1_b = cst["wk1"]
    bk1, bk1_b = cst["bk1"]
    wv1, wv1_b = cst["wv1"]
    bv1, bv1_b = cst["bv1"]
    ones1, ones1_b = ph.bufs["ones1"]

    x2_d = dram["x2"]
    q1v = dram["q1"].rearrange("p (s h t) -> p s h t", s=SEQ, h=8)
    k1v = dram["k1"].rearrange("p (s h t) -> p s h t", s=SEQ, h=8)
    v1v = dram["v1"].rearrange("p (s u m) -> p s u m", s=SEQ, u=16)
    q1_db, k1_db, v1_db = Buf(), Buf(), Buf()

    x2t = [ph.sb(f"x2t{i}", [128, CH * NT], F32) for i in range(2)]
    lat, lat_b = ph.sb("lat1", [128, 2 * NT], BF16)
    pout2 = [ph.sb(f"pout{i}", [128, NT], BF16) for i in range(2)]
    vout, vout_b = ph.sb("voutD", [128, 4 * 1024], BF16)

    ps_q = [ph.ps(f"ps_qD{i}", [128, NT], F32) for i in range(2)]
    ps_v, ps_v_b = ph.ps("ps_vD", [128, 1024], F32)
    ps_A, ps_A_b = ph.ps("ps_AD", [128, NT], F32)
    ps_B, ps_B_b = ph.ps("ps_BD", [128, NT], F32)

    latv = lat[:].rearrange("p (k n) -> p k n", k=2)
    voutv = vout[:].rearrange("p (a m) -> p a m", a=4)

    for nt in range(NTILES):
        s, a4 = nt // 4, nt % 4
        x2_t, x2_b = x2t[nt % 2]
        x2v = x2_t[:].rearrange("p (c n) -> p c n", c=CH)
        ctx.op("q", lambda x2_t=x2_t, nt=nt: nc.sync.dma_start(
            out=x2_t[:].rearrange("p (c n) -> p c n", c=CH),
            in_=x2_d[:, nt]), writes=(x2_b,), sem="DQ")
        xn, xn_b, _ = emit_ln(ctx, ph, "D", x2v, x2_b, (),
                              (ps_v, ps_v_b), (ps_A, ps_A_b), (ps_B, ps_B_b))
        xnv = xn[:].rearrange("p (c n) -> p c n", c=CH)
        for hp in range(8):
            ps, ps_b = ps_q[hp % 2]

            def qmm(hp=hp, ps=ps):
                i = None
                for kc in range(CH):
                    i = nc.tensor.matmul(out=ps[:],
                                         lhsT=wq1[:, kc, hp * 128:(hp + 1) * 128],
                                         rhs=xnv[:, kc, :],
                                         start=(kc == 0), stop=(kc == CH - 1))
                return i
            ctx.op("t", qmm, reads=(xn_b, wq1_b), writes=(ps_b,))
            pout, pout_b = pout2[hp % 2]
            ctx.op("s", lambda hp=hp, ps=ps, pout=pout: nc.scalar.activation(
                out=pout[:], in_=ps[:], func=AF.Identity, bias=bq1[:, hp:hp + 1]),
                reads=(ps_b, bq1_b), writes=(pout_b,))
            ctx.op("s", lambda s=s, hp=hp, a4=a4, pout=pout: nc.scalar.dma_start(
                out=q1v[:, s, hp, a4 * NT:(a4 + 1) * NT], in_=pout[:]),
                reads=(pout_b,), writes=(q1_db,), sem="DS")
        for lc in range(2):
            ps, ps_b = ps_q[lc % 2]

            def lmm(lc=lc, ps=ps):
                i = None
                for kc in range(CH):
                    i = nc.tensor.matmul(out=ps[:],
                                         lhsT=wl1[:, kc, lc * 128:(lc + 1) * 128],
                                         rhs=xnv[:, kc, :],
                                         start=(kc == 0), stop=(kc == CH - 1))
                return i
            ctx.op("t", lmm, reads=(xn_b, wl1_b), writes=(ps_b,))
            ctx.op("s", lambda lc=lc, ps=ps: nc.scalar.activation(
                out=latv[:, lc, :], in_=ps[:], func=AF.Identity,
                bias=bl1[:, lc:lc + 1]), reads=(ps_b, bl1_b), writes=(lat_b,))
        for hp in range(8):
            ps, ps_b = ps_q[hp % 2]

            def kmm(hp=hp, ps=ps):
                i = None
                for kc in range(2):
                    i = nc.tensor.matmul(out=ps[:],
                                         lhsT=wk1[:, kc, hp * 128:(hp + 1) * 128],
                                         rhs=latv[:, kc, :],
                                         start=(kc == 0), stop=(kc == 1))
                return i
            ctx.op("t", kmm, reads=(lat_b, wk1_b), writes=(ps_b,))
            pout, pout_b = pout2[hp % 2]
            ctx.op("s", lambda hp=hp, ps=ps, pout=pout: nc.scalar.activation(
                out=pout[:], in_=ps[:], func=AF.Identity, bias=bk1[:, hp:hp + 1]),
                reads=(ps_b, bk1_b), writes=(pout_b,))
            ctx.op("s", lambda s=s, hp=hp, a4=a4, pout=pout: nc.scalar.dma_start(
                out=k1v[:, s, hp, a4 * NT:(a4 + 1) * NT], in_=pout[:]),
                reads=(pout_b,), writes=(k1_db,), sem="DS")
        for sub in range(4):
            def vmm(sub=sub):
                i = None
                for half in range(2):
                    osl = slice(half * 512, half * 512 + 512)
                    for kc in range(2):
                        nc.tensor.matmul(out=ps_v[:, osl],
                                         lhsT=latv[:, kc, sub * 128:(sub + 1) * 128],
                                         rhs=wv1[:, kc, osl],
                                         start=(kc == 0), stop=False)
                    i = nc.tensor.matmul(out=ps_v[:, osl], lhsT=ones1[:],
                                         rhs=bv1[0:1, osl], start=False, stop=True)
                return i
            ctx.op("t", vmm, reads=(lat_b, wv1_b, bv1_b, ones1_b), writes=(ps_v_b,))
            ctx.op("v", lambda sub=sub: nc.vector.tensor_copy(out=voutv[:, sub, :],
                   in_=ps_v[:]), reads=(ps_v_b,), writes=(vout_b,))
        ctx.op("s", lambda s=s, a4=a4: nc.scalar.dma_start(
            out=v1v[:, s, a4 * 4:(a4 + 1) * 4, :], in_=voutv),
            reads=(vout_b,), writes=(v1_db,), sem="DS")
    ph.close()



def emit_bound_cheap(ctx, ph, nc, src, src_b, ncols, bmax, bmax_b, sq2, sq2_bs,
                     psn, psn_b, nrows=64):
    ones128_bf, ones128_b = ph.bufs["ones128_bf"]
    nchunks = ncols // 512
    for ci in range(nchunks):
        sq, sq_b = sq2[ci % 2], sq2_bs[ci % 2]
        sl = slice(ci * 512, (ci + 1) * 512)
        ctx.op("p", lambda sl=sl, sq=sq: nc.gpsimd.tensor_mul(out=sq[0:nrows, :],
               in0=src[0:nrows, sl], in1=src[0:nrows, sl]),
               reads=(src_b,), writes=(sq_b,))
        ctx.op("t", lambda sq=sq, ci=ci: nc.tensor.matmul(out=psn[0:1, 0:512],
               lhsT=ones128_bf[0:nrows, :], rhs=sq[0:nrows, :], start=(ci == 0),
               stop=(ci == nchunks - 1)), reads=(sq_b, ones128_b), writes=(psn_b,))
    ctx.op("v", lambda: nc.vector.reduce_max(out=bmax[0:1, 0:1], in_=psn[0:1, 0:512],
           axis=AX.X), reads=(psn_b,), writes=(bmax_b,))


def build_phaseE(ctx, dram):
    import os
    E_SEQS = int(os.environ.get("E_SEQS", SEQ))
    E_QTGS = int(os.environ.get("E_QTGS", 4))
    E_HH = int(os.environ.get("E_HH", 2))
    E_HEADS = int(os.environ.get("E_HEADS", 8))
    E_LN = int(os.environ.get("E_LN", 1))
    nc = ctx.nc
    ph = Phase(ctx)
    init_common(ctx, ph)
    cst = load_consts(ctx, ph, dram, [("bo1", F32)])
    bo1, bo1_b = cst["bo1"]
    ones1, ones1_b = ph.bufs["ones1"]
    wo, wo_b = ph.sb("wo_sb", [64, 16 * 1024], BF16)
    ctx.op("p", lambda: nc.gpsimd.dma_start(
        out=wo[:].rearrange("r (h m) -> r h m", h=16),
        in_=dram["wo1"].rearrange("(h r) m -> r h m", r=64)),
        writes=(wo_b,), sem="DP")
    wov = wo[:].rearrange("r (h m) -> r h m", h=16)

    q1v = dram["q1"].rearrange("p (s h t) -> p s h t", s=SEQ, h=8)
    k1v = dram["k1"].rearrange("p (s h t) -> p s h t", s=SEQ, h=8)
    v1v = dram["v1"].rearrange("p (s u m) -> p s u m", s=SEQ, u=16)
    x2_d = dram["x2"]
    x3_d = dram["x3"]
    xn3_d = dram["xn3"]
    xn3v = xn3_d
    x3_db, xn3_db = Buf(), Buf()

    q_sb, q_b = ph.sb("q_sbE", [64, 16 * 512], BF16)
    k2 = [ph.sb(f"k_sbE{i}", [64, 8 * 2048], BF16) for i in range(2)]
    v_sb, v_b = ph.sb("v_sbE", [128, 16 * 8 * 65], BF16)
    pt2 = [ph.sb(f"pt_sbE{i}", [128, 16 * 512], BF16) for i in range(2)]
    attn, attn_b = ph.sb("attnE", [64, 8 * 512], BF16)
    sq2t = [ph.sb(f"sqtE{i}", [64, 512], BF16) for i in range(2)]
    mred, mred_b = ph.sb("mredE", [1, 8], F32)
    bq, bq_b = ph.sb("bqE", [1, 1], F32)
    bk, bk_b = ph.sb("bkE", [1, 1], F32)
    bkh, bkh_b = ph.sb("bkhE", [1, 1], F32)
    negb, negb_b = ph.sb("negbE", [128, 1], F32)
    _rden = ph.sb("rdenE0", [1, 512], F32)
    rden2 = [_rden, _rden]
    _rb = ph.sb("rbE0", [64, 512], BF16)
    rb2 = [_rb, _rb]
    x2t, x2t_b = ph.sb("x2tE", [128, CH * NT], F32)
    x3t, x3t_b = ph.sb("x3tE", [128, CH * NT], F32)

    ps_st = [ph.ps(f"ps_stE{i}", [128, 1024], F32) for i in range(2)]
    ps_av = [ph.ps(f"ps_avE{i}", [128, 512], F32) for i in range(1)]
    ps_r, ps_r_b = ph.ps("ps_rE", [128, 512], F32)
    ps_w, ps_w_b = ph.ps("ps_wE", [128, 1024], F32)
    ps_B, ps_B_b = ps_r, ps_r_b
    ps_A, ps_A_b = ps_av[0]
    sq2 = [t for t, b in sq2t]
    sq2_bs = [b for t, b in sq2t]

    qv = q_sb[:].rearrange("p (h t) -> p h t", h=16)
    vv = v_sb[:].rearrange("p (u h m) -> p u h m", u=16, h=8)
    attnv = attn[:].rearrange("p (h n) -> p h n", h=8)

    def load_khalf(s, hh, slot):
        k_sb, k_b = k2[slot]
        kvvx = k_sb[:].rearrange("p (h t) -> p h t", h=8)
        for par in range(2):
            ctx.op("q", lambda par=par, s=s, hh=hh, kvvx=kvvx: nc.sync.dma_start(
                out=kvvx[:, par::2, :],
                in_=k1v[par * 64:par * 64 + 64, s, 4 * hh:4 * hh + 4, :]),
                writes=(k_b,), sem="DQ")

    for s in range(E_SEQS):
        for hh in range(2):
            load_khalf(s, hh, hh % 2)
            k_sb, k_b = k2[hh % 2]
            emit_bound_cheap(ctx, ph, nc, k_sb, k_b, 8 * 2048, bkh, bkh_b, sq2,
                             sq2_bs, ps_r, ps_r_b)
            if hh == 0:
                ctx.op("v", lambda: nc.vector.tensor_copy(out=bk[0:1, :],
                       in_=bkh[0:1, :]), reads=(bkh_b,), writes=(bk_b,))
            else:
                ctx.op("v", lambda: nc.vector.tensor_max(out=bk[0:1, :],
                       in0=bk[0:1, :], in1=bkh[0:1, :]),
                       reads=(bkh_b, bk_b), writes=(bk_b,))
        for qtg in range(E_QTGS):
            nt = s * 4 + qtg
            for par in range(2):
                ctx.op("q", lambda par=par, s=s, qtg=qtg: nc.sync.dma_start(
                    out=qv[:, par::2, :],
                    in_=q1v[par * 64:par * 64 + 64, s, :, qtg * 512:(qtg + 1) * 512]),
                    writes=(q_b,), sem="DQ")
            emit_bound_cheap(ctx, ph, nc, q_sb, q_b, 16 * 512, bq, bq_b, sq2,
                             sq2_bs, ps_r, ps_r_b)
            ctx.op("q", lambda nt=nt: nc.sync.dma_start(
                out=x2t[:].rearrange("p (c n) -> p c n", c=CH),
                in_=x2_d[:, nt]), writes=(x2t_b,), sem="DQ")
            ctx.op("v", lambda: nc.vector.tensor_mul(out=bq[0:1, :], in0=bq[0:1, :],
                   in1=bk[0:1, :]), reads=(bk_b, bq_b), writes=(bq_b,))
            ctx.op("s", lambda: nc.scalar.activation(out=mred[0:1, 4:5],
                   in_=bq[0:1, :], func=AF.Sqrt), reads=(bq_b,), writes=(mred_b,))
            ctx.op("v", lambda: nc.vector.tensor_scalar(out=mred[0:1, 4:5],
                   in0=mred[0:1, 4:5], scalar1=60.0 / SCALE0, scalar2=None,
                   op0=ALU.min), reads=(mred_b,), writes=(mred_b,))
            ctx.op("t", lambda: nc.tensor.matmul(out=ps_r[:, 0:1], lhsT=ones1[:],
                   rhs=mred[0:1, 4:5], start=True, stop=True),
                   reads=(mred_b, ones1_b), writes=(ps_r_b,))
            ctx.op("v", lambda: nc.vector.tensor_scalar(out=negb[:],
                   in0=ps_r[:, 0:1], scalar1=-SCALE0, scalar2=None, op0=ALU.mult),
                   reads=(ps_r_b,), writes=(negb_b,))
            x3v = x3t[:].rearrange("p (c n) -> p c n", c=CH)
            x2v = x2t[:].rearrange("p (c n) -> p c n", c=CH)
            for hh in range(E_HH):
                it = qtg * 2 + hh
                if it == 0:
                    load_khalf(s, 0, 0)
                if it + 1 < E_QTGS * 2:
                    load_khalf(s, (it + 1) % 2, (it + 1) % 2)
                k_sb, k_b = k2[it % 2]
                kvv = k_sb[:].rearrange("p (h t) -> p h t", h=8)
                vtmp, vtmp_b = pt2[1]
                ctx.op("q", lambda s=s, hh=hh, vtmp=vtmp: nc.sync.dma_start(
                    out=vtmp[:, 0:8192].rearrange("p (u m) -> p u m", u=16),
                    in_=v1v[:, s, :, 512 * hh:512 * hh + 512]),
                    writes=(vtmp_b,), sem="DQ")
                ctx.op("v", lambda vtmp=vtmp: nc.vector.tensor_copy(
                    out=vv[:, :, :, 0:64],
                    in_=vtmp[:, 0:8192].rearrange("p (u h m) -> p u h m", u=16, h=8)),
                    reads=(vtmp_b,), writes=(v_b,))
                ctx.op("v", lambda: nc.vector.memset(vv[:, :, :, 64:65], 1.0),
                       writes=(v_b,))

                def st_burst(hsub, ktg):
                    h = hh * 8 + hsub
                    pst, pst_b = ps_st[ktg % 2]
                    pt_sb, pt_b = pt2[hsub % 2]
                    ptv = pt_sb[:].rearrange("p (u n) -> p u n", u=16)
                    rhs = qv[:, h, :]
                    for j in range(2):
                        kt = 2 * ktg + j
                        lhsT = kvv[:, hsub, kt * 128:(kt + 1) * 128]
                        ctx.op("t", lambda lhsT=lhsT, rhs=rhs, j=j, pst=pst:
                               nc.tensor.matmul(out=pst[:, j * 512:(j + 1) * 512],
                                                lhsT=lhsT, rhs=rhs,
                                                start=True, stop=True),
                               reads=(q_b, k_b), writes=(pst_b,))
                    ctx.op("s", lambda ktg=ktg, ptv=ptv, pst=pst:
                           nc.scalar.activation(
                               out=ptv[:, 2 * ktg:2 * ktg + 2, :], in_=pst[:],
                               func=AF.Exp, bias=negb[:], scale=SCALE0),
                           reads=(pst_b, negb_b), writes=(pt_b,))

                def av_part(hsub, ktg):
                    pav, pav_b = ps_av[0]
                    pt_sb, pt_b = pt2[hsub % 2]
                    ptv = pt_sb[:].rearrange("p (u n) -> p u n", u=16)

                    def av(hsub=hsub, ktg=ktg, pav=pav, ptv=ptv):
                        i = None
                        for j in range(2):
                            kt = 2 * ktg + j
                            i = nc.tensor.matmul(out=pav[0:65, :],
                                                 lhsT=vv[:, kt, hsub, :],
                                                 rhs=ptv[:, kt, :],
                                                 start=(kt == 0), stop=(kt == 15))
                        return i
                    ctx.op("t", av, reads=(pt_b, v_b), writes=(ps_av[0][1],))

                def norm_head(hsub):
                    pav, pav_b = ps_av[0]
                    rden, rden_b = rden2[hsub % 2]
                    rb, rb_b = rb2[hsub % 2]
                    ctx.op("v", lambda pav=pav, rden=rden: nc.vector.reciprocal(
                        out=rden[0:1, :], in_=pav[64:65, :]),
                        reads=(pav_b,), writes=(rden_b,))
                    ctx.op("t", lambda rden=rden: nc.tensor.matmul(out=ps_r[0:64, :],
                           lhsT=ones1[:, 0:64], rhs=rden[0:1, :], start=True,
                           stop=True), reads=(rden_b, ones1_b), writes=(ps_r_b,))
                    ctx.op("v", lambda rb=rb: nc.vector.tensor_copy(out=rb[:],
                           in_=ps_r[0:64, :]), reads=(ps_r_b,), writes=(rb_b,))
                    ctx.op("v", lambda pav=pav, hsub=hsub, rb=rb:
                           nc.vector.tensor_mul(
                               out=attnv[:, hsub, :], in0=pav[0:64, :], in1=rb[:]),
                           reads=(pav_b, rb_b), writes=(attn_b,))

                for hsub in range(E_HEADS):
                    for ktgc in range(2):
                        for ktg in range(4 * ktgc, 4 * ktgc + 4):
                            st_burst(hsub, ktg)
                        if hsub > 0:
                            for ktg in range(4 * ktgc, 4 * ktgc + 4):
                                av_part(hsub - 1, ktg)
                    if hsub > 0:
                        norm_head(hsub - 1)
                for ktg in range(8):
                    av_part(E_HEADS - 1, ktg)
                norm_head(E_HEADS - 1)
                for oc in range(CH):
                    half = oc % 2
                    osl = slice(half * 512, half * 512 + 512)

                    def wmm(oc=oc, osl=osl, hh=hh):
                        i = None
                        for hsub in range(8):
                            i = nc.tensor.matmul(
                                out=ps_w[:, osl],
                                lhsT=wov[:, hh * 8 + hsub, oc * 128:(oc + 1) * 128],
                                rhs=attnv[:, hsub, :], start=(hsub == 0),
                                stop=(hsub == 7))
                        return i
                    ctx.op("t", wmm, reads=(attn_b, wo_b), writes=(ps_w_b,))
                    if hh == 0:
                        ctx.op("v", lambda oc=oc, osl=osl:
                               nc.vector.scalar_tensor_tensor(
                                   out=x3v[:, oc, :], in0=ps_w[:, osl],
                                   scalar=bo1[:, oc:oc + 1], in1=x2v[:, oc, :],
                                   op0=ALU.add, op1=ALU.add),
                               reads=(ps_w_b, bo1_b, x2t_b), writes=(x3t_b,))
                    else:
                        ctx.op("v", lambda oc=oc, osl=osl: nc.vector.tensor_add(
                               out=x3v[:, oc, :], in0=ps_w[:, osl],
                               in1=x3v[:, oc, :]),
                               reads=(ps_w_b,), writes=(x3t_b,))
            ctx.op("s", lambda nt=nt: nc.scalar.dma_start(
                out=x3_d[:, nt], in_=x3v),
                reads=(x3t_b,), writes=(x3_db,), sem="DS")
            if not E_LN:
                continue
            pt0, pt0_b = pt2[0]
            xn, xn_b, _ = emit_ln(ctx, ph, "E", x3v, x3t_b, (),
                                  (ps_w, ps_w_b), (ps_A, ps_A_b), (ps_B, ps_B_b),
                                  scratch=(pt0[:, 0:CH * NT],
                                           pt0[:, CH * NT:2 * CH * NT], pt0_b),
                                  stat_scratch=(x2t[:], x2t_b),
                                  xn_scratch=(v_sb[:, 0:CH * NT], v_b))
            xnv = xn[:].rearrange("p (c n) -> p c n", c=CH)
            ctx.op("s", lambda nt=nt, xnv=xnv: nc.scalar.dma_start(
                out=xn3v[:, nt], in_=xnv),
                reads=(xn_b,), writes=(xn3_db,), sem="DS")
    ph.close()



def _fold_butterfly(ctx, ph, nc, dram, which, ebuf, ebuf_b, ncols, chunk_bufs=None):
    dsel, dsel_b = ph.sb(which + "_dsel", [128, 5 * 32], F32)
    doth, doth_b = ph.sb(which + "_doth", [128, 5 * 32], F32)
    ctx.op("q", lambda: nc.sync.dma_start(out=dsel[:], in_=dram[which + "dself"]),
           writes=(dsel_b,), sem="DQ")
    ctx.op("q", lambda: nc.sync.dma_start(out=doth[:], in_=dram[which + "doth"]),
           writes=(doth_b,), sem="DQ")
    smat = [ph.sb(f"{which}_sm{i}", [128, 32 * 128], BF16) for i in range(2)]
    ps_f, ps_f_b = ph.ps("ps_fold" + which, [128, 1056], F32)
    ps_f2, ps_f2_b = ph.ps("ps_fold2" + which, [128, 1056], F32)
    tmpA, tmpA_b = ph.sb(which + "_tmpA", [128, 1056], BF16)
    tmpB, tmpB_b = ph.sb(which + "_tmpB", [128, 1056], BF16)
    ev = ebuf[:].rearrange("p (c n) -> p c n", c=32)
    chunk_d = dram[which + "chunk"]

    def load_sm(s_, slot):
        t, b = smat[slot]
        ctx.op("p", lambda s_=s_, t=t: nc.gpsimd.dma_start(
            out=t[:].rearrange("p (c m) -> p c m", c=32), in_=chunk_d[s_]),
            writes=(b,), sem="DP")

    cb = chunk_bufs if chunk_bufs is not None else [ebuf_b] * 32
    stages = list(range(12)) if which == "up" else list(range(11, -1, -1))
    chunk_stages = [s_ for s_ in stages if s_ <= 6]
    diag_stages = [s_ for s_ in stages if s_ >= 7]
    order = stages
    ci = 0
    load_sm(chunk_stages[0], 0)
    for s_ in order:
        if s_ <= 6:
            t, b = smat[ci % 2]
            if ci + 1 < len(chunk_stages):
                load_sm(chunk_stages[ci + 1], (ci + 1) % 2)
            tv = t[:].rearrange("p (c m) -> p c m", c=32)
            for c in range(32):
                psx, psx_b = (ps_f, ps_f_b) if c % 2 == 0 else (ps_f2, ps_f2_b)

                def fmm(c=c, tv=tv, psx=psx):
                    i = None
                    sls = [slice(0, 512), slice(512, 1024)]
                    if ncols > 1024:
                        sls.append(slice(1024, ncols))
                    for sl in sls:
                        i = nc.tensor.matmul(out=psx[:, sl], lhsT=tv[:, c, :],
                                             rhs=ev[:, c, sl], start=True, stop=True)
                    return i
                ctx.op("t", fmm, reads=(cb[c], b), writes=(psx_b,))
                ctx.op("v", lambda c=c, psx=psx: nc.vector.tensor_copy(
                    out=ev[:, c, :], in_=psx[:, 0:ncols]),
                    reads=(psx_b,), writes=(cb[c],))
            ci += 1
        else:
            si = s_ - 7
            bit = (1 << s_) // 128
            for c in range(32):
                if c & bit:
                    continue
                c2 = c | bit
                ctx.op("v", lambda c=c, c2=c2, si=si: nc.vector.tensor_scalar(
                    out=tmpA[:, 0:ncols], in0=ev[:, c, :],
                    scalar1=dsel[:, si * 32 + c:si * 32 + c + 1], scalar2=None,
                    op0=ALU.mult), reads=(cb[c], dsel_b), writes=(tmpA_b,))
                ctx.op("v", lambda c=c, c2=c2, si=si: nc.vector.scalar_tensor_tensor(
                    out=tmpA[:, 0:ncols], in0=ev[:, c2, :],
                    scalar=doth[:, si * 32 + c:si * 32 + c + 1], in1=tmpA[:, 0:ncols],
                    op0=ALU.mult, op1=ALU.add),
                    reads=(cb[c2], doth_b, tmpA_b), writes=(tmpA_b,))
                ctx.op("v", lambda c=c, c2=c2, si=si: nc.vector.tensor_scalar(
                    out=tmpB[:, 0:ncols], in0=ev[:, c2, :],
                    scalar1=dsel[:, si * 32 + c2:si * 32 + c2 + 1], scalar2=None,
                    op0=ALU.mult), reads=(cb[c2], dsel_b), writes=(tmpB_b,))
                ctx.op("v", lambda c=c, c2=c2, si=si: nc.vector.scalar_tensor_tensor(
                    out=tmpB[:, 0:ncols], in0=ev[:, c, :],
                    scalar=doth[:, si * 32 + c2:si * 32 + c2 + 1], in1=tmpB[:, 0:ncols],
                    op0=ALU.mult, op1=ALU.add),
                    reads=(cb[c], doth_b, tmpB_b), writes=(tmpB_b,))
                ctx.op("v", lambda c=c: nc.vector.tensor_copy(
                    out=ev[:, c, :], in_=tmpA[:, 0:ncols]),
                    reads=(tmpA_b,), writes=(cb[c],))
                ctx.op("v", lambda c2=c2: nc.vector.tensor_copy(
                    out=ev[:, c2, :], in_=tmpB[:, 0:ncols]),
                    reads=(tmpB_b,), writes=(cb[c2],))


def build_phaseF(ctx, dram):
    nc = ctx.nc
    ph = Phase(ctx)
    init_common(ctx, ph)
    x3_d = dram["x3"]
    xn3v = dram["xn3"]
    out_d = dram["out"]
    outf = out_d.rearrange("s t f -> (s t) f")
    out_db = Buf()

    ubf, ubf_b = ph.sb("upbias_sb", [128, 32], F32)
    bsum, bsum_b = ph.sb("bsum", [128, 32], F32)
    ctx.op("q", lambda: nc.sync.dma_start(out=ubf[:], in_=dram["upbias"]),
           writes=(ubf_b,), sem="DQ")
    phF = Phase(ctx)
    ebuf, ebuf_b = phF.sb("ebuf", [128, 32 * 1025], BF16)
    ecb = [Buf() for _ in range(32)]
    ebuf_b = BufView(*ecb)
    ctx.op("p", lambda: nc.gpsimd.dma_start(
        out=ebuf[:].rearrange("p (c n) -> p c n", c=32), in_=dram["e0"]),
        writes=(ebuf_b,), sem="DP")
    _fold_butterfly(ctx, phF, nc, dram, "up", ebuf, ebuf_b, 1025, chunk_bufs=ecb)
    ev = ebuf[:].rearrange("p (c n) -> p c n", c=32)
    ctx.op("v", lambda: nc.vector.tensor_add(out=bsum[:], in0=ev[:, :, 1024],
           in1=ubf[:]), reads=(ebuf_b, ubf_b), writes=(bsum_b,))
    escr = dram["escr"]
    escr_db = Buf()
    ctx.op("s", lambda: nc.scalar.dma_start(
        out=escr.rearrange("(c p) n -> p c n", p=128), in_=ev[:, :, 0:1024]),
        reads=(ebuf_b,), writes=(escr_db,), sem="DS")
    phF.close()

    we, we_b = ph.sb("we", [128, 32 * 8 * 128], BF16)
    wev = we[:].rearrange("p (c k m) -> p c k m", c=32, k=8)
    for c in range(32):
        for kc in range(8):
            ctx.op("q", lambda c=c, kc=kc: nc.sync.dma_start(
                out=wev[:, c, kc, :],
                in_=escr[c * 128:(c + 1) * 128, kc * 128:(kc + 1) * 128],
                transpose=True), reads=(escr_db,), writes=(we_b,), sem="DQ")

    gbuf, gbuf_b0 = ph.sb("gbuf", [128, 32 * 1024], BF16)
    gcb = [Buf() for _ in range(32)]
    gbuf_b = BufView(*gcb)
    phG = Phase(ctx)
    ctx.op("p", lambda: nc.gpsimd.dma_start(
        out=gbuf[:].rearrange("p (c n) -> p c n", c=32), in_=dram["g0"]),
        writes=(gbuf_b,), sem="DP")
    _fold_butterfly(ctx, phG, nc, dram, "dn", gbuf, gbuf_b, 1024, chunk_bufs=gcb)
    gv = gbuf[:].rearrange("p (c n) -> p c n", c=32)
    phG.close()

    bs2, bs2_b = ph.sb("bsum2", [128, CH], F32)
    bcf, bcf_b = ph.sb("bconf", [128, CH], F32)
    dbf, dbf_b = ph.sb("dbf", [128, CH], F32)
    ctx.op("q", lambda: nc.sync.dma_start(out=bcf[:], in_=dram["bcon"]),
           writes=(bcf_b,), sem="DQ")
    ctx.op("q", lambda: nc.sync.dma_start(out=dbf[:], in_=dram["dbias"]),
           writes=(dbf_b,), sem="DQ")
    ctx.op("v", lambda: nc.vector.tensor_add(out=bs2[:], in0=bcf[:], in1=dbf[:]),
           reads=(bcf_b, dbf_b), writes=(bs2_b,))

    cst = load_consts(ctx, ph, dram, [("identfF", F32)])
    identf, identf_b = cst["identfF"]

    xnt, xnt_b = ph.sb("xntF", [128, CH * NT], BF16)
    x3t, x3t_b = ph.sb("x3tF", [128, CH * NT], F32)
    hf, hf_b = ph.sb("hfF", [128, 32 * NT], BF16)
    x4, x4_b = ph.sb("x4F", [128, CH * NT], F32)
    outt, outt_b = ph.sb("outtF", [128, 1024], F32)
    ps_h = [ph.ps(f"ps_hF{i}", [128, NT], F32) for i in range(2)]
    ps_y = [ph.ps(f"ps_yF{i}", [128, NT], F32) for i in range(2)]
    ps_t, ps_t_b = ph.ps("ps_tF", [128, 1024], F32)
    xnv = xnt[:].rearrange("p (c n) -> p c n", c=CH)
    x3v = x3t[:].rearrange("p (c n) -> p c n", c=CH)
    hv = hf[:].rearrange("p (c n) -> p c n", c=32)
    x4v = x4[:].rearrange("p (c n) -> p c n", c=CH)

    for nt in range(NTILES):
        ctx.op("q", lambda nt=nt: nc.sync.dma_start(
            out=xnt[:].rearrange("p (c n) -> p c n", c=CH),
            in_=xn3v[:, nt]), writes=(xnt_b,), sem="DQ")
        ctx.op("q", lambda nt=nt: nc.sync.dma_start(
            out=x3t[:].rearrange("p (c n) -> p c n", c=CH),
            in_=x3_d[:, nt]), writes=(x3t_b,), sem="DQ")
        for c in range(32):
            ps, ps_b = ps_h[c % 2]

            def hmm(c=c, ps=ps):
                i = None
                for kc in range(CH):
                    i = nc.tensor.matmul(out=ps[:], lhsT=wev[:, c, kc, :],
                                         rhs=xnv[:, kc, :],
                                         start=(kc == 0), stop=(kc == CH - 1))
                return i
            ctx.op("t", hmm, reads=(xnt_b, we_b), writes=(ps_b,))
            ctx.op("s", lambda c=c, ps=ps: nc.scalar.activation(
                out=hv[:, c, :], in_=ps[:], func=AF.Gelu, bias=bsum[:, c:c + 1]),
                reads=(ps_b, bsum_b), writes=(hf_b,))
        for oc in range(CH):
            ps, ps_b = ps_y[oc % 2]

            def ymm(oc=oc, ps=ps):
                i = None
                for kc in range(32):
                    i = nc.tensor.matmul(out=ps[:], lhsT=gv[:, kc, oc * 128:(oc + 1) * 128],
                                         rhs=hv[:, kc, :],
                                         start=(kc == 0), stop=(kc == 31))
                return i
            ctx.op("t", ymm, reads=(hf_b, gbuf_b), writes=(ps_b,))
            ctx.op("v", lambda oc=oc, ps=ps: nc.vector.scalar_tensor_tensor(
                out=x4v[:, oc, :], in0=ps[:], scalar=bs2[:, oc:oc + 1],
                in1=x3v[:, oc, :], op0=ALU.add, op1=ALU.add),
                reads=(ps_b, bs2_b, x3t_b), writes=(x4_b,))
        for a in range(4):
            def trs(a=a):
                i = None
                for c in range(CH):
                    i = nc.tensor.transpose(out=ps_t[:, c * 128:(c + 1) * 128],
                                            in_=x4v[:, c, a * 128:(a + 1) * 128],
                                            identity=identf[:])
                return i
            ctx.op("t", trs, reads=(x4_b, identf_b), writes=(ps_t_b,))
            ctx.op("v", lambda: nc.vector.tensor_copy(
                out=outt[:], in_=ps_t[:]), reads=(ps_t_b,), writes=(outt_b,))
            ctx.op("s", lambda nt=nt, a=a: nc.scalar.dma_start(
                out=outf[nt * NT + a * 128:nt * NT + (a + 1) * 128, :],
                in_=outt[:]), reads=(outt_b,), writes=(out_db,), sem="DS")
    ph.close()



def _mk_builder(phase_fn, ins, outs, internals=()):
    nc = bass.Bass("TRN2", target_bir_lowering=False, debug=False)
    dram = {}
    for name, shape, dt in ins:
        dram[name] = nc.dram_tensor(name, shape, dt, kind="ExternalInput").ap()
    for name, shape, dt in outs:
        dram[name] = nc.dram_tensor(name, shape, dt, kind="ExternalOutput").ap()
    for name, shape, dt in internals:
        dram[name] = nc.dram_tensor(name, shape, dt, kind="Internal").ap()
    es = ExitStack()
    ctx = Ctx(nc, es)
    with es:
        with nc.Block() as block:
            @block.sync
            def _(sync):
                phase_fn(ctx, dram)
    return nc


class _DefaultNone(dict):
    def __missing__(self, k):
        return None


_SPECS = None


def _phase_specs():
    global _SPECS
    if _SPECS is not None:
        return _SPECS
    f32, bf = F32, BF16
    _SPECS = {
        "A": (build_phaseA,
              [("xin", [SEQ, 2048, 1024], f32), ("wqa", [1024, 64], f32),
               ("wqbp", [64, 1024], f32), ("wkva", [1024, 32], f32),
               ("wk0p", [32, 256], f32), ("wv0", [32, 256], f32),
               ("cossq", [128, 2048], f32), ("sinsq", [128, 2048], f32),
               ("cossk", [128, 2048], f32), ("sinsk", [128, 2048], f32),
               ("permbf", [128, 128], f32), ("ident", [128, 128], f32)],
              [("x0", [128, NTILES, CH, NT], f32), ("q0", [128, SEQ * 8 * 2048], bf),
               ("k0", [128, SEQ * 2 * 2048], bf), ("v0", [128, SEQ * 16 * 256], bf)],
              []),
        "B": (build_phaseB,
              [("q0", [128, SEQ * 8 * 2048], bf), ("k0", [128, SEQ * 2 * 2048], bf),
               ("v0", [128, SEQ * 16 * 256], bf), ("x0", [128, NTILES, CH, NT], f32),
               ("wout", [1024, 1024], f32), ("bout_fm", [128, 8], f32),
               ("rwt", [1024, 8], f32), ("causal", [128, 512], f32),
               ("identf", [128, 128], f32)],
              [("x1", [128, NTILES, CH, NT], f32), ("xn1", [128, NTILES, CH, NT], bf),
               ("gates", [8, T], f32)],
              []),
        "C": (build_phaseC,
              [("xn1", [128, NTILES, CH, NT], bf), ("x1", [128, NTILES, CH, NT], f32),
               ("gates", [8, T], f32), ("w1T", [NE, 1024, 2048], f32),
               ("w2T", [NE, 2048, 1024], f32)],
              [("x2", [128, NTILES, CH, NT], f32)],
              [("accC", [128, NTILES, CH, NT], f32)]),
        "D": (build_phaseD,
              [("x2", [128, NTILES, CH, NT], f32), ("wq1", [1024, 1024], f32),
               ("bq1", [128, 8], f32), ("wl1", [1024, 256], f32),
               ("bl1", [128, 2], f32), ("wk1", [256, 1024], f32),
               ("bk1", [128, 8], f32), ("wv1", [256, 1024], f32),
               ("bv1", [1, 1024], f32)],
              [("q1", [128, SEQ * 8 * 2048], bf), ("k1", [128, SEQ * 8 * 2048], bf),
               ("v1", [128, SEQ * 16 * 1024], bf)],
              []),
        "E": (build_phaseE,
              [("q1", [128, SEQ * 8 * 2048], bf), ("k1", [128, SEQ * 8 * 2048], bf),
               ("v1", [128, SEQ * 16 * 1024], bf), ("x2", [128, NTILES, CH, NT], f32),
               ("wo1", [1024, 1024], f32), ("bo1", [128, 8], f32)],
              [("x3", [128, NTILES, CH, NT], f32), ("xn3", [128, NTILES, CH, NT], bf)],
              []),
        "F": (build_phaseF,
              [("x3", [128, NTILES, CH, NT], f32), ("xn3", [128, NTILES, CH, NT], bf),
               ("e0", [128, 32, 1025], f32), ("g0", [128, 32, 1024], f32),
               ("upchunk", [7, 32, 128, 128], f32), ("dnchunk", [7, 32, 128, 128], f32),
               ("updself", [128, 160], f32), ("updoth", [128, 160], f32),
               ("dndself", [128, 160], f32), ("dndoth", [128, 160], f32),
               ("upbias", [128, 32], f32), ("bcon", [128, CH], f32),
               ("dbias", [128, CH], f32), ("identfF", [128, 128], f32)],
              [("out", [SEQ, 2048, 1024], f32)],
              [("escr", [4096, 1024], bf)]),
    }
    return _SPECS


def _phase_inputs(name, hp, core_arrays):
    core_arrays = _DefaultNone(core_arrays)
    if name == "A":
        return {"xin": core_arrays["xin"], "wqa": hp["Wqa"], "wqbp": hp["Wqb_perm"],
                "wkva": hp["Wkva"], "wk0p": hp["Wk0_perm"], "wv0": hp["Wv0"],
                "cossq": hp["COSSq"], "sinsq": hp["SINSq"], "cossk": hp["COSSk"],
                "sinsk": hp["SINSk"], "permbf": hp["PERM"], "ident": hp["IDENT"]}
    if name == "B":
        return {"q0": core_arrays["q0"], "k0": core_arrays["k0"],
                "v0": core_arrays["v0"], "x0": core_arrays["x0"],
                "wout": hp["Wout"], "bout_fm": hp["bout_fm"], "rwt": hp["router_wT"],
                "causal": hp["CAUSAL"], "identf": hp["IDENT"]}
    if name == "C":
        return {"xn1": core_arrays["xn1"], "x1": core_arrays["x1"],
                "gates": core_arrays["gates"], "w1T": hp["w1T"], "w2T": hp["w2T"]}
    if name == "D":
        return {"x2": core_arrays["x2"], "wq1": hp["Wq1"], "bq1": hp["bq1_fm"],
                "wl1": hp["Wl1"], "bl1": hp["bl1_fm"], "wk1": hp["Wk1"],
                "bk1": hp["bk1_fm"], "wv1": hp["Wv1"], "bv1": hp["bv1_row"]}
    if name == "E":
        return {"q1": core_arrays["q1"], "k1": core_arrays["k1"],
                "v1": core_arrays["v1"], "x2": core_arrays["x2"],
                "wo1": hp["Wo1"], "bo1": hp["bo1_fm"]}
    if name == "F":
        return {"x3": core_arrays["x3"], "xn3": core_arrays["xn3"],
                "e0": hp["E0"], "g0": hp["G0"], "upchunk": hp["UP_CHUNK"],
                "dnchunk": hp["DN_CHUNK"], "updself": hp["UP_DSELF_dev"],
                "updoth": hp["UP_DOTH_dev"], "dndself": hp["DN_DSELF_dev"],
                "dndoth": hp["DN_DOTH_dev"], "upbias": hp["up_bias_fm"],
                "bcon": hp["b_con_fm"], "dbias": hp["down_bias_fm"],
                "identfF": hp["IDENT"]}
    raise KeyError(name)


def _mk_fused():
    specs = _phase_specs()
    nc = bass.Bass("TRN2", target_bir_lowering=False, debug=False)
    dram = {}
    ext_in = set()
    stream_names = {"x0", "x1", "x2", "x3", "xn1", "xn3", "gates",
                    "q0", "k0", "v0", "q1", "k1", "v1"}
    for name in "ABCDEF":
        fn, ins, outs, internals = specs[name]
        for nm, shape, dt in ins:
            if nm in dram:
                continue
            if nm in stream_names:
                continue
            dram[nm] = nc.dram_tensor(nm, shape, dt, kind="ExternalInput").ap()
            ext_in.add(nm)
        for nm, shape, dt in outs:
            if nm in dram:
                continue
            kind = "ExternalOutput" if nm == "out" else "Internal"
            dram[nm] = nc.dram_tensor(nm, shape, dt, kind=kind).ap()
        for nm, shape, dt in internals:
            dram[nm] = nc.dram_tensor(nm, shape, dt, kind="Internal").ap()
    es = ExitStack()
    ctx = Ctx(nc, es)
    with es:
        with nc.Block() as block:
            @block.sync
            def _(sync):
                for name in "ABCDEF":
                    specs[name][0](ctx, dram)
    return nc, ext_in


def kernel(**inputs):
    import os
    hp = host_prep({k: np.asarray(v) for k, v in inputs.items()})
    x_full = np.asarray(inputs["L0_attn_x"], dtype=np.float32)
    ncores = NCORES
    core_state = [{"xin": np.ascontiguousarray(x_full[c * SEQ:(c + 1) * SEQ])}
                  for c in range(ncores)]
    specs = _phase_specs()
    if os.environ.get("MK_UNFUSED"):
        for name in "ABCDEF":
            fn, ins, outs, internals = specs[name]
            nc = _mk_builder(fn, ins, outs, internals)
            in_maps = [_phase_inputs(name, hp, core_state[c]) for c in range(ncores)]
            res = run_bass_kernel_spmd(nc, in_maps, core_ids=list(range(ncores)))
            for c in range(ncores):
                core_state[c].update(res.results[c])
        return np.concatenate([core_state[c]["out"] for c in range(ncores)], axis=0)
    nc, ext_in = _mk_fused()
    in_maps = []
    for c in range(ncores):
        m = {}
        for name in "ABCDEF":
            for k, v in _phase_inputs(name, hp, core_state[c]).items():
                if k in ext_in and v is not None:
                    m[k] = v
        in_maps.append(m)
    res = run_bass_kernel_spmd(nc, in_maps, core_ids=list(range(ncores)))
    return np.concatenate([res.results[c]["out"] for c in range(ncores)], axis=0)


# version 08
# speedup vs baseline: 1.5499x; 1.0941x over previous
import numpy as np
import ml_dtypes
import concourse.bass as bass
import concourse.mybir as mybir
from concourse.bass_utils import run_bass_kernel_spmd

F32 = mybir.dt.float32
BF16 = mybir.dt.bfloat16
AF = mybir.ActivationFunctionType
ALU = mybir.AluOpType
AX = mybir.AxisListType

BF = ml_dtypes.bfloat16

NCORES = 8
B_FULL, S, E = 32, 2048, 1024
SEQ = 4
T = SEQ * S
NT = 512
NTILES = T // NT
CH = E // 128
H, KVH, D = 16, 4, 64
QL, KVL = 64, 32
ROPE_BASE = 10000.0
SCALE0 = D ** -0.5
NE, DFF, TOPK = 8, 2048, 2
LAT = 256
DH = 4096
NSTAGE = 12
LN_EPS = 1e-5



def _deint_idx():
    r = np.arange(64)
    return np.where(r < 32, 2 * r, 2 * (r - 32) + 1)


def _rope_tables():
    inv = 1.0 / ROPE_BASE ** (np.arange(0, D, 2, dtype=np.float64) / D)
    t = np.arange(S, dtype=np.float64)
    fr = np.outer(inv, t)
    cos = np.cos(fr).astype(np.float32)
    sin = np.sin(fr).astype(np.float32)
    coss = np.concatenate([cos, cos, cos, cos], 0)
    sins = np.concatenate([-sin, sin, -sin, sin], 0)
    return coss, sins


def _perm_swap32():
    p = np.arange(128)
    swap = (p // 32 % 2 == 0) * (p + 32) + (p // 32 % 2 == 1) * (p - 32)
    m = np.zeros((128, 128), np.float32)
    m[swap, p] = 1.0
    return m


def _bfly_mats(weights, transpose):
    chunk_lhsT = []
    for s in range(7):
        s_ = 1 << s
        B_full = np.zeros((128, 128), np.float32)
        mats = []
        for c in range(32):
            Bm = np.zeros((128, 128), np.float32)
            for i_lo in range(128):
                i = c * 128 + i_lo
                if i & s_:
                    continue
                j = i + s_
                j_lo = j - c * 128
                k = (i // (2 * s_)) * s_ + (i % s_)
                w = weights[s, k]
                if transpose:
                    w = w.T
                Bm[i_lo, i_lo] = w[0, 0]
                Bm[i_lo, j_lo] = w[0, 1]
                Bm[j_lo, i_lo] = w[1, 0]
                Bm[j_lo, j_lo] = w[1, 1]
            mats.append(np.ascontiguousarray(Bm.T))
        chunk_lhsT.append(np.stack(mats))
    diag = []
    for s in range(7, 12):
        s_ = 1 << s
        bit = s_ // 128
        d_self = np.zeros((32, 128), np.float32)
        d_other = np.zeros((32, 128), np.float32)
        for c in range(32):
            for p in range(128):
                i = c * 128 + p
                k = (i // (2 * s_)) * s_ + (i % s_) if not (i & s_) else None
                if i & s_:
                    ip = i - s_
                    k = (ip // (2 * s_)) * s_ + (ip % s_)
                    w = weights[s, k]
                    if transpose:
                        w = w.T
                    d_self[c, p] = w[1, 1]
                    d_other[c, p] = w[1, 0]
                else:
                    k = (i // (2 * s_)) * s_ + (i % s_)
                    w = weights[s, k]
                    if transpose:
                        w = w.T
                    d_self[c, p] = w[0, 0]
                    d_other[c, p] = w[0, 1]
        diag.append((d_self, d_other))
    return chunk_lhsT, diag


def host_prep(inp):
    d = {}
    deint = _deint_idx()
    d["Wqa"] = np.ascontiguousarray(inp["L0_attn_Wqa"])
    Wqb = inp["L0_attn_Wqb"]
    cols = np.empty(1024, np.int64)
    for hp in range(H // 2):
        for half in range(2):
            h = 2 * hp + half
            cols[hp * 128 + half * 64:hp * 128 + half * 64 + 64] = h * 64 + deint
    d["Wqb_perm"] = np.ascontiguousarray(Wqb[:, cols])
    d["Wkva"] = np.ascontiguousarray(inp["L0_attn_Wkva"])
    Wkvb = inp["L0_attn_Wkvb"]
    kcols = np.empty(256, np.int64)
    vcols = np.empty(256, np.int64)
    for kp in range(KVH // 2):
        for half in range(2):
            g = 2 * kp + half
            kcols[kp * 128 + half * 64:kp * 128 + half * 64 + 64] = g * 128 + 2 * deint
    for g in range(KVH):
        vcols[g * 64:(g + 1) * 64] = g * 128 + 2 * np.arange(64) + 1
    d["Wk0_perm"] = np.ascontiguousarray(Wkvb[:, kcols])
    d["Wv0"] = np.ascontiguousarray(Wkvb[:, vcols])
    d["Wout"] = np.ascontiguousarray(inp["L0_attn_Wout"])
    d["bout_fm"] = np.ascontiguousarray(inp["L0_attn_bout"].reshape(CH, 128).T)
    coss, sins = _rope_tables()
    d["COSSq"] = coss * np.float32(SCALE0)
    d["SINSq"] = sins * np.float32(SCALE0)
    d["COSSk"] = coss
    d["SINSk"] = sins
    d["PERM"] = _perm_swap32()
    d["IDENT"] = np.eye(128, dtype=np.float32)
    causal = np.zeros((128, 512), np.float32)
    kk, q = np.meshgrid(np.arange(128), np.arange(128), indexing="ij")
    cm = np.where(kk > q, np.float32(-1e30), np.float32(0.0))
    for j in range(4):
        causal[:, j * 128:(j + 1) * 128] = cm
    d["CAUSAL"] = causal
    d["router_wT"] = np.ascontiguousarray(inp["L0_ffn_router_w"].T)
    d["w1T"] = np.ascontiguousarray(inp["L0_ffn_w1"].transpose(0, 2, 1))
    d["w2T"] = np.ascontiguousarray(inp["L0_ffn_w2"].transpose(0, 2, 1))
    d["Wq1"] = np.ascontiguousarray(inp["L1_attn_Wq"])
    d["bq1_fm"] = np.ascontiguousarray(inp["L1_attn_bq"].reshape(CH, 128).T)
    d["Wl1"] = np.ascontiguousarray(inp["L1_attn_Wl"])
    d["bl1_fm"] = np.ascontiguousarray(inp["L1_attn_bl"].reshape(2, 128).T)
    d["Wk1"] = np.ascontiguousarray(inp["L1_attn_Wk"])
    d["bk1_fm"] = np.ascontiguousarray(inp["L1_attn_bk"].reshape(CH, 128).T)
    d["Wv1"] = np.ascontiguousarray(inp["L1_attn_Wv"])
    d["bv1_row"] = np.ascontiguousarray(inp["L1_attn_bv"].reshape(1, 1024))
    d["Wo1"] = np.ascontiguousarray(inp["L1_attn_Wo"])
    d["bo1_fm"] = np.ascontiguousarray(inp["L1_attn_bo"].reshape(CH, 128).T)
    E0 = np.concatenate([inp["L1_ffn_W_exp"].T,
                         inp["L1_ffn_b_exp"].reshape(DH, 1)], axis=1)
    d["E0"] = np.ascontiguousarray(E0.reshape(32, 128, 1025).transpose(1, 0, 2))
    G0 = inp["L1_ffn_W_con"]
    d["G0"] = np.ascontiguousarray(G0.reshape(32, 128, 1024).transpose(1, 0, 2))
    up_chunk, up_diag = _bfly_mats(inp["L1_ffn_up_weights"], transpose=False)
    dn_chunk, dn_diag = _bfly_mats(inp["L1_ffn_down_weights"], transpose=True)
    d["UP_CHUNK"] = np.stack(up_chunk)
    d["DN_CHUNK"] = np.stack(dn_chunk)
    d["UP_DSELF"] = np.stack([a for a, b in up_diag])
    d["UP_DOTH"] = np.stack([b for a, b in up_diag])
    d["DN_DSELF"] = np.stack([a for a, b in dn_diag])
    d["DN_DOTH"] = np.stack([b for a, b in dn_diag])
    for k in ("UP_DSELF", "UP_DOTH", "DN_DSELF", "DN_DOTH"):
        d[k + "_dev"] = np.ascontiguousarray(
            d[k].transpose(2, 0, 1).reshape(128, 5 * 32))
    d["up_bias_fm"] = np.ascontiguousarray(
        inp["L1_ffn_up_bias"].reshape(32, 128).T)
    d["b_con_fm"] = np.ascontiguousarray(inp["L1_ffn_b_con"].reshape(CH, 128).T)
    d["down_bias_fm"] = np.ascontiguousarray(inp["L1_ffn_down_bias"].reshape(CH, 128).T)
    return d



from contextlib import ExitStack


class Buf:
    __slots__ = ("w", "r")

    def __init__(self):
        self.w = None
        self.r = {}

    def w_toks(self):
        return [self.w] if self.w is not None else []

    def r_toks(self):
        return list(self.r.items())

    def note_read(self, tok):
        if self.r.get(tok[0], 0) < tok[1]:
            self.r[tok[0]] = tok[1]

    def note_write(self, tok):
        self.w = tok
        self.r = {}


class BufView:

    def __init__(self, *bufs):
        self.bufs = bufs

    def w_toks(self):
        return [t for b in self.bufs for t in b.w_toks()]

    def r_toks(self):
        return [t for b in self.bufs for t in b.r_toks()]

    def note_read(self, tok):
        for b in self.bufs:
            b.note_read(tok)

    def note_write(self, tok):
        for b in self.bufs:
            b.note_write(tok)


class Ctx:
    def __init__(self, nc, es):
        self.nc = nc
        self.es = es
        self.sems = {}
        self.cnt = {}
        self.waited = {}
        self.eng = {"t": nc.tensor, "v": nc.vector, "s": nc.scalar,
                    "p": nc.gpsimd, "q": nc.sync}
        self.esem = {"t": "T", "v": "V", "s": "S", "p": "P", "q": "Q"}
        self.dma_k = {"T": 1, "V": 1, "S": 1, "P": 1, "Q": 1,
                      "DQ": 16, "DP": 16, "DS": 16}
        self.dma_sems = {"DQ", "DP", "DS"}

    def sem(self, name):
        if name not in self.sems:
            self.sems[name] = self.es.enter_context(self.nc.semaphore(name))
            self.cnt[name] = 0
        return self.sems[name]

    def _wait(self, ename, toks, skip_self=True, dma_sem=None):
        skip = ()
        if skip_self:
            if ename == "t":
                skip = ("T",)
            if dma_sem is not None:
                skip = skip + (dma_sem,)
        need = {}
        for t in toks:
            if t is None:
                continue
            s, v = t
            if s in skip:
                continue
            if v > need.get(s, 0):
                need[s] = v
        eng = self.eng[ename]
        w = self.waited.setdefault(ename, {})
        for s, v in sorted(need.items()):
            if w.get(s, -1) >= v:
                continue
            eng.wait_ge(self.sem(s), v)
            w[s] = v

    def op(self, ename, f, reads=(), writes=(), sem=None, dep=()):
        sem = sem or self.esem[ename]
        toks = list(dep)
        for b in reads:
            toks.extend(b.w_toks())
        for b in writes:
            toks.extend(b.w_toks())
            toks.extend(b.r_toks())
        self._wait(ename, toks,
                   dma_sem=sem if sem in self.dma_sems else None)
        inst = f()
        k = self.dma_k.get(sem, 1)
        h = self.sem(sem)
        inst.then_inc(h, k)
        self.cnt[sem] += k
        tok = (sem, self.cnt[sem])
        for b in reads:
            b.note_read(tok)
        for b in writes:
            b.note_write(tok)
        return tok

    def barrier(self, tokens=None):
        toks = [(s, c) for s, c in self.cnt.items()]
        for ename in self.eng:
            self._wait(ename, toks, skip_self=False)


def _mk_dram(nc, name, shape, dtype, kind):
    return nc.dram_tensor(name, shape, dtype, kind=kind).ap()


class Phase:
    _n = 0

    def __init__(self, ctx):
        self.ctx = ctx
        self.es = ExitStack()
        self.bufs = {}
        Phase._n += 1
        self.pfx = f"ph{Phase._n}_"

    def sb(self, name, shape, dtype):
        t = self.es.enter_context(self.ctx.nc.sbuf_tensor(self.pfx + name, shape, dtype))
        b = Buf()
        self.bufs[name] = (t, b)
        return t, b

    def ps(self, name, shape, dtype=F32):
        t = self.es.enter_context(self.ctx.nc.psum_tensor(self.pfx + name, shape, dtype))
        b = Buf()
        self.bufs[name] = (t, b)
        return t, b

    def close(self):
        self.ctx.barrier()
        self.es.close()



def load_consts(ctx, ph, dram, names):
    out = {}
    for name, sbdtype in names:
        ap = dram[name]
        shp = list(ap.tensor.shape)
        if len(shp) == 2 and shp[0] > 128:
            assert shp[0] % 128 == 0
            ap = ap.rearrange("(c p) m -> p c m", p=128)
            shp = [128, shp[0] // 128, shp[1]]
        t, b = ph.sb("c_" + name, shp, sbdtype)
        eng = "p" if sbdtype != ap.tensor.dtype else "q"
        sem = "DP" if eng == "p" else "DQ"
        ctx.op(eng, lambda t=t, ap=ap, eng=eng: ctx.eng[eng].dma_start(out=t[:], in_=ap),
               writes=(b,), sem=sem)
        out[name] = (t, b)
    return out



def emit_ln(ctx, ph, sfx, x3, x_buf, dep, psS_p, NTc=NT, scratch=None,
            stat_scratch=None, xn_scratch=None):
    nc = ctx.nc
    key = "ln_key" + sfx
    if key not in ph.bufs:
        ph.sb("ln_key" + sfx, [1, 1], F32)
        if stat_scratch is None:
            ph.sb("ln_stat" + sfx, [128, 3 * NTc], F32)
        if xn_scratch is None:
            ph.sb("ln_xn" + sfx, [128, CH * NTc], BF16)
        if scratch is None:
            ph.sb("ln_xb" + sfx, [128, CH * NTc], BF16)
            ph.sb("ln_sq" + sfx, [128, CH * NTc], BF16)
    if scratch is None:
        xbt, xb_b = ph.bufs["ln_xb" + sfx]
        sqt, sq_b = ph.bufs["ln_sq" + sfx]
        xb, sq = xbt[:], sqt[:]
    else:
        xb, sq, shared = scratch
        xb_b = sq_b = shared
    if stat_scratch is None:
        statt, stat_b = ph.bufs["ln_stat" + sfx]
        stat = statt[:]
    else:
        stat, stat_b = stat_scratch
    if xn_scratch is None:
        xnt_, xn_b = ph.bufs["ln_xn" + sfx]
        xn = xnt_[:]
    else:
        xn, xn_b = xn_scratch
    psS, psS_b = psS_p
    ones128_bf, ones128_b = ph.bufs["ones128_bf"]
    epst, epst_b = ph.bufs["epsln"]

    xbv = xb.rearrange("p (c n) -> p c n", c=CH)
    sqv = sq.rearrange("p (c n) -> p c n", c=CH)
    n = NTc
    for c in range(CH):
        ctx.op("v", lambda c=c: nc.vector.tensor_copy(out=xbv[:, c, :], in_=x3[:, c, :]),
               reads=(x_buf,), writes=(xb_b,), dep=dep)
        ctx.op("p", lambda c=c: nc.gpsimd.tensor_mul(out=sqv[:, c, :], in0=xbv[:, c, :],
               in1=xbv[:, c, :]), reads=(xb_b,), writes=(sq_b,))

        def mm2(c=c):
            nc.tensor.matmul(out=psS[:, 0:n], lhsT=ones128_bf[:], rhs=xbv[:, c, :],
                             start=(c == 0), stop=(c == CH - 1))
            return nc.tensor.matmul(out=psS[:, n:2 * n], lhsT=ones128_bf[:],
                                    rhs=sqv[:, c, :], start=(c == 0), stop=(c == CH - 1))
        ctx.op("t", mm2, reads=(xb_b, sq_b, ones128_b), writes=(psS_b,))
    mean, a_r, b_r = stat[:, 0:n], stat[:, n:2 * n], stat[:, 2 * n:3 * n]
    ctx.op("v", lambda: nc.vector.tensor_scalar(out=mean, in0=psS[:, 0:n],
           scalar1=1.0 / E, scalar2=None, op0=ALU.mult), reads=(psS_b,), writes=(stat_b,))
    ctx.op("v", lambda: nc.vector.tensor_scalar(out=a_r, in0=psS[:, n:2 * n],
           scalar1=1.0 / E, scalar2=None, op0=ALU.mult), reads=(psS_b,), writes=(stat_b,))
    ctx.op("v", lambda: nc.vector.tensor_mul(out=b_r, in0=mean, in1=mean),
           reads=(stat_b,), writes=(stat_b,))
    ctx.op("v", lambda: nc.vector.tensor_sub(out=a_r, in0=a_r, in1=b_r),
           reads=(stat_b,), writes=(stat_b,))
    ctx.op("s", lambda: nc.scalar.activation(out=b_r, in_=a_r, func=AF.Sqrt,
           bias=epst[:]), reads=(stat_b, epst_b), writes=(stat_b,))
    ctx.op("v", lambda: nc.vector.reciprocal(out=a_r, in_=b_r),
           reads=(stat_b,), writes=(stat_b,))
    ctx.op("v", lambda: nc.vector.scalar_tensor_tensor(out=b_r, in0=mean, scalar=-1.0,
           in1=a_r, op0=ALU.mult, op1=ALU.mult), reads=(stat_b,), writes=(stat_b,))
    xnv = xn.rearrange("p (c n) -> p c n", c=CH)
    tok = None
    for c in range(CH):
        ctx.op("v", lambda c=c: nc.vector.tensor_mul(out=xnv[:, c, :], in0=x3[:, c, :],
               in1=a_r), reads=(x_buf, stat_b), writes=(xn_b,))
        tok = ctx.op("v", lambda c=c: nc.vector.tensor_add(out=xnv[:, c, :],
               in0=xnv[:, c, :], in1=b_r), reads=(stat_b, xn_b), writes=(xn_b,))
    return xn, xn_b, tok


def init_common(ctx, ph):
    nc = ctx.nc
    o128, o128_b = ph.sb("ones128_bf", [128, 128], BF16)
    o1, o1_b = ph.sb("ones1", [1, 128], F32)
    o1b, o1b_b = ph.sb("ones1b", [1, 128], BF16)
    eps, eps_b = ph.sb("epsln", [128, 1], F32)
    ctx.op("v", lambda: nc.vector.memset(o128[:], 1.0), writes=(o128_b,))
    ctx.op("v", lambda: nc.vector.memset(o1[:], 1.0), writes=(o1_b,))
    ctx.op("v", lambda: nc.vector.memset(o1b[:], 1.0), writes=(o1b_b,))
    ctx.op("v", lambda: nc.vector.memset(eps[:], LN_EPS), writes=(eps_b,))


def emit_negb(ctx, ph, nc, negb_p, row_p, nt, psum_p):
    negb, negb_b = negb_p
    row, row_b = row_p
    pst, pst_b = psum_p
    ones1, ones1_b = ph.bufs["ones1"]
    ctx.op("t", lambda: nc.tensor.matmul(out=pst, lhsT=ones1[:],
           rhs=row[0:1, nt:nt + 1], start=True, stop=True),
           reads=(row_b, ones1_b), writes=(pst_b,))
    ctx.op("v", lambda: nc.vector.tensor_copy(out=negb[:], in_=pst),
           reads=(pst_b,), writes=(negb_b,))


def build_phaseA(ctx, dram):
    nc = ctx.nc
    ph = Phase(ctx)
    init_common(ctx, ph)
    cst = load_consts(ctx, ph, dram, [
        ("wqa", BF16), ("wqbp", BF16), ("wkva", BF16), ("wk0p", BF16), ("wv0", BF16),
        ("cossq", F32), ("sinsq", F32), ("cossk", F32), ("sinsk", F32),
        ("permbf", BF16), ("ident", F32),
    ])
    wqa, wqa_b = cst["wqa"]
    wqav = wqa[:]
    wqbp, wqbp_b = cst["wqbp"]
    wkva, wkva_b = cst["wkva"]
    wkvav = wkva[:]
    wk0p, wk0p_b = cst["wk0p"]
    wv0, wv0_b = cst["wv0"]
    cossq, cossq_b = cst["cossq"]
    sinsq, sinsq_b = cst["sinsq"]
    cossk, cossk_b = cst["cossk"]
    sinsk, sinsk_b = cst["sinsk"]
    perm, perm_b = cst["permbf"]
    ident, ident_b = cst["ident"]

    xin = dram["xin"]
    xin_flat = xin.rearrange("s t f -> (s t) f")
    x0_d = dram["x0"]
    q0_d = dram["q0"]
    k0_d = dram["k0"]
    v0_d = dram["v0"]
    negb0_d = dram["negb0"]
    x0_db, q0_db, k0_db, v0_db = Buf(), Buf(), Buf(), Buf()
    negb0_db = Buf()
    q0v = q0_d.rearrange("p (s h t) -> p s h t", s=SEQ, h=8)
    k0v = k0_d.rearrange("p (s h t) -> p s h t", s=SEQ, h=2)
    v0v = v0_d.rearrange("p (s u m) -> p s u m", s=SEQ, u=16)

    xt = [ph.sb(f"xt{i}", [128, 4 * 1024], F32) for i in range(2)]
    x0sb = [ph.sb(f"x0sb{i}", [128, CH * NT], F32) for i in range(2)]
    latq = [ph.sb(f"latq{i}", [64, NT], BF16) for i in range(2)]
    latkv = [ph.sb(f"latkv{i}", [32, NT], BF16) for i in range(2)]
    qbf2 = [ph.sb(f"qbf{i}", [128, NT], BF16) for i in range(2)]
    ro12 = [ph.sb(f"ro1{i}", [128, NT], F32) for i in range(2)]
    ro22 = [ph.sb(f"ro2{i}", [128, NT], F32) for i in range(2)]
    qout2 = [ph.sb(f"qout{i}", [128, NT], BF16) for i in range(2)]
    vout = ph.sb("vout", [128, 4 * 256], BF16)
    sqA = [ph.sb(f"sqA{i}", [128, NT], BF16) for i in range(2)]
    qb = ph.sb("qbA", [1, NTILES], F32)
    kb = ph.sb("kbA", [1, SEQ], F32)
    nbr = ph.sb("nbrA", [1, 2 * NTILES], F32)
    bred = ph.sb("bredA", [1, 2], F32)

    ps_x = ph.ps("ps_x", [128, 1024], F32)
    ps_lat = ph.ps("ps_lat", [128, NT], F32)
    ps_q = [ph.ps(f"ps_q{i}", [128, NT], F32) for i in range(2)]
    ps_sw = ph.ps("ps_sw", [128, NT], F32)
    ps_n = ph.ps("ps_nA", [128, NT], F32)
    ps_v = ps_lat
    ones128_bf, ones128_b = ph.bufs["ones128_bf"]

    for nt in range(NTILES):
        s, a4 = nt // 4, nt % 4
        pos0 = a4 * NT
        xt_t, xt_b = xt[nt % 2]
        x0_t, x0_b = x0sb[nt % 2]
        src = xin_flat[nt * NT:(nt + 1) * NT, :].rearrange("(a p) f -> p a f", p=128)
        ctx.op("q", lambda xt_t=xt_t, src=src: nc.sync.dma_start(out=xt_t[:].rearrange(
            "p (a f) -> p a f", a=4), in_=src), writes=(xt_b,), sem="DQ")
        xtv = xt_t[:].rearrange("p (a f) -> p a f", a=4)
        x0v = x0_t[:].rearrange("p (c n) -> p c n", c=CH)
        for a in range(4):
            def trs(a=a, xtv=xtv):
                i = None
                for c in range(CH):
                    i = nc.tensor.transpose(out=ps_x[0][:, c * 128:(c + 1) * 128],
                                            in_=xtv[:, a, c * 128:(c + 1) * 128],
                                            identity=ident[:])
                return i
            ctx.op("t", trs, reads=(xt_b, ident_b), writes=(ps_x[1],))
            ctx.op("v", lambda a=a, x0v=x0v: nc.vector.tensor_copy(
                out=x0v[:, :, a * 128:(a + 1) * 128],
                in_=ps_x[0][:].rearrange("p (c n) -> p c n", c=CH)),
                reads=(ps_x[1],), writes=(x0_b,))
        ctx.op("s", lambda x0v=x0v, nt=nt: nc.scalar.dma_start(
            out=x0_d[:, nt], in_=x0v), reads=(x0_b,), sem="DS")
        xn, xn_b, _ = emit_ln(ctx, ph, "A", x0v, x0_b, (), ps_x)
        xnv = xn[:].rearrange("p (c n) -> p c n", c=CH)
        latq_t, latq_b = latq[nt % 2]
        latkv_t, latkv_b = latkv[nt % 2]

        def mm_lat():
            for c in range(CH):
                nc.tensor.matmul(out=ps_lat[0][0:64, :], lhsT=wqav[:, c, :],
                                 rhs=xnv[:, c, :], start=(c == 0), stop=(c == CH - 1))
            for c in range(CH):
                i = nc.tensor.matmul(out=ps_lat[0][64:96, :], lhsT=wkvav[:, c, :],
                                     rhs=xnv[:, c, :], start=(c == 0), stop=(c == CH - 1))
            return i
        ctx.op("t", mm_lat, reads=(xn_b, wqa_b, wkva_b), writes=(ps_lat[1],))
        ctx.op("v", lambda latq_t=latq_t: nc.vector.tensor_copy(out=latq_t[:],
               in_=ps_lat[0][0:64, :]), reads=(ps_lat[1],), writes=(latq_b,))
        ctx.op("s", lambda latkv_t=latkv_t: nc.scalar.copy(out=latkv_t[:],
               in_=ps_lat[0][64:96, :]), reads=(ps_lat[1],), writes=(latkv_b,))
        for hp in range(8):
            pq, pq_b = ps_q[hp % 2]
            qbf = qbf2[hp % 2]
            ro1 = ro12[hp % 2]
            ro2 = ro22[hp % 2]
            qout = qout2[hp % 2]
            ctx.op("t", lambda hp=hp, pq=pq, latq_t=latq_t: nc.tensor.matmul(
                out=pq[:], lhsT=wqbp[:, hp * 128:(hp + 1) * 128], rhs=latq_t[:],
                start=True, stop=True), reads=(latq_b, wqbp_b), writes=(pq_b,))
            ctx.op("s", lambda pq=pq: nc.scalar.copy(out=qbf[0][:], in_=pq[:]),
                   reads=(pq_b,), writes=(qbf[1],))
            ctx.op("t", lambda: nc.tensor.matmul(out=ps_sw[0][:], lhsT=perm[:],
                   rhs=qbf[0][:], start=True, stop=True),
                   reads=(qbf[1], perm_b), writes=(ps_sw[1],))
            ctx.op("v", lambda pos0=pos0: nc.vector.tensor_mul(out=ro1[0][:],
                   in0=ps_sw[0][:], in1=sinsq[:, pos0:pos0 + NT]),
                   reads=(ps_sw[1], sinsq_b), writes=(ro1[1],))
            ctx.op("v", lambda pos0=pos0: nc.vector.tensor_mul(out=ro2[0][:],
                   in0=qbf[0][:], in1=cossq[:, pos0:pos0 + NT]),
                   reads=(qbf[1], cossq_b), writes=(ro2[1],))
            ctx.op("v", lambda: nc.vector.tensor_add(out=qout[0][:], in0=ro1[0][:],
                   in1=ro2[0][:]), reads=(ro1[1], ro2[1]), writes=(qout[1],))
            ctx.op("s", lambda s=s, hp=hp, pos0=pos0: nc.scalar.dma_start(
                out=q0v[:, s, hp, pos0:pos0 + NT], in_=qout[0][:]),
                reads=(qout[1],), sem="DS", writes=(q0_db,))
            sqt_, sqb_ = sqA[hp % 2]
            ctx.op("p", lambda sqt_=sqt_: nc.gpsimd.tensor_mul(
                out=sqt_[:], in0=qout[0][:], in1=qout[0][:]),
                reads=(qout[1],), writes=(sqb_,))
            ctx.op("t", lambda hp=hp, sqt_=sqt_: nc.tensor.matmul(
                out=ps_n[0][0:1, :], lhsT=ones128_bf[:, 0:1], rhs=sqt_[:],
                start=(hp == 0), stop=(hp == 7)),
                reads=(sqb_, ones128_b), writes=(ps_n[1],))
        ctx.op("v", lambda nt=nt: nc.vector.reduce_max(out=qb[0][0:1, nt:nt + 1],
               in_=ps_n[0][0:1, :], axis=AX.X), reads=(ps_n[1],), writes=(qb[1],))
        for kp in range(2):
            pq, pq_b = ps_q[kp % 2]
            qbf = qbf2[kp % 2]
            ro1 = ro12[kp % 2]
            ro2 = ro22[kp % 2]
            qout = qout2[kp % 2]
            ctx.op("t", lambda kp=kp, pq=pq, latkv_t=latkv_t: nc.tensor.matmul(
                out=pq[:], lhsT=wk0p[:, kp * 128:(kp + 1) * 128], rhs=latkv_t[:],
                start=True, stop=True), reads=(latkv_b, wk0p_b), writes=(pq_b,))
            ctx.op("s", lambda pq=pq: nc.scalar.copy(out=qbf[0][:], in_=pq[:]),
                   reads=(pq_b,), writes=(qbf[1],))
            ctx.op("t", lambda: nc.tensor.matmul(out=ps_sw[0][:], lhsT=perm[:],
                   rhs=qbf[0][:], start=True, stop=True),
                   reads=(qbf[1], perm_b), writes=(ps_sw[1],))
            ctx.op("v", lambda pos0=pos0: nc.vector.tensor_mul(out=ro1[0][:],
                   in0=ps_sw[0][:], in1=sinsk[:, pos0:pos0 + NT]),
                   reads=(ps_sw[1], sinsk_b), writes=(ro1[1],))
            ctx.op("v", lambda pos0=pos0: nc.vector.tensor_mul(out=ro2[0][:],
                   in0=qbf[0][:], in1=cossk[:, pos0:pos0 + NT]),
                   reads=(qbf[1], cossk_b), writes=(ro2[1],))
            ctx.op("v", lambda: nc.vector.tensor_add(out=qout[0][:], in0=ro1[0][:],
                   in1=ro2[0][:]), reads=(ro1[1], ro2[1]), writes=(qout[1],))
            ctx.op("s", lambda s=s, kp=kp, pos0=pos0: nc.scalar.dma_start(
                out=k0v[:, s, kp, pos0:pos0 + NT], in_=qout[0][:]),
                reads=(qout[1],), sem="DS", writes=(k0_db,))
            sqt_, sqb_ = sqA[kp % 2]
            ctx.op("p", lambda sqt_=sqt_: nc.gpsimd.tensor_mul(
                out=sqt_[:], in0=qout[0][:], in1=qout[0][:]),
                reads=(qout[1],), writes=(sqb_,))
            ctx.op("t", lambda kp=kp, sqt_=sqt_: nc.tensor.matmul(
                out=ps_n[0][0:1, :], lhsT=ones128_bf[:, 0:1], rhs=sqt_[:],
                start=(kp == 0), stop=(kp == 1)),
                reads=(sqb_, ones128_b), writes=(ps_n[1],))
        ctx.op("v", lambda: nc.vector.reduce_max(out=bred[0][0:1, 0:1],
               in_=ps_n[0][0:1, :], axis=AX.X), reads=(ps_n[1],), writes=(bred[1],))
        if a4 == 0:
            ctx.op("v", lambda s=s: nc.vector.tensor_copy(out=kb[0][0:1, s:s + 1],
                   in_=bred[0][0:1, 0:1]), reads=(bred[1],), writes=(kb[1],))
        else:
            ctx.op("v", lambda s=s: nc.vector.tensor_max(out=kb[0][0:1, s:s + 1],
                   in0=kb[0][0:1, s:s + 1], in1=bred[0][0:1, 0:1]),
                   reads=(bred[1], kb[1]), writes=(kb[1],))
        voutv = vout[0][:].rearrange("p (a m) -> p a m", a=4)
        for sub in range(4):
            ctx.op("t", lambda sub=sub, latkv_t=latkv_t: nc.tensor.matmul(
                out=ps_v[0][:, 0:256], lhsT=latkv_t[:, sub * 128:(sub + 1) * 128],
                rhs=wv0[:], start=True, stop=True),
                reads=(latkv_b, wv0_b), writes=(ps_v[1],))
            ctx.op("v", lambda sub=sub: nc.vector.tensor_copy(out=voutv[:, sub, :],
                   in_=ps_v[0][:, 0:256]), reads=(ps_v[1],), writes=(vout[1],))
        ctx.op("s", lambda s=s, a4=a4: nc.scalar.dma_start(
            out=v0v[:, s, a4 * 4:(a4 + 1) * 4, :], in_=voutv),
            reads=(vout[1],), sem="DS", writes=(v0_db,))
    for s in range(SEQ):
        ctx.op("v", lambda s=s: nc.vector.tensor_scalar(
            out=nbr[0][0:1, 4 * s:4 * s + 4], in0=qb[0][0:1, 4 * s:4 * s + 4],
            scalar1=kb[0][0:1, s:s + 1], scalar2=None, op0=ALU.mult),
            reads=(qb[1], kb[1]), writes=(nbr[1],))
    ctx.op("s", lambda: nc.scalar.activation(out=nbr[0][0:1, NTILES:2 * NTILES],
           in_=nbr[0][0:1, 0:NTILES], func=AF.Sqrt), reads=(nbr[1],), writes=(nbr[1],))
    ctx.op("v", lambda: nc.vector.tensor_scalar(out=nbr[0][0:1, NTILES:2 * NTILES],
           in0=nbr[0][0:1, NTILES:2 * NTILES], scalar1=60.0, scalar2=None,
           op0=ALU.min), reads=(nbr[1],), writes=(nbr[1],))
    ctx.op("v", lambda: nc.vector.tensor_scalar(out=nbr[0][0:1, NTILES:2 * NTILES],
           in0=nbr[0][0:1, NTILES:2 * NTILES], scalar1=-1.0, scalar2=None,
           op0=ALU.mult), reads=(nbr[1],), writes=(nbr[1],))
    ctx.op("s", lambda: nc.scalar.dma_start(out=negb0_d,
           in_=nbr[0][0:1, NTILES:2 * NTILES]), reads=(nbr[1],), sem="DS",
           writes=(negb0_db,))
    ph.close()



def build_phaseB(ctx, dram):
    nc = ctx.nc
    ph = Phase(ctx)
    init_common(ctx, ph)
    cst = load_consts(ctx, ph, dram, [
        ("bout_fm", F32), ("rwt", BF16), ("causal", F32),
        ("identf", F32), ("negb0", F32),
    ])
    wout, wout_b = ph.sb("wout_sb", [64, 16 * 1024], BF16)
    ctx.op("p", lambda: nc.gpsimd.dma_start(
        out=wout[:].rearrange("r (h m) -> r h m", h=16),
        in_=dram["wout"].rearrange("(h r) m -> r h m", r=64)),
        writes=(wout_b,), sem="DP")
    woutv = wout[:].rearrange("r (h m) -> r h m", h=16)
    bout, bout_b = cst["bout_fm"]
    rwt, rwt_b = cst["rwt"]
    causal, causal_b = cst["causal"]
    identf, identf_b = cst["identf"]
    negbr, negbr_b = cst["negb0"]
    ones1b, ones1b_b = ph.bufs["ones1b"]

    q0v = dram["q0"].rearrange("p (s h t) -> p s h t", s=SEQ, h=8)
    k0v = dram["k0"].rearrange("p (s h t) -> p s h t", s=SEQ, h=2)
    v0v = dram["v0"].rearrange("p (s u m) -> p s u m", s=SEQ, u=16)
    x0_d = dram["x0"]
    x1_d = dram["x1"]
    xn1_d = dram["xn1"]
    xn1v = xn1_d
    gates_d = dram["gates"]
    x1_db, xn1_db, gates_db = Buf(), Buf(), Buf()

    q_sb, q_b = ph.sb("q_sb", [64, 16 * 512], BF16)
    k_sb, k_b = ph.sb("k_sb", [64, 4 * 2048], BF16)
    v_sb, v_b = ph.sb("v_sb", [128, 16 * 260], BF16)
    pt2 = [ph.sb(f"pt_sb{i}", [128, 16 * 512], BF16) for i in range(2)]
    attn, attn_b = ph.sb("attn", [64, 16 * 512], BF16)
    den2 = [ph.sb(f"denB{i}", [1, 512], BF16) for i in range(2)]
    rb2 = [ph.sb(f"rbB{i}", [64, 512], F32) for i in range(2)]
    negbt, negb_b = ph.sb("negbB", [128, 1], F32)
    x0t = [ph.sb(f"x0t{i}", [128, CH * NT], F32) for i in range(2)]
    x1t, x1t_b = ph.sb("x1t", [128, CH * NT], F32)
    stat, stat_b = ph.sb("statB", [128, 3 * NT], F32)
    xnb, xnb_b = ph.sb("xnB", [128, CH * NT], BF16)
    glog, glog_b = ph.sb("glog", [128, 8], F32)
    gm, gm_b = ph.sb("gm", [128, 4], F32)
    gmask, gmask_b = ph.sb("gmask", [128, 8], F32)
    gtm, gtm_b = ph.sb("gtm", [128, 8], F32)
    gexp, gexp_b = ph.sb("gexp", [128, 8], F32)
    grow, grow_b = ph.sb("grow", [8, 512], F32)

    ps_st = [ph.ps(f"ps_st{i}", [128, 1024], F32) for i in range(2)]
    ps_av = [ph.ps(f"ps_av{i}", [128, 512], F32) for i in range(2)]
    ps_w, ps_w_b = ph.ps("ps_w", [128, 1024], F32)

    qv = q_sb[:].rearrange("p (h t) -> p h t", h=16)
    kv = k_sb[:].rearrange("p (h t) -> p h t", h=4)
    vv = v_sb[:].rearrange("p (u g m) -> p u g m", u=16, g=4)
    attnv = attn[:].rearrange("p (h n) -> p h n", h=16)

    def load_q(nt):
        s2, qtg2 = nt // 4, nt % 4
        for par in range(2):
            ctx.op("q", lambda par=par, s2=s2, qtg2=qtg2: nc.sync.dma_start(
                out=qv[:, par::2, :],
                in_=q0v[par * 64:par * 64 + 64, s2, :, qtg2 * 512:(qtg2 + 1) * 512]),
                writes=(q_b,), sem="DQ")

    def st_chain_B(g, qt, sub, i):
        pt_sb, pt_b = pt2[i % 2]
        ptv = pt_sb[:].rearrange("p (u n) -> p u n", u=16)
        rhs = qv[:, 4 * g:4 * g + 4, sub * 128:(sub + 1) * 128]
        nkt = qt + 1
        for ktg in range((nkt + 1) // 2):
            pst, pst_b = ps_st[ktg % 2]
            nj = min(2, nkt - 2 * ktg)
            for j in range(nj):
                kt = 2 * ktg + j
                lhsT = kv[:, g, kt * 128:(kt + 1) * 128]
                ctx.op("t", lambda lhsT=lhsT, rhs=rhs, pst=pst, j=j:
                       nc.tensor.matmul(out=pst[:, j * 512:(j + 1) * 512],
                                        lhsT=lhsT, rhs=rhs,
                                        start=True, stop=True),
                       reads=(q_b, k_b), writes=(pst_b,))
                if kt == qt:
                    ctx.op("v", lambda pst=pst, j=j: nc.vector.tensor_add(
                           out=pst[:, j * 512:(j + 1) * 512],
                           in0=pst[:, j * 512:(j + 1) * 512], in1=causal[:]),
                           reads=(causal_b,), writes=(pst_b,))
            ctx.op("s", lambda ktg=ktg, nj=nj, pst=pst, ptv=ptv:
                   nc.scalar.activation(
                       out=ptv[:, 2 * ktg:2 * ktg + nj, :],
                       in_=pst[:, 0:nj * 512], func=AF.Exp, bias=negbt[:]),
                   reads=(pst_b, negb_b), writes=(pt_b,))

    def avB(i, g, qt):
        pav, pav_b = ps_av[i % 2]
        pt_sb, pt_b = pt2[i % 2]
        ptv = pt_sb[:].rearrange("p (u n) -> p u n", u=16)

        def av():
            r = None
            for kt in range(qt + 1):
                r = nc.tensor.matmul(out=pav[0:65, :],
                                     lhsT=vv[:, kt, g, :],
                                     rhs=ptv[:, kt, :],
                                     start=(kt == 0), stop=(kt == qt))
            return r
        ctx.op("t", av, reads=(pt_b, v_b), writes=(pav_b,))
        den, den_b = den2[i % 2]
        ctx.op("s", lambda pav=pav, den=den: nc.scalar.copy(out=den[:],
               in_=pav[64:65, :]), reads=(pav_b,), writes=(den_b,))

    def normB(i, g, sub):
        pav, pav_b = ps_av[i % 2]
        den, den_b = den2[i % 2]
        rb, rb_b = rb2[i % 2]
        ctx.op("t", lambda den=den: nc.tensor.matmul(out=ps_w[0:64, 512:1024],
               lhsT=ones1b[:, 0:64], rhs=den[:], start=True, stop=True),
               reads=(den_b, ones1b_b), writes=(ps_w_b,))
        ctx.op("v", lambda rb=rb: nc.vector.reciprocal(out=rb[:],
               in_=ps_w[0:64, 512:1024]), reads=(ps_w_b,), writes=(rb_b,))
        dst = attnv[:, 4 * g:4 * g + 4, sub * 128:(sub + 1) * 128]
        ctx.op("v", lambda pav=pav, dst=dst, rb=rb: nc.vector.tensor_mul(
            out=dst, in0=pav[0:64, :].rearrange("p (a n) -> p a n", a=4),
            in1=rb[:].rearrange("p (a n) -> p a n", a=4)),
            reads=(pav_b, rb_b), writes=(attn_b,))

    gseq = [(sub, g) for sub in range(4) for g in range(4)]
    for s in range(SEQ):
        for par in range(2):
            ctx.op("q", lambda par=par, s=s: nc.sync.dma_start(
                out=kv[:, par::2, :], in_=k0v[par * 64:par * 64 + 64, s, :, :]),
                writes=(k_b,), sem="DQ")
        vtmp, vtmp_b = pt2[0]
        ctx.op("q", lambda s=s, vtmp=vtmp: nc.sync.dma_start(
            out=vtmp[:, 0:4096].rearrange("p (u m) -> p u m", u=16),
            in_=v0v[:, s]), writes=(vtmp_b,), sem="DQ")
        ctx.op("v", lambda vtmp=vtmp: nc.vector.tensor_copy(
            out=vv[:, :, :, 0:64],
            in_=vtmp[:, 0:4096].rearrange("p (u g m) -> p u g m", u=16, g=4)),
            reads=(vtmp_b,), writes=(v_b,))
        if s == 0:
            ctx.op("v", lambda: nc.vector.memset(vv[:, :, :, 64:65], 1.0),
                   writes=(v_b,))
            load_q(0)

        for qtg in range(4):
            nt = s * 4 + qtg
            emit_negb(ctx, ph, nc, (negbt, negb_b), (negbr, negbr_b), nt,
                      (ps_w[:, 512:513], ps_w_b))
            x0_t, x0_b = x0t[qtg % 2]
            x0v3 = x0_t[:].rearrange("p (c n) -> p c n", c=CH)
            ctx.op("q", lambda x0_t=x0_t, nt=nt: nc.sync.dma_start(
                out=x0_t[:].rearrange("p (c n) -> p c n", c=CH),
                in_=x0_d[:, nt]), writes=(x0_b,), sem="DQ")
            for i, (sub, g) in enumerate(gseq):
                qt = qtg * 4 + sub
                st_chain_B(g, qt, sub, i)
                if i == 15 and nt + 1 < NTILES:
                    load_q(nt + 1)
                if i >= 2:
                    normB(i - 2, gseq[i - 2][1], gseq[i - 2][0])
                if i >= 1:
                    avB(i - 1, gseq[i - 1][1], qtg * 4 + gseq[i - 1][0])
            avB(15, gseq[15][1], qtg * 4 + gseq[15][0])
            normB(14, gseq[14][1], gseq[14][0])
            normB(15, gseq[15][1], gseq[15][0])
            x1v = x1t[:].rearrange("p (c n) -> p c n", c=CH)
            for oc in range(CH):
                half = oc % 2
                osl = slice(half * 512, half * 512 + 512)

                def wmm(oc=oc, osl=osl):
                    r = None
                    for h in range(16):
                        r = nc.tensor.matmul(
                            out=ps_w[:, osl],
                            lhsT=woutv[:, h, oc * 128:(oc + 1) * 128],
                            rhs=attnv[:, h, :], start=(h == 0), stop=(h == 15))
                    return r
                ctx.op("t", wmm, reads=(attn_b, wout_b), writes=(ps_w_b,))
                ctx.op("v", lambda oc=oc, osl=osl, x0v3=x0v3:
                       nc.vector.scalar_tensor_tensor(
                           out=x1v[:, oc, :], in0=ps_w[:, osl],
                           scalar=bout[:, oc:oc + 1], in1=x0v3[:, oc, :],
                           op0=ALU.add, op1=ALU.add),
                       reads=(ps_w_b, bout_b, x0_b), writes=(x1t_b,))
            ctx.op("s", lambda nt=nt: nc.scalar.dma_start(
                out=x1_d[:, nt], in_=x1v),
                reads=(x1t_b,), writes=(x1_db,), sem="DS")
            pt0B, pt0B_b = pt2[0]
            xn, xn_b, _ = emit_ln(ctx, ph, "B", x1v, x1t_b, (), (ps_w, ps_w_b),
                                  scratch=(pt0B[:, 0:CH * NT],
                                           pt0B[:, CH * NT:2 * CH * NT], pt0B_b),
                                  stat_scratch=(stat[:], stat_b),
                                  xn_scratch=(xnb[:], xnb_b))
            xnv = xn.rearrange("p (c n) -> p c n", c=CH)
            ctx.op("s", lambda nt=nt, xnv=xnv: nc.scalar.dma_start(
                out=xn1v[:, nt], in_=xnv),
                reads=(xn_b,), writes=(xn1_db,), sem="DS")
            pav0t, pav0_b = ps_av[0]
            pav1t, pav1_b = ps_av[1]
            for sub in range(4):
                def gmm(sub=sub, xnv=xnv):
                    r = None
                    for c in range(CH):
                        r = nc.tensor.matmul(out=pav0t[:, 0:8],
                                             lhsT=xnv[:, c, sub * 128:(sub + 1) * 128],
                                             rhs=rwt[:, c, :],
                                             start=(c == 0), stop=(c == CH - 1))
                    return r
                ctx.op("t", gmm, reads=(xn_b, rwt_b), writes=(pav0_b,))
                m1 = gm[:, 0:1]
                negm1 = gm[:, 1:2]
                m2 = gm[:, 2:3]
                rsum = gm[:, 3:4]
                ctx.op("v", lambda: nc.vector.tensor_copy(out=glog[:], in_=pav0t[:, 0:8]),
                       reads=(pav0_b,), writes=(glog_b,))
                ctx.op("v", lambda: nc.vector.reduce_max(out=m1, in_=glog[:], axis=AX.X),
                       reads=(glog_b,), writes=(gm_b,))
                ctx.op("v", lambda: nc.vector.tensor_scalar(out=gmask[:], in0=glog[:],
                       scalar1=m1, scalar2=None, op0=ALU.is_ge),
                       reads=(glog_b, gm_b), writes=(gmask_b,))
                ctx.op("v", lambda: nc.vector.scalar_tensor_tensor(out=gmask[:],
                       in0=gmask[:], scalar=-1e30, in1=glog[:], op0=ALU.mult,
                       op1=ALU.add), reads=(glog_b, gmask_b), writes=(gmask_b,))
                ctx.op("v", lambda: nc.vector.reduce_max(out=m2, in_=gmask[:], axis=AX.X),
                       reads=(gmask_b,), writes=(gm_b,))
                ctx.op("v", lambda: nc.vector.tensor_scalar(out=gmask[:], in0=glog[:],
                       scalar1=m2, scalar2=None, op0=ALU.is_ge),
                       reads=(glog_b, gm_b), writes=(gmask_b,))
                ctx.op("v", lambda: nc.vector.tensor_scalar(out=negm1, in0=m1,
                       scalar1=-1.0, scalar2=None, op0=ALU.mult),
                       reads=(gm_b,), writes=(gm_b,))
                ctx.op("s", lambda: nc.scalar.activation(out=gexp[:], in_=glog[:],
                       func=AF.Exp, bias=negm1), reads=(glog_b, gm_b), writes=(gexp_b,))
                ctx.op("v", lambda: nc.vector.tensor_mul(out=gexp[:], in0=gexp[:],
                       in1=gmask[:]), reads=(gmask_b, gexp_b), writes=(gexp_b,))
                ctx.op("v", lambda: nc.vector.reduce_sum(out=rsum, in_=gexp[:], axis=AX.X),
                       reads=(gexp_b,), writes=(gm_b,))
                ctx.op("v", lambda: nc.vector.reciprocal(out=rsum, in_=rsum),
                       reads=(gm_b,), writes=(gm_b,))
                ctx.op("v", lambda: nc.vector.tensor_scalar(out=gtm[:], in0=gexp[:],
                       scalar1=rsum, scalar2=None, op0=ALU.mult),
                       reads=(gexp_b, gm_b), writes=(gtm_b,))
                ctx.op("t", lambda: nc.tensor.matmul(out=pav1t[0:8, 0:128], lhsT=gtm[:],
                       rhs=identf[:], start=True, stop=True),
                       reads=(gtm_b, identf_b), writes=(pav1_b,))
                ctx.op("v", lambda sub=sub: nc.vector.tensor_copy(
                    out=grow[:, sub * 128:(sub + 1) * 128], in_=pav1t[0:8, 0:128]),
                    reads=(pav1_b,), writes=(grow_b,))
            ctx.op("s", lambda nt=nt: nc.scalar.dma_start(
                out=gates_d[:, nt * NT:(nt + 1) * NT], in_=grow[:]),
                reads=(grow_b,), writes=(gates_db,), sem="DS")
    ph.close()



def build_phaseC(ctx, dram):
    nc = ctx.nc
    ph = Phase(ctx)
    init_common(ctx, ph)
    ones1, ones1_b = ph.bufs["ones1"]
    xn1v = dram["xn1"]
    x1_d = dram["x1"]
    x2_d = dram["x2"]
    acc_d = dram["accC"]
    gates_d = dram["gates"]
    w1T_d = dram["w1T"]
    w2T_d = dram["w2T"]
    x2_db, acc_db = Buf(), Buf()

    w1 = [ph.sb(f"w1_{i}", [128, 8 * 2048], BF16) for i in range(2)]
    w2 = [ph.sb(f"w2_{i}", [128, 16 * 1024], BF16) for i in range(2)]
    xnt = [ph.sb(f"xnt{i}", [128, CH * NT], BF16) for i in range(2)]
    hsc2 = [ph.sb(f"hsc{i}", [128, 16 * NT], BF16) for i in range(2)]
    gb2 = [ph.sb(f"gb{i}", [128, NT], BF16) for i in range(2)]
    grow2 = [ph.sb(f"growC{i}", [1, NT], F32) for i in range(2)]
    acc, acc_b = ph.sb("accsb", [128, CH * NT], F32)

    ps_h = [ph.ps(f"ps_h{i}", [128, NT], F32) for i in range(2)]
    ps_y = [ph.ps(f"ps_y{i}", [128, NT], F32) for i in range(2)]
    ps_g, ps_g_b = ph.ps("ps_g", [128, NT], F32)

    def load_w(e, slot):
        w1t, w1b = w1[slot]
        w2t, w2b = w2[slot]
        ctx.op("p", lambda e=e, w1t=w1t: nc.gpsimd.dma_start(
            out=w1t[:].rearrange("p (k m) -> p k m", k=8),
            in_=w1T_d[e].rearrange("(k p) m -> p k m", p=128)),
            writes=(w1b,), sem="DP")
        ctx.op("p", lambda e=e, w2t=w2t: nc.gpsimd.dma_start(
            out=w2t[:].rearrange("p (k m) -> p k m", k=16),
            in_=w2T_d[e].rearrange("(k p) m -> p k m", p=128)),
            writes=(w2b,), sem="DP")

    accv = acc[:].rearrange("p (c n) -> p c n", c=CH)

    def emit_h(e, nt, slot):
        w1t, w1b = w1[e % 2]
        w1v = w1t[:].rearrange("p (k m) -> p k m", k=8)
        hsc, hsc_b = hsc2[slot]
        hv = hsc[:].rearrange("p (k m) -> p k m", k=16)
        gb, gb_b = gb2[slot]
        grow, grow_b = grow2[slot]
        xt, xt_b = xnt[nt % 2]
        xv = xt[:].rearrange("p (c n) -> p c n", c=CH)
        ctx.op("q", lambda xt=xt, nt=nt: nc.sync.dma_start(
            out=xt[:].rearrange("p (c n) -> p c n", c=CH),
            in_=xn1v[:, nt]), writes=(xt_b,), sem="DQ")
        ctx.op("q", lambda e=e, nt=nt, grow=grow: nc.sync.dma_start(
            out=grow[:], in_=gates_d[e:e + 1, nt * NT:(nt + 1) * NT]),
            writes=(grow_b,), sem="DQ")
        ctx.op("t", lambda grow=grow: nc.tensor.matmul(out=ps_g[:], lhsT=ones1[:],
               rhs=grow[0:1, :], start=True, stop=True),
               reads=(grow_b, ones1_b), writes=(ps_g_b,))
        ctx.op("v", lambda gb=gb: nc.vector.tensor_copy(out=gb[:], in_=ps_g[:]),
               reads=(ps_g_b,), writes=(gb_b,))
        for mc in range(16):
            ps, ps_b = ps_h[mc % 2]

            def hmm(mc=mc, ps=ps, xv=xv, w1v=w1v):
                r = None
                for kc in range(CH):
                    r = nc.tensor.matmul(out=ps[:],
                                         lhsT=w1v[:, kc, mc * 128:(mc + 1) * 128],
                                         rhs=xv[:, kc, :],
                                         start=(kc == 0), stop=(kc == CH - 1))
                return r
            ctx.op("t", hmm, reads=(xt_b, w1b), writes=(ps_b,))
            ctx.op("s", lambda mc=mc, ps=ps, hv=hv: nc.scalar.activation(
                out=hv[:, mc, :], in_=ps[:], func=AF.Gelu),
                reads=(ps_b,), writes=(hsc_b,))
            ctx.op("v", lambda mc=mc, hv=hv, gb=gb: nc.vector.tensor_mul(
                out=hv[:, mc, :], in0=hv[:, mc, :], in1=gb[:]),
                reads=(gb_b, hsc_b), writes=(hsc_b,))

    def emit_y(e, nt, slot):
        w2t, w2b = w2[e % 2]
        w2v = w2t[:].rearrange("p (k m) -> p k m", k=16)
        hsc, hsc_b = hsc2[slot]
        hv = hsc[:].rearrange("p (k m) -> p k m", k=16)
        src = x1_d if e == 0 else acc_d
        ctx.op("q", lambda src=src, nt=nt: nc.sync.dma_start(
            out=acc[:].rearrange("p (c n) -> p c n", c=CH),
            in_=src[:, nt]), writes=(acc_b,), sem="DQ",
            dep=(acc_db.w,) if e > 0 else ())
        for oc in range(CH):
            ps, ps_b = ps_y[oc % 2]

            def ymm(oc=oc, ps=ps, hv=hv, w2v=w2v):
                r = None
                for kc in range(16):
                    r = nc.tensor.matmul(out=ps[:],
                                         lhsT=w2v[:, kc, oc * 128:(oc + 1) * 128],
                                         rhs=hv[:, kc, :],
                                         start=(kc == 0), stop=(kc == 15))
                return r
            ctx.op("t", ymm, reads=(hsc_b, w2b), writes=(ps_b,))
            ctx.op("v", lambda oc=oc, ps=ps: nc.vector.tensor_add(
                out=accv[:, oc, :], in0=ps[:], in1=accv[:, oc, :]),
                reads=(ps_b, acc_b), writes=(acc_b,))
        dst_d, dst_db = (x2_d, x2_db) if e == NE - 1 else (acc_d, acc_db)
        ctx.op("s", lambda dst_d=dst_d, nt=nt: nc.scalar.dma_start(
            out=dst_d[:, nt],
            in_=acc[:].rearrange("p (c n) -> p c n", c=CH)),
            reads=(acc_b,), writes=(dst_db,), sem="DS")

    load_w(0, 0)
    prev = None
    for e in range(NE):
        for nt in range(NTILES):
            if nt == 1 and e + 1 < NE:
                load_w(e + 1, (e + 1) % 2)
            slot = (e * NTILES + nt) % 2
            emit_h(e, nt, slot)
            if prev is not None:
                emit_y(*prev)
            prev = (e, nt, slot)
    emit_y(*prev)
    ph.close()



def build_phaseD(ctx, dram):
    nc = ctx.nc
    ph = Phase(ctx)
    init_common(ctx, ph)
    cst = load_consts(ctx, ph, dram, [
        ("wq1", BF16), ("bq1", F32), ("wl1", BF16), ("bl1", F32),
        ("wk1", BF16), ("bk1", F32), ("wv1", BF16), ("bv1", F32),
    ])
    wq1, wq1_b = cst["wq1"]
    bq1, bq1_b = cst["bq1"]
    wl1, wl1_b = cst["wl1"]
    bl1, bl1_b = cst["bl1"]
    wk1, wk1_b = cst["wk1"]
    bk1, bk1_b = cst["bk1"]
    wv1, wv1_b = cst["wv1"]
    bv1, bv1_b = cst["bv1"]
    ones1, ones1_b = ph.bufs["ones1"]
    ones128_bf, ones128_b = ph.bufs["ones128_bf"]

    x2_d = dram["x2"]
    q1v = dram["q1"].rearrange("p (s h t) -> p s h t", s=SEQ, h=8)
    k1v = dram["k1"].rearrange("p (s h t) -> p s h t", s=SEQ, h=8)
    v1v = dram["v1"].rearrange("p (s u m) -> p s u m", s=SEQ, u=16)
    negb1_d = dram["negb1"]
    q1_db, k1_db, v1_db, negb1_db = Buf(), Buf(), Buf(), Buf()

    x2t = [ph.sb(f"x2t{i}", [128, CH * NT], F32) for i in range(2)]
    xn2 = [ph.sb(f"xnD{i}", [128, CH * NT], BF16) for i in range(2)]
    lat, lat_b = ph.sb("lat1", [128, 2 * NT], BF16)
    pout2 = [ph.sb(f"pout{i}", [128, NT], BF16) for i in range(2)]
    vout, vout_b = ph.sb("voutD", [128, 4 * 1024], BF16)
    sqD = [ph.sb(f"sqD{i}", [128, NT], BF16) for i in range(2)]
    qb = ph.sb("qbD", [1, NTILES], F32)
    kbs = ph.sb("kbsD", [1, SEQ], F32)
    nbr = ph.sb("nbrD", [1, 2 * NTILES], F32)
    bred = ph.sb("bredD", [1, 2], F32)

    ps_q = [ph.ps(f"ps_qD{i}", [128, NT], F32) for i in range(2)]
    ps_v, ps_v_b = ph.ps("ps_vD", [128, 1024], F32)
    ps_n = ph.ps("ps_nD", [128, NT], F32)

    latv = lat[:].rearrange("p (k n) -> p k n", k=2)
    voutv = vout[:].rearrange("p (a m) -> p a m", a=4)

    def emit_LN(nt):
        x2_t, x2_b = x2t[nt % 2]
        ctx.op("q", lambda x2_t=x2_t, nt=nt: nc.sync.dma_start(
            out=x2_t[:].rearrange("p (c n) -> p c n", c=CH),
            in_=x2_d[:, nt]), writes=(x2_b,), sem="DQ")
        x2v = x2_t[:].rearrange("p (c n) -> p c n", c=CH)
        xnt_, xn_b = xn2[nt % 2]
        emit_ln(ctx, ph, "D", x2v, x2_b, (), (ps_v, ps_v_b),
                xn_scratch=(xnt_[:], xn_b))

    def bound_sq(pout_p, hp, first, last):
        pout, pout_b = pout_p
        sqt_, sqb_ = sqD[hp % 2]
        ctx.op("v", lambda sqt_=sqt_, pout=pout: nc.vector.tensor_mul(
            out=sqt_[:], in0=pout[:], in1=pout[:]),
            reads=(pout_b,), writes=(sqb_,))
        ctx.op("t", lambda sqt_=sqt_, first=first, last=last: nc.tensor.matmul(
            out=ps_n[0][0:1, :], lhsT=ones128_bf[:, 0:1], rhs=sqt_[:],
            start=first, stop=last),
            reads=(sqb_, ones128_b), writes=(ps_n[1],))

    emit_LN(0)
    for nt in range(NTILES):
        s, a4 = nt // 4, nt % 4
        if nt + 1 < NTILES:
            emit_LN(nt + 1)
        xnt_, xn_b = xn2[nt % 2]
        xnv = xnt_[:].rearrange("p (c n) -> p c n", c=CH)
        for hp in range(8):
            ps, ps_b = ps_q[hp % 2]

            def qmm(hp=hp, ps=ps):
                r = None
                for kc in range(CH):
                    r = nc.tensor.matmul(out=ps[:],
                                         lhsT=wq1[:, kc, hp * 128:(hp + 1) * 128],
                                         rhs=xnv[:, kc, :],
                                         start=(kc == 0), stop=(kc == CH - 1))
                return r
            ctx.op("t", qmm, reads=(xn_b, wq1_b), writes=(ps_b,))
            pout, pout_b = pout2[hp % 2]
            ctx.op("s", lambda hp=hp, ps=ps, pout=pout: nc.scalar.activation(
                out=pout[:], in_=ps[:], func=AF.Identity, bias=bq1[:, hp:hp + 1]),
                reads=(ps_b, bq1_b), writes=(pout_b,))
            ctx.op("s", lambda s=s, hp=hp, a4=a4, pout=pout: nc.scalar.dma_start(
                out=q1v[:, s, hp, a4 * NT:(a4 + 1) * NT], in_=pout[:]),
                reads=(pout_b,), writes=(q1_db,), sem="DS")
            bound_sq(pout2[hp % 2], hp, hp == 0, hp == 7)
        ctx.op("v", lambda nt=nt: nc.vector.reduce_max(out=qb[0][0:1, nt:nt + 1],
               in_=ps_n[0][0:1, :], axis=AX.X), reads=(ps_n[1],), writes=(qb[1],))
        for lc in range(2):
            ps, ps_b = ps_q[lc % 2]

            def lmm(lc=lc, ps=ps):
                r = None
                for kc in range(CH):
                    r = nc.tensor.matmul(out=ps[:],
                                         lhsT=wl1[:, kc, lc * 128:(lc + 1) * 128],
                                         rhs=xnv[:, kc, :],
                                         start=(kc == 0), stop=(kc == CH - 1))
                return r
            ctx.op("t", lmm, reads=(xn_b, wl1_b), writes=(ps_b,))
            ctx.op("s", lambda lc=lc, ps=ps: nc.scalar.activation(
                out=latv[:, lc, :], in_=ps[:], func=AF.Identity,
                bias=bl1[:, lc:lc + 1]), reads=(ps_b, bl1_b), writes=(lat_b,))
        for hp in range(8):
            ps, ps_b = ps_q[hp % 2]

            def kmm(hp=hp, ps=ps):
                r = None
                for kc in range(2):
                    r = nc.tensor.matmul(out=ps[:],
                                         lhsT=wk1[:, kc, hp * 128:(hp + 1) * 128],
                                         rhs=latv[:, kc, :],
                                         start=(kc == 0), stop=(kc == 1))
                return r
            ctx.op("t", kmm, reads=(lat_b, wk1_b), writes=(ps_b,))
            pout, pout_b = pout2[hp % 2]
            ctx.op("s", lambda hp=hp, ps=ps, pout=pout: nc.scalar.activation(
                out=pout[:], in_=ps[:], func=AF.Identity, bias=bk1[:, hp:hp + 1]),
                reads=(ps_b, bk1_b), writes=(pout_b,))
            ctx.op("s", lambda s=s, hp=hp, a4=a4, pout=pout: nc.scalar.dma_start(
                out=k1v[:, s, hp, a4 * NT:(a4 + 1) * NT], in_=pout[:]),
                reads=(pout_b,), writes=(k1_db,), sem="DS")
            bound_sq(pout2[hp % 2], hp, hp == 0, hp == 7)
        ctx.op("v", lambda: nc.vector.reduce_max(out=bred[0][0:1, 0:1],
               in_=ps_n[0][0:1, :], axis=AX.X), reads=(ps_n[1],), writes=(bred[1],))
        if a4 == 0:
            ctx.op("v", lambda s=s: nc.vector.tensor_copy(out=kbs[0][0:1, s:s + 1],
                   in_=bred[0][0:1, 0:1]), reads=(bred[1],), writes=(kbs[1],))
        else:
            ctx.op("v", lambda s=s: nc.vector.tensor_max(out=kbs[0][0:1, s:s + 1],
                   in0=kbs[0][0:1, s:s + 1], in1=bred[0][0:1, 0:1]),
                   reads=(bred[1], kbs[1]), writes=(kbs[1],))
        for sub in range(4):
            def vmm(sub=sub):
                r = None
                for half in range(2):
                    osl = slice(half * 512, half * 512 + 512)
                    for kc in range(2):
                        nc.tensor.matmul(out=ps_v[:, osl],
                                         lhsT=latv[:, kc, sub * 128:(sub + 1) * 128],
                                         rhs=wv1[:, kc, osl],
                                         start=(kc == 0), stop=False)
                    r = nc.tensor.matmul(out=ps_v[:, osl], lhsT=ones1[:],
                                         rhs=bv1[0:1, osl], start=False, stop=True)
                return r
            ctx.op("t", vmm, reads=(lat_b, wv1_b, bv1_b, ones1_b), writes=(ps_v_b,))
            ctx.op("v", lambda sub=sub: nc.vector.tensor_copy(out=voutv[:, sub, :],
                   in_=ps_v[:]), reads=(ps_v_b,), writes=(vout_b,))
        ctx.op("s", lambda s=s, a4=a4: nc.scalar.dma_start(
            out=v1v[:, s, a4 * 4:(a4 + 1) * 4, :], in_=voutv),
            reads=(vout_b,), writes=(v1_db,), sem="DS")
    for s in range(SEQ):
        ctx.op("v", lambda s=s: nc.vector.tensor_scalar(
            out=nbr[0][0:1, 4 * s:4 * s + 4], in0=qb[0][0:1, 4 * s:4 * s + 4],
            scalar1=kbs[0][0:1, s:s + 1], scalar2=None, op0=ALU.mult),
            reads=(qb[1], kbs[1]), writes=(nbr[1],))
    ctx.op("s", lambda: nc.scalar.activation(out=nbr[0][0:1, NTILES:2 * NTILES],
           in_=nbr[0][0:1, 0:NTILES], func=AF.Sqrt), reads=(nbr[1],), writes=(nbr[1],))
    ctx.op("v", lambda: nc.vector.tensor_scalar(out=nbr[0][0:1, NTILES:2 * NTILES],
           in0=nbr[0][0:1, NTILES:2 * NTILES], scalar1=60.0 / SCALE0, scalar2=None,
           op0=ALU.min), reads=(nbr[1],), writes=(nbr[1],))
    ctx.op("v", lambda: nc.vector.tensor_scalar(out=nbr[0][0:1, NTILES:2 * NTILES],
           in0=nbr[0][0:1, NTILES:2 * NTILES], scalar1=-SCALE0, scalar2=None,
           op0=ALU.mult), reads=(nbr[1],), writes=(nbr[1],))
    ctx.op("s", lambda: nc.scalar.dma_start(out=negb1_d,
           in_=nbr[0][0:1, NTILES:2 * NTILES]), reads=(nbr[1],), sem="DS",
           writes=(negb1_db,))
    ph.close()



def build_phaseE(ctx, dram):
    nc = ctx.nc
    ph = Phase(ctx)
    init_common(ctx, ph)
    cst = load_consts(ctx, ph, dram, [("bo1", F32), ("negb1", F32)])
    bo1, bo1_b = cst["bo1"]
    negbr, negbr_b = cst["negb1"]
    ones1b, ones1b_b = ph.bufs["ones1b"]
    wo, wo_b = ph.sb("wo_sb", [64, 16 * 1024], BF16)
    ctx.op("p", lambda: nc.gpsimd.dma_start(
        out=wo[:].rearrange("r (h m) -> r h m", h=16),
        in_=dram["wo1"].rearrange("(h r) m -> r h m", r=64)),
        writes=(wo_b,), sem="DP")
    wov = wo[:].rearrange("r (h m) -> r h m", h=16)

    q1v = dram["q1"].rearrange("p (s h t) -> p s h t", s=SEQ, h=8)
    k1v = dram["k1"].rearrange("p (s h t) -> p s h t", s=SEQ, h=8)
    v1v = dram["v1"].rearrange("p (s u m) -> p s u m", s=SEQ, u=16)
    x2_d = dram["x2"]
    x3_d = dram["x3"]
    xn3_d = dram["xn3"]
    xn3v = xn3_d
    x3_db, xn3_db = Buf(), Buf()

    q_sb, q_b = ph.sb("q_sbE", [128, 8 * 512], BF16)
    k_sb, k_b = ph.sb("k_sbE", [128, 8 * 2048], BF16)
    v2 = [ph.sb(f"v_sbE{i}", [128, 16 * 8 * 65], BF16) for i in range(2)]
    pt2 = [ph.sb(f"pt_sbE{i}", [128, 16 * 512], BF16) for i in range(2)]
    attn, attn_b = ph.sb("attnE", [64, 16 * 512], BF16)
    den2 = [ph.sb(f"denE{i}", [1, 512], BF16) for i in range(2)]
    rb2 = [ph.sb(f"rbE{i}", [64, 512], F32) for i in range(2)]
    negbt, negb_b = ph.sb("negbE", [128, 1], F32)
    x3t, x3t_b = ph.sb("x3tE", [128, CH * NT], F32)

    ps_st = [ph.ps(f"ps_stE{i}", [128, 1024], F32) for i in range(2)]
    ps_av = [ph.ps(f"ps_avE{i}", [128, 512], F32) for i in range(2)]
    ps_w, ps_w_b = ph.ps("ps_wE", [128, 1024], F32)

    qv = q_sb[:].rearrange("p (h t) -> p h t", h=8)
    kvv = k_sb[:].rearrange("p (h t) -> p h t", h=8)
    vvv = [v2[i][0][:].rearrange("p (u h m) -> p u h m", u=16, h=8) for i in range(2)]
    attnv = attn[:].rearrange("p (h n) -> p h n", h=16)
    pt1t = pt2[1][0]
    statE = pt1t[:, 0:3072].bitcast(F32)
    xnE = pt1t[:, 3072:3072 + CH * NT]

    for half in range(2):
        ctx.op("v", lambda half=half: nc.vector.memset(
            vvv[half][:, :, :, 64:65], 1.0), writes=(v2[half][1],))

    def load_q(nt):
        s2, qtg2 = nt // 4, nt % 4
        for half in range(2):
            for par in range(2):
                ctx.op("q", lambda par=par, half=half, s2=s2, qtg2=qtg2:
                       nc.sync.dma_start(
                           out=qv[64 * half:64 * half + 64, par::2, :],
                           in_=q1v[par * 64:par * 64 + 64, s2, 4 * half:4 * half + 4,
                                   qtg2 * 512:(qtg2 + 1) * 512]),
                       writes=(q_b,), sem="DQ")

    def load_k(s):
        for half in range(2):
            for par in range(2):
                ctx.op("q", lambda par=par, s=s, half=half: nc.sync.dma_start(
                    out=kvv[64 * half:64 * half + 64, par::2, :],
                    in_=k1v[par * 64:par * 64 + 64, s, 4 * half:4 * half + 4, :]),
                    writes=(k_b,), sem="DQ")

    def load_v(s, half):
        vtmp, vtmp_b = pt2[0]
        ctx.op("q", lambda s=s, half=half, vtmp=vtmp: nc.sync.dma_start(
            out=vtmp[:, 0:8192].rearrange("p (u m) -> p u m", u=16),
            in_=v1v[:, s, :, 512 * half:512 * half + 512]),
            writes=(vtmp_b,), sem="DQ")
        ctx.op("v", lambda vtmp=vtmp, half=half: nc.vector.tensor_copy(
            out=vvv[half][:, :, :, 0:64],
            in_=vtmp[:, 0:8192].rearrange("p (u h m) -> p u h m", u=16, h=8)),
            reads=(vtmp_b,), writes=(v2[half][1],))

    def st_burst(h):
        half = h // 8
        k_bb = k_b
        pt_sb, pt_b = pt2[h % 2]
        ptv = pt_sb[:].rearrange("p (u n) -> p u n", u=16)
        rhs = qv[64 * half:64 * half + 64, h % 8, :]
        for ktg in range(8):
            pst, pst_b = ps_st[ktg % 2]
            for j in range(2):
                kt = 2 * ktg + j
                lhsT = kvv[64 * half:64 * half + 64, h % 8, kt * 128:(kt + 1) * 128]
                ctx.op("t", lambda lhsT=lhsT, rhs=rhs, j=j, pst=pst:
                       nc.tensor.matmul(out=pst[:, j * 512:(j + 1) * 512],
                                        lhsT=lhsT, rhs=rhs,
                                        start=True, stop=True),
                       reads=(q_b, k_bb), writes=(pst_b,))
            ctx.op("s", lambda ktg=ktg, ptv=ptv, pst=pst: nc.scalar.activation(
                   out=ptv[:, 2 * ktg:2 * ktg + 2, :], in_=pst[:],
                   func=AF.Exp, bias=negbt[:], scale=SCALE0),
                   reads=(pst_b, negb_b), writes=(pt_b,))

    def avE(h):
        half = h // 8
        pav, pav_b = ps_av[h % 2]
        pt_sb, pt_b = pt2[h % 2]
        ptv = pt_sb[:].rearrange("p (u n) -> p u n", u=16)

        def av():
            r = None
            for kt in range(16):
                r = nc.tensor.matmul(out=pav[0:65, :],
                                     lhsT=vvv[half][:, kt, h % 8, :],
                                     rhs=ptv[:, kt, :],
                                     start=(kt == 0), stop=(kt == 15))
            return r
        ctx.op("t", av, reads=(pt_b, v2[half][1]), writes=(pav_b,))
        den, den_b = den2[h % 2]
        ctx.op("s", lambda pav=pav, den=den: nc.scalar.copy(out=den[:],
               in_=pav[64:65, :]), reads=(pav_b,), writes=(den_b,))

    def normE(h):
        pav, pav_b = ps_av[h % 2]
        den, den_b = den2[h % 2]
        rb, rb_b = rb2[h % 2]
        ctx.op("t", lambda den=den: nc.tensor.matmul(out=ps_w[0:64, 512:1024],
               lhsT=ones1b[:, 0:64], rhs=den[:], start=True, stop=True),
               reads=(den_b, ones1b_b), writes=(ps_w_b,))
        ctx.op("v", lambda rb=rb: nc.vector.reciprocal(out=rb[:],
               in_=ps_w[0:64, 512:1024]), reads=(ps_w_b,), writes=(rb_b,))
        ctx.op("v", lambda pav=pav, h=h, rb=rb: nc.vector.tensor_mul(
               out=attnv[:, h, :], in0=pav[0:64, :], in1=rb[:]),
               reads=(pav_b, rb_b), writes=(attn_b,))

    for s in range(SEQ):
        load_k(s)
        load_v(s, 0)
        load_v(s, 1)
        if s == 0:
            load_q(0)
        for qtg in range(4):
            nt = s * 4 + qtg
            emit_negb(ctx, ph, nc, (negbt, negb_b), (negbr, negbr_b), nt,
                      (ps_w[:, 512:513], ps_w_b))
            ctx.op("q", lambda nt=nt: nc.sync.dma_start(
                out=x3t[:].rearrange("p (c n) -> p c n", c=CH),
                in_=x2_d[:, nt]), writes=(x3t_b,), sem="DQ")
            for h in range(16):
                st_burst(h)
                if h == 15 and nt + 1 < NTILES:
                    load_q(nt + 1)
                if h >= 2:
                    normE(h - 2)
                if h >= 1:
                    avE(h - 1)
            avE(15)
            normE(14)
            normE(15)
            x3v = x3t[:].rearrange("p (c n) -> p c n", c=CH)
            for oc in range(CH):
                half = oc % 2
                osl = slice(half * 512, half * 512 + 512)

                def wmm(oc=oc, osl=osl):
                    r = None
                    for h in range(16):
                        r = nc.tensor.matmul(
                            out=ps_w[:, osl],
                            lhsT=wov[:, h, oc * 128:(oc + 1) * 128],
                            rhs=attnv[:, h, :], start=(h == 0), stop=(h == 15))
                    return r
                ctx.op("t", wmm, reads=(attn_b, wo_b), writes=(ps_w_b,))
                ctx.op("v", lambda oc=oc, osl=osl: nc.vector.scalar_tensor_tensor(
                       out=x3v[:, oc, :], in0=ps_w[:, osl],
                       scalar=bo1[:, oc:oc + 1], in1=x3v[:, oc, :],
                       op0=ALU.add, op1=ALU.add),
                       reads=(ps_w_b, bo1_b), writes=(x3t_b,))
            ctx.op("s", lambda nt=nt: nc.scalar.dma_start(
                out=x3_d[:, nt], in_=x3v),
                reads=(x3t_b,), writes=(x3_db,), sem="DS")
            pt0t, pt0_b = pt2[0]
            pt1_b = pt2[1][1]
            xn, xn_b, _ = emit_ln(ctx, ph, "E", x3v, x3t_b, (), (ps_w, ps_w_b),
                                  scratch=(pt0t[:, 0:CH * NT],
                                           pt0t[:, CH * NT:2 * CH * NT], pt0_b),
                                  stat_scratch=(statE, pt1_b),
                                  xn_scratch=(xnE, pt1_b))
            xnv = xn.rearrange("p (c n) -> p c n", c=CH)
            ctx.op("s", lambda nt=nt, xnv=xnv: nc.scalar.dma_start(
                out=xn3v[:, nt], in_=xnv),
                reads=(xn_b,), writes=(xn3_db,), sem="DS")
    ph.close()



def _fold_butterfly(ctx, ph, nc, dram, which, ebuf, ebuf_b, ncols, chunk_bufs=None):
    dsel, dsel_b = ph.sb(which + "_dsel", [128, 5 * 32], F32)
    doth, doth_b = ph.sb(which + "_doth", [128, 5 * 32], F32)
    ctx.op("q", lambda: nc.sync.dma_start(out=dsel[:], in_=dram[which + "dself"]),
           writes=(dsel_b,), sem="DQ")
    ctx.op("q", lambda: nc.sync.dma_start(out=doth[:], in_=dram[which + "doth"]),
           writes=(doth_b,), sem="DQ")
    smat = [ph.sb(f"{which}_sm{i}", [128, 32 * 128], BF16) for i in range(2)]
    ps_f, ps_f_b = ph.ps("ps_fold" + which, [128, 1056], F32)
    ps_f2, ps_f2_b = ph.ps("ps_fold2" + which, [128, 1056], F32)
    tmpA, tmpA_b = ph.sb(which + "_tmpA", [128, 1056], BF16)
    tmpB, tmpB_b = ph.sb(which + "_tmpB", [128, 1056], BF16)
    ev = ebuf[:].rearrange("p (c n) -> p c n", c=32)
    chunk_d = dram[which + "chunk"]

    def load_sm(s_, slot):
        t, b = smat[slot]
        ctx.op("p", lambda s_=s_, t=t: nc.gpsimd.dma_start(
            out=t[:].rearrange("p (c m) -> p c m", c=32), in_=chunk_d[s_]),
            writes=(b,), sem="DP")

    cb = chunk_bufs if chunk_bufs is not None else [ebuf_b] * 32
    stages = list(range(12)) if which == "up" else list(range(11, -1, -1))
    chunk_stages = [s_ for s_ in stages if s_ <= 6]
    diag_stages = [s_ for s_ in stages if s_ >= 7]
    order = stages
    ci = 0
    load_sm(chunk_stages[0], 0)
    for s_ in order:
        if s_ <= 6:
            t, b = smat[ci % 2]
            if ci + 1 < len(chunk_stages):
                load_sm(chunk_stages[ci + 1], (ci + 1) % 2)
            tv = t[:].rearrange("p (c m) -> p c m", c=32)
            for c in range(32):
                psx, psx_b = (ps_f, ps_f_b) if c % 2 == 0 else (ps_f2, ps_f2_b)

                def fmm(c=c, tv=tv, psx=psx):
                    i = None
                    sls = [slice(0, 512), slice(512, 1024)]
                    if ncols > 1024:
                        sls.append(slice(1024, ncols))
                    for sl in sls:
                        i = nc.tensor.matmul(out=psx[:, sl], lhsT=tv[:, c, :],
                                             rhs=ev[:, c, sl], start=True, stop=True)
                    return i
                ctx.op("t", fmm, reads=(cb[c], b), writes=(psx_b,))
                ctx.op("v", lambda c=c, psx=psx: nc.vector.tensor_copy(
                    out=ev[:, c, :], in_=psx[:, 0:ncols]),
                    reads=(psx_b,), writes=(cb[c],))
            ci += 1
        else:
            si = s_ - 7
            bit = (1 << s_) // 128
            for c in range(32):
                if c & bit:
                    continue
                c2 = c | bit
                ctx.op("v", lambda c=c, c2=c2, si=si: nc.vector.tensor_scalar(
                    out=tmpA[:, 0:ncols], in0=ev[:, c, :],
                    scalar1=dsel[:, si * 32 + c:si * 32 + c + 1], scalar2=None,
                    op0=ALU.mult), reads=(cb[c], dsel_b), writes=(tmpA_b,))
                ctx.op("v", lambda c=c, c2=c2, si=si: nc.vector.scalar_tensor_tensor(
                    out=tmpA[:, 0:ncols], in0=ev[:, c2, :],
                    scalar=doth[:, si * 32 + c:si * 32 + c + 1], in1=tmpA[:, 0:ncols],
                    op0=ALU.mult, op1=ALU.add),
                    reads=(cb[c2], doth_b, tmpA_b), writes=(tmpA_b,))
                ctx.op("v", lambda c=c, c2=c2, si=si: nc.vector.tensor_scalar(
                    out=tmpB[:, 0:ncols], in0=ev[:, c2, :],
                    scalar1=dsel[:, si * 32 + c2:si * 32 + c2 + 1], scalar2=None,
                    op0=ALU.mult), reads=(cb[c2], dsel_b), writes=(tmpB_b,))
                ctx.op("v", lambda c=c, c2=c2, si=si: nc.vector.scalar_tensor_tensor(
                    out=tmpB[:, 0:ncols], in0=ev[:, c, :],
                    scalar=doth[:, si * 32 + c2:si * 32 + c2 + 1], in1=tmpB[:, 0:ncols],
                    op0=ALU.mult, op1=ALU.add),
                    reads=(cb[c], doth_b, tmpB_b), writes=(tmpB_b,))
                ctx.op("v", lambda c=c: nc.vector.tensor_copy(
                    out=ev[:, c, :], in_=tmpA[:, 0:ncols]),
                    reads=(tmpA_b,), writes=(cb[c],))
                ctx.op("v", lambda c2=c2: nc.vector.tensor_copy(
                    out=ev[:, c2, :], in_=tmpB[:, 0:ncols]),
                    reads=(tmpB_b,), writes=(cb[c2],))


def build_phaseF(ctx, dram):
    nc = ctx.nc
    ph = Phase(ctx)
    init_common(ctx, ph)
    x3_d = dram["x3"]
    xn3v = dram["xn3"]
    out_d = dram["out"]
    outf = out_d.rearrange("s t f -> (s t) f")
    out_db = Buf()

    ubf, ubf_b = ph.sb("upbias_sb", [128, 32], F32)
    bsum, bsum_b = ph.sb("bsum", [128, 32], F32)
    ctx.op("q", lambda: nc.sync.dma_start(out=ubf[:], in_=dram["upbias"]),
           writes=(ubf_b,), sem="DQ")
    phF = Phase(ctx)
    ebuf, ebuf_b = phF.sb("ebuf", [128, 32 * 1025], BF16)
    ecb = [Buf() for _ in range(32)]
    ebuf_b = BufView(*ecb)
    ctx.op("p", lambda: nc.gpsimd.dma_start(
        out=ebuf[:].rearrange("p (c n) -> p c n", c=32), in_=dram["e0"]),
        writes=(ebuf_b,), sem="DP")
    _fold_butterfly(ctx, phF, nc, dram, "up", ebuf, ebuf_b, 1025, chunk_bufs=ecb)
    ev = ebuf[:].rearrange("p (c n) -> p c n", c=32)
    ctx.op("v", lambda: nc.vector.tensor_add(out=bsum[:], in0=ev[:, :, 1024],
           in1=ubf[:]), reads=(ebuf_b, ubf_b), writes=(bsum_b,))
    escr = dram["escr"]
    escr_db = Buf()
    ctx.op("s", lambda: nc.scalar.dma_start(
        out=escr.rearrange("(c p) n -> p c n", p=128), in_=ev[:, :, 0:1024]),
        reads=(ebuf_b,), writes=(escr_db,), sem="DS")
    phF.close()

    we, we_b = ph.sb("we", [128, 32 * 8 * 128], BF16)
    wev = we[:].rearrange("p (c k m) -> p c k m", c=32, k=8)
    for c in range(32):
        for kc in range(8):
            ctx.op("q", lambda c=c, kc=kc: nc.sync.dma_start(
                out=wev[:, c, kc, :],
                in_=escr[c * 128:(c + 1) * 128, kc * 128:(kc + 1) * 128],
                transpose=True), reads=(escr_db,), writes=(we_b,), sem="DQ")

    gbuf, gbuf_b0 = ph.sb("gbuf", [128, 32 * 1024], BF16)
    gcb = [Buf() for _ in range(32)]
    gbuf_b = BufView(*gcb)
    phG = Phase(ctx)
    ctx.op("p", lambda: nc.gpsimd.dma_start(
        out=gbuf[:].rearrange("p (c n) -> p c n", c=32), in_=dram["g0"]),
        writes=(gbuf_b,), sem="DP")
    _fold_butterfly(ctx, phG, nc, dram, "dn", gbuf, gbuf_b, 1024, chunk_bufs=gcb)
    gv = gbuf[:].rearrange("p (c n) -> p c n", c=32)
    phG.close()

    bs2, bs2_b = ph.sb("bsum2", [128, CH], F32)
    bcf, bcf_b = ph.sb("bconf", [128, CH], F32)
    dbf, dbf_b = ph.sb("dbf", [128, CH], F32)
    ctx.op("q", lambda: nc.sync.dma_start(out=bcf[:], in_=dram["bcon"]),
           writes=(bcf_b,), sem="DQ")
    ctx.op("q", lambda: nc.sync.dma_start(out=dbf[:], in_=dram["dbias"]),
           writes=(dbf_b,), sem="DQ")
    ctx.op("v", lambda: nc.vector.tensor_add(out=bs2[:], in0=bcf[:], in1=dbf[:]),
           reads=(bcf_b, dbf_b), writes=(bs2_b,))

    cst = load_consts(ctx, ph, dram, [("identfF", F32)])
    identf, identf_b = cst["identfF"]

    xnt, xnt_b = ph.sb("xntF", [128, CH * NT], BF16)
    x3t, x3t_b = ph.sb("x3tF", [128, CH * NT], F32)
    hf, hf_b = ph.sb("hfF", [128, 32 * NT], BF16)
    x4, x4_b = ph.sb("x4F", [128, CH * NT], F32)
    outt, outt_b = ph.sb("outtF", [128, 1024], F32)
    ps_h = [ph.ps(f"ps_hF{i}", [128, NT], F32) for i in range(2)]
    ps_y = [ph.ps(f"ps_yF{i}", [128, NT], F32) for i in range(2)]
    ps_t, ps_t_b = ph.ps("ps_tF", [128, 1024], F32)
    xnv = xnt[:].rearrange("p (c n) -> p c n", c=CH)
    x3v = x3t[:].rearrange("p (c n) -> p c n", c=CH)
    hv = hf[:].rearrange("p (c n) -> p c n", c=32)
    x4v = x4[:].rearrange("p (c n) -> p c n", c=CH)

    for nt in range(NTILES):
        ctx.op("q", lambda nt=nt: nc.sync.dma_start(
            out=xnt[:].rearrange("p (c n) -> p c n", c=CH),
            in_=xn3v[:, nt]), writes=(xnt_b,), sem="DQ")
        ctx.op("q", lambda nt=nt: nc.sync.dma_start(
            out=x3t[:].rearrange("p (c n) -> p c n", c=CH),
            in_=x3_d[:, nt]), writes=(x3t_b,), sem="DQ")
        for c in range(32):
            ps, ps_b = ps_h[c % 2]

            def hmm(c=c, ps=ps):
                i = None
                for kc in range(CH):
                    i = nc.tensor.matmul(out=ps[:], lhsT=wev[:, c, kc, :],
                                         rhs=xnv[:, kc, :],
                                         start=(kc == 0), stop=(kc == CH - 1))
                return i
            ctx.op("t", hmm, reads=(xnt_b, we_b), writes=(ps_b,))
            ctx.op("s", lambda c=c, ps=ps: nc.scalar.activation(
                out=hv[:, c, :], in_=ps[:], func=AF.Gelu, bias=bsum[:, c:c + 1]),
                reads=(ps_b, bsum_b), writes=(hf_b,))
        for oc in range(CH):
            ps, ps_b = ps_y[oc % 2]

            def ymm(oc=oc, ps=ps):
                i = None
                for kc in range(32):
                    i = nc.tensor.matmul(out=ps[:], lhsT=gv[:, kc, oc * 128:(oc + 1) * 128],
                                         rhs=hv[:, kc, :],
                                         start=(kc == 0), stop=(kc == 31))
                return i
            ctx.op("t", ymm, reads=(hf_b, gbuf_b), writes=(ps_b,))
            ctx.op("v", lambda oc=oc, ps=ps: nc.vector.scalar_tensor_tensor(
                out=x4v[:, oc, :], in0=ps[:], scalar=bs2[:, oc:oc + 1],
                in1=x3v[:, oc, :], op0=ALU.add, op1=ALU.add),
                reads=(ps_b, bs2_b, x3t_b), writes=(x4_b,))
        for a in range(4):
            def trs(a=a):
                i = None
                for c in range(CH):
                    i = nc.tensor.transpose(out=ps_t[:, c * 128:(c + 1) * 128],
                                            in_=x4v[:, c, a * 128:(a + 1) * 128],
                                            identity=identf[:])
                return i
            ctx.op("t", trs, reads=(x4_b, identf_b), writes=(ps_t_b,))
            ctx.op("v", lambda: nc.vector.tensor_copy(
                out=outt[:], in_=ps_t[:]), reads=(ps_t_b,), writes=(outt_b,))
            ctx.op("s", lambda nt=nt, a=a: nc.scalar.dma_start(
                out=outf[nt * NT + a * 128:nt * NT + (a + 1) * 128, :],
                in_=outt[:]), reads=(outt_b,), writes=(out_db,), sem="DS")
    ph.close()



def _mk_builder(phase_fn, ins, outs, internals=()):
    nc = bass.Bass("TRN2", target_bir_lowering=False, debug=False)
    dram = {}
    for name, shape, dt in ins:
        dram[name] = nc.dram_tensor(name, shape, dt, kind="ExternalInput").ap()
    for name, shape, dt in outs:
        dram[name] = nc.dram_tensor(name, shape, dt, kind="ExternalOutput").ap()
    for name, shape, dt in internals:
        dram[name] = nc.dram_tensor(name, shape, dt, kind="Internal").ap()
    es = ExitStack()
    ctx = Ctx(nc, es)
    with es:
        with nc.Block() as block:
            @block.sync
            def _(sync):
                phase_fn(ctx, dram)
    return nc


class _DefaultNone(dict):
    def __missing__(self, k):
        return None


_SPECS = None


def _phase_specs():
    global _SPECS
    if _SPECS is not None:
        return _SPECS
    f32, bf = F32, BF16
    _SPECS = {
        "A": (build_phaseA,
              [("xin", [SEQ, 2048, 1024], f32), ("wqa", [1024, 64], f32),
               ("wqbp", [64, 1024], f32), ("wkva", [1024, 32], f32),
               ("wk0p", [32, 256], f32), ("wv0", [32, 256], f32),
               ("cossq", [128, 2048], f32), ("sinsq", [128, 2048], f32),
               ("cossk", [128, 2048], f32), ("sinsk", [128, 2048], f32),
               ("permbf", [128, 128], f32), ("ident", [128, 128], f32)],
              [("x0", [128, NTILES, CH, NT], f32), ("q0", [128, SEQ * 8 * 2048], bf),
               ("k0", [128, SEQ * 2 * 2048], bf), ("v0", [128, SEQ * 16 * 256], bf),
               ("negb0", [1, NTILES], f32)],
              []),
        "B": (build_phaseB,
              [("q0", [128, SEQ * 8 * 2048], bf), ("k0", [128, SEQ * 2 * 2048], bf),
               ("v0", [128, SEQ * 16 * 256], bf), ("x0", [128, NTILES, CH, NT], f32),
               ("negb0", [1, NTILES], f32),
               ("wout", [1024, 1024], f32), ("bout_fm", [128, 8], f32),
               ("rwt", [1024, 8], f32), ("causal", [128, 512], f32),
               ("identf", [128, 128], f32)],
              [("x1", [128, NTILES, CH, NT], f32), ("xn1", [128, NTILES, CH, NT], bf),
               ("gates", [8, T], f32)],
              []),
        "C": (build_phaseC,
              [("xn1", [128, NTILES, CH, NT], bf), ("x1", [128, NTILES, CH, NT], f32),
               ("gates", [8, T], f32), ("w1T", [NE, 1024, 2048], f32),
               ("w2T", [NE, 2048, 1024], f32)],
              [("x2", [128, NTILES, CH, NT], f32)],
              [("accC", [128, NTILES, CH, NT], f32)]),
        "D": (build_phaseD,
              [("x2", [128, NTILES, CH, NT], f32), ("wq1", [1024, 1024], f32),
               ("bq1", [128, 8], f32), ("wl1", [1024, 256], f32),
               ("bl1", [128, 2], f32), ("wk1", [256, 1024], f32),
               ("bk1", [128, 8], f32), ("wv1", [256, 1024], f32),
               ("bv1", [1, 1024], f32)],
              [("q1", [128, SEQ * 8 * 2048], bf), ("k1", [128, SEQ * 8 * 2048], bf),
               ("v1", [128, SEQ * 16 * 1024], bf), ("negb1", [1, NTILES], f32)],
              []),
        "E": (build_phaseE,
              [("q1", [128, SEQ * 8 * 2048], bf), ("k1", [128, SEQ * 8 * 2048], bf),
               ("v1", [128, SEQ * 16 * 1024], bf), ("x2", [128, NTILES, CH, NT], f32),
               ("negb1", [1, NTILES], f32),
               ("wo1", [1024, 1024], f32), ("bo1", [128, 8], f32)],
              [("x3", [128, NTILES, CH, NT], f32), ("xn3", [128, NTILES, CH, NT], bf)],
              []),
        "F": (build_phaseF,
              [("x3", [128, NTILES, CH, NT], f32), ("xn3", [128, NTILES, CH, NT], bf),
               ("e0", [128, 32, 1025], f32), ("g0", [128, 32, 1024], f32),
               ("upchunk", [7, 32, 128, 128], f32), ("dnchunk", [7, 32, 128, 128], f32),
               ("updself", [128, 160], f32), ("updoth", [128, 160], f32),
               ("dndself", [128, 160], f32), ("dndoth", [128, 160], f32),
               ("upbias", [128, 32], f32), ("bcon", [128, CH], f32),
               ("dbias", [128, CH], f32), ("identfF", [128, 128], f32)],
              [("out", [SEQ, 2048, 1024], f32)],
              [("escr", [4096, 1024], bf)]),
    }
    return _SPECS


def _phase_inputs(name, hp, core_arrays):
    core_arrays = _DefaultNone(core_arrays)
    if name == "A":
        return {"xin": core_arrays["xin"], "wqa": hp["Wqa"], "wqbp": hp["Wqb_perm"],
                "wkva": hp["Wkva"], "wk0p": hp["Wk0_perm"], "wv0": hp["Wv0"],
                "cossq": hp["COSSq"], "sinsq": hp["SINSq"], "cossk": hp["COSSk"],
                "sinsk": hp["SINSk"], "permbf": hp["PERM"], "ident": hp["IDENT"]}
    if name == "B":
        return {"q0": core_arrays["q0"], "k0": core_arrays["k0"],
                "v0": core_arrays["v0"], "x0": core_arrays["x0"],
                "negb0": core_arrays["negb0"],
                "wout": hp["Wout"], "bout_fm": hp["bout_fm"], "rwt": hp["router_wT"],
                "causal": hp["CAUSAL"], "identf": hp["IDENT"]}
    if name == "C":
        return {"xn1": core_arrays["xn1"], "x1": core_arrays["x1"],
                "gates": core_arrays["gates"], "w1T": hp["w1T"], "w2T": hp["w2T"]}
    if name == "D":
        return {"x2": core_arrays["x2"], "wq1": hp["Wq1"], "bq1": hp["bq1_fm"],
                "wl1": hp["Wl1"], "bl1": hp["bl1_fm"], "wk1": hp["Wk1"],
                "bk1": hp["bk1_fm"], "wv1": hp["Wv1"], "bv1": hp["bv1_row"]}
    if name == "E":
        return {"q1": core_arrays["q1"], "k1": core_arrays["k1"],
                "v1": core_arrays["v1"], "x2": core_arrays["x2"],
                "negb1": core_arrays["negb1"],
                "wo1": hp["Wo1"], "bo1": hp["bo1_fm"]}
    if name == "F":
        return {"x3": core_arrays["x3"], "xn3": core_arrays["xn3"],
                "e0": hp["E0"], "g0": hp["G0"], "upchunk": hp["UP_CHUNK"],
                "dnchunk": hp["DN_CHUNK"], "updself": hp["UP_DSELF_dev"],
                "updoth": hp["UP_DOTH_dev"], "dndself": hp["DN_DSELF_dev"],
                "dndoth": hp["DN_DOTH_dev"], "upbias": hp["up_bias_fm"],
                "bcon": hp["b_con_fm"], "dbias": hp["down_bias_fm"],
                "identfF": hp["IDENT"]}
    raise KeyError(name)


def _mk_fused():
    specs = _phase_specs()
    nc = bass.Bass("TRN2", target_bir_lowering=False, debug=False)
    dram = {}
    ext_in = set()
    stream_names = {"x0", "x1", "x2", "x3", "xn1", "xn3", "gates",
                    "q0", "k0", "v0", "q1", "k1", "v1", "negb0", "negb1"}
    for name in "ABCDEF":
        fn, ins, outs, internals = specs[name]
        for nm, shape, dt in ins:
            if nm in dram:
                continue
            if nm in stream_names:
                continue
            dram[nm] = nc.dram_tensor(nm, shape, dt, kind="ExternalInput").ap()
            ext_in.add(nm)
        for nm, shape, dt in outs:
            if nm in dram:
                continue
            kind = "ExternalOutput" if nm == "out" else "Internal"
            dram[nm] = nc.dram_tensor(nm, shape, dt, kind=kind).ap()
        for nm, shape, dt in internals:
            dram[nm] = nc.dram_tensor(nm, shape, dt, kind="Internal").ap()
    es = ExitStack()
    ctx = Ctx(nc, es)
    with es:
        with nc.Block() as block:
            @block.sync
            def _(sync):
                for name in "ABCDEF":
                    specs[name][0](ctx, dram)
    return nc, ext_in


def kernel(**inputs):
    import os
    hp = host_prep({k: np.asarray(v) for k, v in inputs.items()})
    x_full = np.asarray(inputs["L0_attn_x"], dtype=np.float32)
    ncores = NCORES
    core_state = [{"xin": np.ascontiguousarray(x_full[c * SEQ:(c + 1) * SEQ])}
                  for c in range(ncores)]
    specs = _phase_specs()
    if os.environ.get("MK_UNFUSED"):
        for name in "ABCDEF":
            fn, ins, outs, internals = specs[name]
            nc = _mk_builder(fn, ins, outs, internals)
            in_maps = [_phase_inputs(name, hp, core_state[c]) for c in range(ncores)]
            res = run_bass_kernel_spmd(nc, in_maps, core_ids=list(range(ncores)))
            for c in range(ncores):
                core_state[c].update(res.results[c])
        return np.concatenate([core_state[c]["out"] for c in range(ncores)], axis=0)
    nc, ext_in = _mk_fused()
    in_maps = []
    for c in range(ncores):
        m = {}
        for name in "ABCDEF":
            for k, v in _phase_inputs(name, hp, core_state[c]).items():
                if k in ext_in and v is not None:
                    m[k] = v
        in_maps.append(m)
    res = run_bass_kernel_spmd(nc, in_maps, core_ids=list(range(ncores)))
    return np.concatenate([res.results[c]["out"] for c in range(ncores)], axis=0)
